# version 29
# baseline (speedup 1.0000x reference)
import numpy as np
import ml_dtypes
from contextlib import ExitStack

import concourse.bass as bass
import concourse.mybir as mybir
from concourse.bass_utils import run_bass_kernel_spmd

F32 = mybir.dt.float32
BF16 = mybir.dt.bfloat16
I32 = mybir.dt.int32
AF = mybir.ActivationFunctionType
ALU = mybir.AluOpType
AXX = mybir.AxisListType.X
BFNP = ml_dtypes.bfloat16

DM = 1024
DI = 2048
NS = 16
DTR = 64
INNER = 4096
CH = 256
NCORES = 8
NW_ELEMS = float(INNER * DM)
RMASK = 0xFFFF8000 - (1 << 32)
RBIAS = 384.5


class _Op:
    __slots__ = ("q", "dma", "fn", "waits", "inc", "cum", "skey")

    def __init__(self):
        self.waits = set()
        self.inc = False


class Prog:
    QUEUES = ("sync", "gpsimd", "tensor", "scalar", "vector")
    DMA_SLOTS = 8

    def __init__(self, nc):
        self.nc = nc
        self.qops = {q: [] for q in self.QUEUES}
        self.streams = {}
        self.slotlast = {}
        self.dmacnt = {q: 0 for q in self.QUEUES}
        self.tags = {}
        self.pend = {}

    def op(self, q, fn, reads=(), writes=(), dma=False, wr_override=()):
        o = _Op()
        o.q, o.dma, o.fn = q, dma, fn
        pend = self.pend.pop(q, None)
        deps = set(pend) if pend else set()
        for t in reads:
            e = self.tags.get(t)
            if e is not None and e[0] is not None:
                deps.add(e[0])
        for t in writes:
            e = self.tags.get(t)
            if e is not None:
                if e[0] is not None:
                    deps.add(e[0])
                deps.update(e[1])
        if dma:
            slot = self.dmacnt[q] % self.DMA_SLOTS
            self.dmacnt[q] += 1
            o.skey = (q, "d%d" % slot)
            prev = self.slotlast.get(o.skey)
            if prev is not None:
                deps.add(prev)
            self.slotlast[o.skey] = o
            o.inc = True
        else:
            o.skey = (q, "c")
        for d in deps:
            if d is o:
                continue
            d.inc = True
            o.waits.add(d)
        for t in reads:
            self.tags.setdefault(t, [None, []])[1].append(o)
        for t in writes:
            self.tags[t] = [o, []]
        for t in wr_override:
            self.tags[t] = [o, []]
        c = self.streams.get(o.skey, 0) + 1
        self.streams[o.skey] = c
        o.cum = c * (16 if dma else 1)
        self.qops[q].append(o)
        return o

    def barrier(self):
        last = set()
        for q in self.QUEUES:
            for o in reversed(self.qops[q]):
                if not o.dma:
                    last.add(o)
                    o.inc = True
                    break
        for o in self.slotlast.values():
            last.add(o)
            o.inc = True
        for q in self.QUEUES:
            self.pend[q] = self.pend.get(q, set()) | set(last)

    def emit(self):
        nc = self.nc
        es = ExitStack()
        sems = {}
        for key in self.streams:
            sems[key] = es.enter_context(nc.semaphore("sem_%s_%s" % key))
        with nc.Block() as block:
            for q in self.QUEUES:
                lst = self.qops[q]
                if not lst:
                    continue

                def body(eng, lst=lst, q=q):
                    sat = {}
                    last_inc = {}
                    for o in lst:
                        wl = {}
                        for d in o.waits:
                            if d.cum > wl.get(d.skey, 0):
                                wl[d.skey] = d.cum
                        for k in sorted(wl, key=lambda t: (t[0], t[1])):
                            v = wl[k]
                            if v > sat.get(k, 0):
                                eng.wait_ge(sems[k], v)
                                sat[k] = v
                        inst = o.fn(eng) if o.fn is not None else None
                        if o.inc:
                            k = o.skey
                            amt = o.cum - last_inc.get(k, 0)
                            inst.then_inc(sems[k], amt)
                            last_inc[k] = o.cum

                getattr(block, q)(body)
        es.close()


def build_program(TOK):
    NT = TOK // 128
    NSC = TOK // CH
    nc = bass.Bass("TRN2", target_bir_lowering=False)

    xh = nc.dram_tensor("xh", [TOK + 2, DM], F32, kind="ExternalInput")
    w_inT = nc.dram_tensor("w_inT", [DM, 2 * DI], F32, kind="ExternalInput")
    w_xT = nc.dram_tensor("w_xT", [DI, 96], F32, kind="ExternalInput")
    w_dtT = nc.dram_tensor("w_dtT", [DTR, DI], F32, kind="ExternalInput")
    w_outT = nc.dram_tensor("w_outT", [DI, DM], F32, kind="ExternalInput")
    wgT = nc.dram_tensor("wgT", [DM, INNER], F32, kind="ExternalInput")
    wuT = nc.dram_tensor("wuT", [DM, INNER], F32, kind="ExternalInput")
    wdT = nc.dram_tensor("wdT", [INNER, DM], F32, kind="ExternalInput")
    b_in_t = nc.dram_tensor("b_in_t", [128, 32], F32, kind="ExternalInput")
    conv_w_t = nc.dram_tensor("conv_w_t", [128, 48], F32, kind="ExternalInput")
    conv_b_t = nc.dram_tensor("conv_b_t", [128, 16], F32, kind="ExternalInput")
    b_dt_t = nc.dram_tensor("b_dt_t", [128, 16], F32, kind="ExternalInput")
    alog_t = nc.dram_tensor("alog_t", [128, 256], F32, kind="ExternalInput")
    b_out_bc_d = nc.dram_tensor("b_out_bc", [128, DM], F32, kind="ExternalInput")
    eye_in = nc.dram_tensor("eye_in", [128, 128], BF16, kind="ExternalInput")
    hmask_in = nc.dram_tensor("hmask_in", [128, 2], F32, kind="ExternalInput")
    out_d = nc.dram_tensor("out", [TOK, DM], F32, kind="ExternalOutput")

    s_winT = nc.dram_tensor("s_winT", [DM, 2 * DI], BF16)
    s_woutT = nc.dram_tensor("s_woutT", [DI, DM], BF16)
    s_xnfm = nc.dram_tensor("s_xnfm", [DM, TOK], BF16)
    s_xqfm = nc.dram_tensor("s_xqfm", [DM, TOK], BF16)
    s_wdq = nc.dram_tensor("s_wdq", [INNER, DM], BF16)
    s_mm = nc.dram_tensor("s_mm", [TOK, DM], F32)

    P = Prog(nc)
    half = 0.5

    pstack = ExitStack()

    def salloc(stack, name, shape, dt):
        return stack.enter_context(nc.sbuf_tensor(name, shape, dt))

    hmask = salloc(pstack, "hmask", [128, 2], F32)
    eye = salloc(pstack, "eye", [128, 128], BF16)
    invsx = salloc(pstack, "invsx", [128, NT], F32)
    swrow = salloc(pstack, "swrow", [1, 16], F32)
    swbc = salloc(pstack, "swbc", [128, 9], F32)
    ones2 = salloc(pstack, "ones2", [128, 128], F32)
    combog = salloc(pstack, "combog", [128, NT], F32)
    combou = salloc(pstack, "combou", [128, NT], F32)
    combod = salloc(pstack, "combod", [128, NT], F32)
    gmparts = salloc(pstack, "gmparts", [128, NT * 8], F32)
    sgu = salloc(pstack, "sgu", [128, NT], F32)
    invsgu = salloc(pstack, "invsgu", [128, NT], F32)
    ones1 = salloc(pstack, "ones1", [128, 1], F32)
    csts = salloc(pstack, "csts", [128, 3], F32)
    acc = salloc(pstack, "acc", [128, 96], F32)

    psA = pstack.enter_context(nc.psum_tensor("psA", [128, 512], F32))
    psB = pstack.enter_context(nc.psum_tensor("psB", [128, 512], F32))

    pmstk = ExitStack()
    psI = [pmstk.enter_context(nc.psum_tensor("psI%d" % i, [128, 256], F32)) for i in range(2)]
    bint = salloc(pmstk, "bint", [128, 32], F32)
    cwt = salloc(pmstk, "cwt", [128, 48], F32)
    cbt = salloc(pmstk, "cbt", [128, 16], F32)
    bdtt = salloc(pmstk, "bdtt", [128, 16], F32)
    alneg = salloc(pmstk, "alneg", [128, 256], F32)
    wxT_bf = salloc(pmstk, "wxT_bf", [128, 16 * 96], BF16)
    wdtT_bf = salloc(pmstk, "wdtT_bf", [64, DI], BF16)
    xnh_fm = salloc(pmstk, "xnh_fm", [128, 16], BF16)

    p0 = ExitStack()
    psT0 = [p0.enter_context(nc.psum_tensor("psT%d" % i, [128, 1024], BF16))
            for i in range(2)]
    ROT = 3
    sxs = salloc(p0, "sxs", [128, NT], F32)
    xt = [salloc(p0, "xt%d" % i, [128, DM], F32) for i in range(4)]
    dump = [salloc(p0, "dump%d" % i, [128, DM], F32) for i in range(2)]
    stats = [salloc(p0, "stats%d" % i, [128, 16], F32) for i in range(ROT)]
    xn = [salloc(p0, "xn%d" % i, [128, DM], F32) for i in range(ROT)]
    xnb = [salloc(p0, "xnb%d" % i, [128, DM], BF16) for i in range(ROT)]
    tq0 = [salloc(p0, "tq0%d" % i, [128, DM], F32) for i in range(ROT)]
    tm0 = [salloc(p0, "tm0%d" % i, [128, DM], F32) for i in range(ROT)]
    xqb = [salloc(p0, "xqb%d" % i, [128, DM], BF16) for i in range(ROT)]
    trev = [salloc(p0, "trev%d" % i, [128, DM], BF16) for i in range(ROT)]
    wcf = [salloc(p0, "wcf%d" % i, [128, 4096], F32) for i in range(2)]
    wcb = [salloc(p0, "wcb%d" % i, [128, 4096], BF16) for i in range(2)]
    xhalo = salloc(p0, "xhalo", [2, DM], F32)
    hstat = salloc(p0, "hstat", [2, 16], F32)
    xnhb = salloc(p0, "xnhb", [2, DM], BF16)
    asum = salloc(p0, "asum", [128, 3], F32)

    P.op("sync", lambda e: e.dma_start(bint[:], b_in_t.ap()), writes=["bint"], dma=True)
    P.op("sync", lambda e: e.dma_start(cwt[:], conv_w_t.ap()), writes=["cwt"], dma=True)
    P.op("sync", lambda e: e.dma_start(cbt[:], conv_b_t.ap()), writes=["cbt"], dma=True)
    P.op("sync", lambda e: e.dma_start(bdtt[:], b_dt_t.ap()), writes=["bdtt"], dma=True)
    P.op("sync", lambda e: e.dma_start(alneg[:], alog_t.ap()), writes=["alraw"], dma=True)
    P.op("sync", lambda e: e.dma_start(hmask[:], hmask_in.ap()), writes=["hmask"], dma=True)
    P.op("sync", lambda e: e.dma_start(eye[:], eye_in.ap()), writes=["eye"], dma=True)
    P.op("vector", lambda v: v.memset(ones1[:], 1.0), writes=["ones1"])
    P.op("vector", lambda v: v.memset(csts[:, 0:1], 1e-5), writes=["cst0"])
    P.op("vector", lambda v: v.memset(csts[:, 1:2], 1.0), writes=["cst1"])
    P.op("vector", lambda v: v.memset(csts[:, 2:3], RBIAS), writes=["cst2"])
    P.op("vector", lambda v: v.memset(ones2[:], 1.0), writes=["ones2"])

    P.op("scalar", lambda s: s.activation(alneg[:], alneg[:], AF.Exp, bias=0.0, scale=1.0),
         reads=["alraw"], writes=["alneg_t"])
    P.op("vector", lambda v: v.tensor_scalar_mul(alneg[:], alneg[:], -1.0),
         reads=["alneg_t"], writes=["alneg"])

    P.op("sync", lambda e: e.dma_start(
        wcf[0][:, 0:16 * 96].rearrange("part (k c) -> part k c", k=16),
        w_xT.ap().rearrange("(k part) c -> part k c", k=16)),
        writes=["wcf:0"], dma=True)
    P.op("vector", lambda v: v.tensor_copy(wxT_bf[:], wcf[0][:, 0:16 * 96]),
         reads=["wcf:0"], writes=["wxT_bf"])
    P.op("sync", lambda e: e.dma_start(wcf[1][0:64, 0:DI], w_dtT.ap()),
         writes=["wcf:1"], dma=True)
    P.op("vector", lambda v: v.tensor_copy(wdtT_bf[:], wcf[1][0:64, 0:DI]),
         reads=["wcf:1"], writes=["wdtT_bf"])

    def wcast_gen():
        wcnt = 0
        for (src, dst, rows, cols, tagbase) in ((w_inT, s_winT, DM, 2 * DI, "swin"),
                                                (w_outT, s_woutT, DI, DM, "swout")):
            ntile = rows // 128
            ncol = cols // 4096 if cols >= 4096 else 1
            cw = 4096 if cols >= 4096 else cols
            for r in range(ntile):
                for c in range(ncol):
                    b = wcnt % 2
                    lq = "sync" if wcnt % 2 == 0 else "scalar"
                    wcnt += 1
                    yield lambda src=src, r=r, c=c, b=b, cw=cw, lq=lq: P.op(
                        lq, lambda e: e.dma_start(
                            wcf[b][:, 0:cw],
                            src.ap()[r * 128:(r + 1) * 128, c * cw:(c + 1) * cw]),
                        writes=["wcf:%d" % b], dma=True)
                    yield lambda b=b, cw=cw: P.op("vector", lambda v: v.tensor_copy(
                        wcb[b][:, 0:cw], wcf[b][:, 0:cw]),
                        reads=["wcf:%d" % b], writes=["wcb:%d" % b])
                    yield lambda dst=dst, r=r, c=c, b=b, cw=cw, tagbase=tagbase: P.op(
                        "gpsimd", lambda e: e.dma_start(
                            dst.ap()[r * 128:(r + 1) * 128, c * cw:(c + 1) * cw],
                            wcb[b][:, 0:cw]),
                        reads=["wcb:%d" % b],
                        writes=["%s:%d:%d" % (tagbase, r, c)], dma=True)

    def _tile_ops(i):
        if True:
            ishalo = (i == NT)
            part = 2 if ishalo else 128
            b = i % ROT
            bx = i % 4
            db = i % 2
            xtt = xhalo if ishalo else xt[bx]
            stt = hstat if ishalo else stats[b]
            xnt = xn[b][0:part, :] if not ishalo else xn[0][0:2, :]
            xnbt = xnhb if ishalo else xnb[b]
            rt = "h" if ishalo else str(bx)
            strt = "hs" if ishalo else "s%d" % b
            txn = ("xn:%d" % b) if not ishalo else "xn:0"
            txnb = ("xnb:%d" % b) if not ishalo else "xnb:h"

            if ishalo:
                yield lambda: P.op("sync", lambda e: e.dma_start(
                    xhalo[:], xh.ap()[0:2, :]),
                    writes=["xt:h"], dma=True)
            else:
                yield lambda i=i, bx=bx, rt=rt: P.op("sync", lambda e: e.dma_start(
                    xt[bx][:], xh.ap()[2 + i * 128:2 + (i + 1) * 128, :]),
                    writes=["xt:" + rt], dma=True)
            xv = xtt[0:part, :]
            sv_ = stt[0:part, :]
            yield lambda xv=xv, sv_=sv_, rt=rt, strt=strt: P.op(
                "vector", lambda v: v.reduce_sum(sv_[:, 0:1], xv, axis=AXX),
                reads=["xt:" + rt], writes=[strt + ":sum"])
            yield lambda xv=xv, sv_=sv_, db=db, part=part, rt=rt, strt=strt: P.op(
                "scalar", lambda s: s.activation(
                    dump[db][0:part, :], xv, AF.Square, bias=0.0, scale=1.0,
                    accum_out=sv_[:, 1:2]),
                reads=["xt:" + rt], writes=["dump:%d" % db, strt + ":sq"])
            yield lambda sv_=sv_, strt=strt: P.op("vector", lambda v: v.tensor_scalar_mul(
                sv_[:, 2:3], sv_[:, 0:1], 1.0 / DM),
                reads=[strt + ":sum"], writes=[strt + ":mean"])
            yield lambda sv_=sv_, strt=strt: P.op("vector", lambda v: v.tensor_scalar_mul(
                sv_[:, 3:4], sv_[:, 1:2], 1.0 / DM),
                reads=[strt + ":sq"], writes=[strt + ":ex2"])
            yield lambda sv_=sv_, strt=strt: P.op("vector", lambda v: v.tensor_mul(
                sv_[:, 4:5], sv_[:, 2:3], sv_[:, 2:3]),
                reads=[strt + ":mean"], writes=[strt + ":msq"])
            yield lambda sv_=sv_, strt=strt: P.op("vector", lambda v: v.tensor_sub(
                sv_[:, 5:6], sv_[:, 3:4], sv_[:, 4:5]),
                reads=[strt + ":ex2", strt + ":msq"], writes=[strt + ":var"])
            yield lambda sv_=sv_, part=part, strt=strt: P.op("scalar", lambda s: s.activation(
                sv_[:, 11:12], sv_[:, 5:6], AF.Sqrt, bias=csts[0:part, 0:1], scale=1.0),
                reads=[strt + ":var", "cst0"], writes=[strt + ":std"])
            yield lambda sv_=sv_, strt=strt: P.op("vector", lambda v: v.reciprocal(
                sv_[:, 6:7], sv_[:, 11:12]),
                reads=[strt + ":std"], writes=[strt + ":rstd"])
            yield lambda sv_=sv_, strt=strt: P.op("vector", lambda v: v.tensor_scalar(
                sv_[:, 7:8], sv_[:, 2:3], sv_[:, 6:7], -1.0, ALU.mult, ALU.mult),
                reads=[strt + ":mean", strt + ":rstd"], writes=[strt + ":mrs"])
            yield lambda xv=xv, sv_=sv_, xnt=xnt, rt=rt, strt=strt, txn=txn: P.op(
                "scalar", lambda s: s.activation(
                    xnt, xv, AF.Identity, bias=sv_[:, 7:8], scale=sv_[:, 6:7]),
                reads=["xt:" + rt, strt + ":rstd", strt + ":mrs"], writes=[txn])
            yield lambda xnt=xnt, xnbt=xnbt, part=part, txn=txn, txnb=txnb: P.op(
                "vector", lambda v: v.tensor_copy(xnbt[0:part, :], xnt),
                reads=[txn], writes=[txnb])
            if not ishalo:
                yield lambda xnt=xnt, sv_=sv_, txn=txn, strt=strt: P.op(
                    "vector", lambda v: v.reduce_max(
                        sv_[:, 9:10], xnt, axis=AXX, apply_absolute_value=True),
                    reads=[txn], writes=[strt + ":rmax"])
                yield lambda sv_=sv_, strt=strt: P.op("vector", lambda v: v.tensor_single_scalar(
                    sv_[:, 8:9], sv_[:, 9:10], 1e-5, ALU.max),
                    reads=[strt + ":rmax"], writes=[strt + ":clip"])
                yield lambda sv_=sv_, i=i, strt=strt: P.op("vector", lambda v: v.tensor_scalar_mul(
                    invsx[:, i:i + 1], sv_[:, 8:9], 1.0 / 127.0),
                    reads=[strt + ":clip"], writes=["invsx:%d" % i])
                yield lambda sv_=sv_, strt=strt: P.op("vector", lambda v: v.reciprocal(
                    sv_[:, 10:11], sv_[:, 8:9]),
                    reads=[strt + ":clip"], writes=[strt + ":rcp"])
                yield lambda sv_=sv_, i=i, strt=strt: P.op("vector", lambda v: v.tensor_scalar_mul(
                    sxs[:, i:i + 1], sv_[:, 10:11], 127.0),
                    reads=[strt + ":rcp"], writes=["sx:%d" % i])
                yield lambda xnt=xnt, b=b, i=i, txn=txn: P.op("scalar", lambda s: s.activation(
                    tq0[b][:], xnt, AF.Identity, bias=csts[:, 2:3], scale=sxs[:, i:i + 1]),
                    reads=[txn, "sx:%d" % i, "cst2"], writes=["tq0:%d" % b])
                yield lambda b=b: P.op("vector", lambda v: v.tensor_single_scalar(
                    tm0[b][:].bitcast(I32), tq0[b][:].bitcast(I32), RMASK, ALU.bitwise_and),
                    reads=["tq0:%d" % b], writes=["tm0:%d" % b])
                yield lambda b=b: P.op("vector", lambda v: v.tensor_single_scalar(
                    xqb[b][:], tm0[b][:], -384.0, ALU.add),
                    reads=["tm0:%d" % b], writes=["xqb:%d" % b])
                for wi, (srcb, srct, dstd, tg) in enumerate(
                        ((xnb[b], txnb, s_xnfm, "sxn"),
                         (xqb[b], "xqb:%d" % b, s_xqfm, "sxq"))):
                    tp = (i * 2 + wi) % 2
                    for kb in range(8):
                        yield lambda srcb=srcb, kb=kb, srct=srct, tp=tp: P.op(
                            "tensor", lambda t: t.matmul(
                                psT0[tp][:, kb * 128:(kb + 1) * 128],
                                srcb[:, kb * 128:(kb + 1) * 128],
                                eye[:], is_transpose=True, start=(kb == 0), stop=(kb == 7)),
                            reads=[srct, "eye"],
                            writes=(["psT:%d" % tp] if kb == 0 else ()),
                            wr_override=(["psT:%d" % tp] if kb == 7 else ()))
                    yield lambda b=b, tp=tp: P.op("scalar", lambda s: s.activation(
                        trev[b][:], psT0[tp][:, 0:1024], AF.Copy, bias=0.0, scale=1.0),
                        reads=["psT:%d" % tp], writes=["trev:%d" % b])
                    yield lambda b=b, dstd=dstd, i=i, tg=tg: P.op(
                        "gpsimd", lambda e: e.dma_start(
                            dstd.ap()[:, i * 128:(i + 1) * 128].rearrange(
                                "(k part) c -> part k c", k=8),
                            trev[b][:].rearrange("part (k c) -> part k c", k=8)),
                        reads=["trev:%d" % b],
                        writes=["%s:%d:%d" % (tg, kb, i) for kb in range(8)], dma=True)
            else:
                for kb in range(8):
                    yield lambda kb=kb: P.op("tensor", lambda t: t.matmul(
                        psT0[0][:, kb * 2:(kb + 1) * 2], xnhb[0:2, kb * 128:(kb + 1) * 128],
                        eye[0:2, 0:2], is_transpose=True, start=(kb == 0), stop=(kb == 7)),
                        reads=["xnb:h", "eye"],
                        writes=(["psT:0"] if kb == 0 else ()),
                        wr_override=(["psT:0"] if kb == 7 else ()))
                yield lambda: P.op("scalar", lambda s: s.activation(
                    xnh_fm[:], psT0[0][:, 0:16], AF.Copy, bias=0.0, scale=1.0),
                    reads=["psT:0"], writes=["xnh_fm"])

    def ln_skewed():
        tiles = []
        for i in range(NT):
            it = _tile_ops(i)
            tiles.append([fn for fn in it])
        cuts = (0, 1, 3, 10, 20, None)
        for j in range(NT + 4):
            for st in range(5):
                i = j - st
                if 0 <= i < NT:
                    a, b = cuts[st], cuts[st + 1]
                    for fn in tiles[i][a:b]:
                        yield fn
        for fn in _tile_ops(NT):
            yield fn

    _wc = iter(wcast_gen())
    _ln = iter(ln_skewed())
    while True:
        got = False
        for _ in range(5):
            fn = next(_ln, None)
            if fn is not None:
                got = True
                fn()
        fn = next(_wc, None)
        if fn is not None:
            got = True
            fn()
        if not got:
            break

    p0.close()
    P.barrier()

    mstk = ExitStack()
    psBC = [mstk.enter_context(nc.psum_tensor("psBC%d" % i, [128, 512], F32))
            for i in range(2)]
    psO2 = mstk.enter_context(nc.psum_tensor("psO2", [128, 512], F32))
    psOut = mstk.enter_context(nc.psum_tensor("psOut", [128, 512], F32))
    xnsc = [salloc(mstk, "xnsc%d" % i, [128, 8 * CH], BF16) for i in range(2)]
    wch = [salloc(mstk, "wch%d" % i, [128, 2048], BF16) for i in range(2)]
    CH2 = CH + 2
    xc_t = salloc(mstk, "xc_t", [128, 16 * CH2], F32)
    sres = [salloc(mstk, "sres%d" % i, [128, 16 * CH], BF16) for i in range(2)]
    delta_t = [salloc(mstk, "delta_t%d" % i, [128, 16 * CH], BF16) for i in range(2)]
    du_t = [salloc(mstk, "du_t%d" % i, [128, 16 * CH], BF16) for i in range(2)]
    dltbf = [salloc(mstk, "dltbf%d" % i, [64, CH], BF16) for i in range(2)]
    BCt = [salloc(mstk, "BCt%d" % i, [128, CH], F32) for i in range(2)]
    ubf = salloc(mstk, "ubf", [128, 16 * CH], BF16)
    yA = salloc(mstk, "yA", [128, 16 * CH], F32)
    gated = [salloc(mstk, "gated%d" % i, [128, 16 * CH], BF16) for i in range(2)]
    shstB = salloc(mstk, "shstB", [128, CH], F32)
    shstC = salloc(mstk, "shstC", [128, CH], F32)
    NDA = 26
    dAb = [salloc(mstk, "dAb%d" % i, [128, CH], BF16) for i in range(NDA)]
    dbub = [salloc(mstk, "dbub%d" % i, [128, CH], BF16) for i in range(3)]
    hbuf = [salloc(mstk, "hbuf%d" % i, [128, CH], BF16) for i in range(3)]
    ypb = [salloc(mstk, "ypb%d" % i, [128, CH], BF16) for i in range(4)]
    pa = [salloc(mstk, "pa%d" % i, [128, CH], BF16) for i in range(16)]
    Bsb = [salloc(mstk, "Bsb%d" % i, [128, CH], BF16) for i in range(3)]
    Csb = [salloc(mstk, "Csb%d" % i, [128, CH], BF16) for i in range(3)]
    cva = [salloc(mstk, "cva%d" % i, [128, CH], F32) for i in range(2)]
    cvb = [salloc(mstk, "cvb%d" % i, [128, CH], F32) for i in range(2)]
    cvc = cva
    xpb = [salloc(mstk, "xpb%d" % i, [128, CH], F32) for i in range(2)]
    axb = [salloc(mstk, "axb0", [128, CH], F32)] * 2
    eab = [salloc(mstk, "eab0", [128, CH], F32)] * 2
    lab = [salloc(mstk, "lab%d" % i, [128, CH], F32) for i in range(2)]
    hv = salloc(mstk, "hv", [128, 2], F32)
    hv2 = salloc(mstk, "hv2", [128, 2], F32)
    wo_buf = [salloc(mstk, "wo_buf%d" % i, [128, 4 * 512], BF16) for i in range(2)]
    xrel = [salloc(mstk, "xrel%d" % i, [128, DM], F32) for i in range(2)]
    mmo = [salloc(mstk, "mmo%d" % i, [128, DM], F32) for i in range(2)]
    wab = [salloc(mstk, "wab%d" % i, [128, 1024], F32) for i in range(2)]

    def pre_a_mm(sc):
        p = sc % 2
        t0 = sc * CH
        yield lambda: P.op("sync", lambda e: e.dma_start(
            xnsc[p][:].rearrange("part (k c) -> part k c", k=8),
            s_xnfm.ap()[:, t0:t0 + CH].rearrange("(k part) c -> part k c", k=8)),
            reads=["sxn:%d:%d" % (k2, (t0 // 128) + j) for k2 in range(8)
                   for j in range(CH // 128)],
            writes=["xnsc:%d" % p], dma=True)
        for mg in range(16):
            wb = mg % 2
            yield lambda mg=mg, wb=wb: P.op("sync" if mg % 2 == 0 else "scalar",
                lambda e: e.dma_start(
                    wch[wb][:].rearrange("part (k c) -> part k c", k=8),
                    s_winT.ap()[:, mg * 256:(mg + 1) * 256].rearrange(
                        "(k part) c -> part k c", k=8)),
                reads=["swin:%d:0" % k2 for k2 in range(8)],
                writes=["wch:%d" % wb], dma=True)
            for m in range(2):
                mt = mg * 2 + m
                mp = m % 2
                for k2 in range(8):
                    yield lambda wb=wb, k2=k2, m=m, mp=mp, p=p: P.op(
                        "tensor", lambda t: t.matmul(
                            psI[mp][:, 0:CH],
                            wch[wb][:, k2 * 256 + m * 128:k2 * 256 + (m + 1) * 128],
                            xnsc[p][:, k2 * CH:(k2 + 1) * CH],
                            start=(k2 == 0), stop=(k2 == 7)),
                        reads=["wch:%d" % wb, "xnsc:%d" % p],
                        writes=(["psI:%d" % mp] if k2 == 0 else ()),
                        wr_override=(["psI:%d" % mp] if k2 == 7 else ()))
                if sc == 0 and mt < 16:
                    for k2 in range(8):
                        yield lambda wb=wb, k2=k2, m=m: P.op(
                            "tensor", lambda t: t.matmul(
                                psA[:, 0:2],
                                wch[wb][:, k2 * 256 + m * 128:k2 * 256 + (m + 1) * 128],
                                xnh_fm[:, k2 * 2:(k2 + 1) * 2],
                                start=(k2 == 0), stop=(k2 == 7)),
                            reads=["wch:%d" % wb, "xnh_fm"],
                            writes=(["psA"] if k2 == 0 else ()),
                            wr_override=(["psA"] if k2 == 7 else ()))
                    yield lambda mt=mt: P.op("scalar", lambda s: s.activation(
                        hv[:], psA[:, 0:2], AF.Identity, bias=bint[:, mt:mt + 1],
                        scale=1.0),
                        reads=["psA", "bint"], writes=["hv"])
                    yield lambda: P.op("vector", lambda v: v.tensor_mul(
                        hv2[:], hv[:], hmask[:]),
                        reads=["hv", "hmask"], writes=["hv2"])
                    yield lambda mt=mt: P.op("vector", lambda v: v.tensor_copy(
                        xc_t[:, mt * CH2:mt * CH2 + 2], hv2[:]),
                        reads=["hv2"], writes=["xch:%d" % mt])
                if mt < 16:
                    yield lambda mp=mp, mt=mt: P.op("scalar", lambda s: s.activation(
                        xc_t[:, mt * CH2 + 2:mt * CH2 + 2 + CH], psI[mp][:, 0:CH],
                        AF.Identity, bias=bint[:, mt:mt + 1], scale=1.0),
                        reads=["psI:%d" % mp, "bint"], writes=["xcb:%d" % mt])
                else:
                    yield lambda mp=mp, mt=mt, p=p: P.op("scalar", lambda s: s.activation(
                        sres[p][:, (mt - 16) * CH:(mt - 15) * CH], psI[mp][:, 0:CH],
                        AF.Identity, bias=bint[:, mt:mt + 1], scale=1.0),
                        reads=["psI:%d" % mp, "bint"], writes=["sr:%d:%d" % (p, mt - 16)])

    def pre_a_conv(sc):
        p = sc % 2
        for q in range(16):
            cb2 = q % 2
            yield lambda q=q, cb2=cb2: P.op("vector", lambda v: v.tensor_single_scalar(
                cva[cb2][:], xc_t[:, q * CH2 + 0:q * CH2 + CH],
                cwt[:, 0 + q:0 + q + 1], ALU.mult),
                reads=["xch:%d" % q, "xcb:%d" % q, "cwt"],
                writes=["cva:%d" % cb2])
            yield lambda q=q, cb2=cb2: P.op("vector", lambda v: v.scalar_tensor_tensor(
                cvb[cb2][:], xc_t[:, q * CH2 + 1:q * CH2 + 1 + CH],
                cwt[:, 16 + q:16 + q + 1], cva[cb2][:], ALU.mult, ALU.add),
                reads=["xch:%d" % q, "xcb:%d" % q, "cva:%d" % cb2, "cwt"],
                writes=["cvb:%d" % cb2])
            yield lambda q=q, cb2=cb2: P.op("vector", lambda v: v.scalar_tensor_tensor(
                cvc[cb2][:], xc_t[:, q * CH2 + 2:q * CH2 + 2 + CH],
                cwt[:, 32 + q:32 + q + 1], cvb[cb2][:], ALU.mult, ALU.add),
                reads=["xcb:%d" % q, "cvb:%d" % cb2, "cwt"],
                writes=["cva:%d" % cb2])
            yield lambda q=q, cb2=cb2: P.op("scalar", lambda s: s.activation(
                ubf[:, q * CH:(q + 1) * CH], cvc[cb2][:], AF.Silu,
                bias=cbt[:, q:q + 1], scale=1.0),
                reads=["cva:%d" % cb2, "cbt"], writes=["ubf:%d" % q])
            if sc + 1 < NSC:
                yield lambda q=q: P.op("vector", lambda v: v.tensor_copy(
                    xc_t[:, q * CH2:q * CH2 + 2],
                    xc_t[:, q * CH2 + CH:q * CH2 + CH + 2]),
                    reads=["xcb:%d" % q], writes=["xch:%d" % q])
        for q in range(16):
            yield lambda q=q, p=p: P.op("scalar", lambda s: s.activation(
                sres[p][:, q * CH:(q + 1) * CH], sres[p][:, q * CH:(q + 1) * CH],
                AF.Silu, bias=0.0, scale=1.0),
                reads=["sr:%d:%d" % (p, q)], writes=["sr:%d:%d" % (p, q)])

    def pre_b(sc):
        p = sc % 2
        for k in range(16):
            yield lambda k=k: P.op("tensor", lambda t: t.matmul(
                psA[0:96, 0:CH], wxT_bf[:, k * 96:(k + 1) * 96],
                ubf[:, k * CH:(k + 1) * CH], start=(k == 0), stop=(k == 15)),
                reads=["wxT_bf", "ubf:%d" % k],
                writes=(["psA"] if k == 0 else ()),
                wr_override=(["psA"] if k == 15 else ()))
        yield lambda p=p: P.op("scalar", lambda s: s.activation(
            dltbf[p][:, :], psA[0:64, 0:CH], AF.Copy, bias=0.0, scale=1.0),
            reads=["psA"], writes=["dltbf:%d" % p])
        yield lambda p=p: P.op("scalar", lambda s: s.activation(
            BCt[p][64:96, :], psA[64:96, 0:CH], AF.Copy, bias=0.0, scale=1.0),
            reads=["psA"], writes=["BCt:%d" % p])
        for mt in range(16):
            b = mt % 2
            yield lambda mt=mt, p=p: P.op("tensor", lambda t: t.matmul(
                psB[:, 0:CH], wdtT_bf[:, mt * 128:(mt + 1) * 128], dltbf[p][:, :],
                start=True, stop=True),
                reads=["wdtT_bf", "dltbf:%d" % p], writes=["psB"], wr_override=["psB"])
            yield lambda b=b, mt=mt: P.op("scalar", lambda s: s.activation(
                xpb[b][:], psB[:, 0:CH], AF.Identity, bias=bdtt[:, mt:mt + 1], scale=1.0),
                reads=["psB", "bdtt"], writes=["xpb:%d" % b])
            yield lambda b=b: P.op("scalar", lambda s: s.activation(
                axb[b][:], xpb[b][:], AF.Abs, bias=0.0, scale=1.0),
                reads=["xpb:%d" % b], writes=["axb:%d" % b])
            yield lambda b=b: P.op("scalar", lambda s: s.activation(
                eab[b][:], axb[b][:], AF.Exp, bias=0.0, scale=-1.0),
                reads=["axb:%d" % b], writes=["eab:%d" % b])
            yield lambda b=b: P.op("scalar", lambda s: s.activation(
                lab[b][:], eab[b][:], AF.Ln, bias=csts[:, 1:2], scale=1.0),
                reads=["eab:%d" % b, "cst1"], writes=["lab:%d" % b])
            yield lambda b=b, mt=mt, p=p: P.op("vector", lambda v: v.scalar_tensor_tensor(
                delta_t[p][:, mt * CH:(mt + 1) * CH], xpb[b][:], 0.0, lab[b][:],
                ALU.max, ALU.add),
                reads=["xpb:%d" % b, "lab:%d" % b], writes=["dl:%d:%d" % (p, mt)])
            yield lambda mt=mt, p=p: P.op("vector", lambda v: v.tensor_mul(
                du_t[p][:, mt * CH:(mt + 1) * CH],
                delta_t[p][:, mt * CH:(mt + 1) * CH], ubf[:, mt * CH:(mt + 1) * CH]),
                reads=["dl:%d:%d" % (p, mt), "ubf:%d" % mt], writes=["du:%d:%d" % (p, mt)])

    def post_scan(sc):
        p = sc % 2
        g0 = sc * 16
        bodies = [(n, q) for n in range(16) for q in range(16)]
        L = len(bodies)

        def emit_bc(gg):
            n = gg % 16
            pp = (gg // 16) % 2
            bk = gg % 2
            e2 = gg % 3
            P.op("vector", lambda v, n=n, pp=pp: v.stream_shuffle(
                shstB[64:96, :], BCt[pp][64:96, :], [n] * 32),
                reads=["BCt:%d" % pp], writes=["shB"])
            P.op("tensor", lambda t, bk=bk: t.matmul(
                psBC[bk][:, 0:CH], ones2[64:65, 0:128], shstB[64:65, :],
                start=True, stop=True),
                reads=["shB", "ones2"], writes=["BbC:%d" % bk])
            P.op("vector", lambda v, n=n, pp=pp: v.stream_shuffle(
                shstC[64:96, :], BCt[pp][64:96, :], [16 + n] * 32),
                reads=["BCt:%d" % pp], writes=["shC"])
            P.op("tensor", lambda t, bk=bk: t.matmul(
                psBC[bk][:, CH:2 * CH], ones2[64:65, 0:128], shstC[64:65, :],
                start=True, stop=True),
                reads=["shC", "ones2"], wr_override=["BbC:%d" % bk])
            P.op("scalar", lambda s, bk=bk, e2=e2: s.activation(
                Bsb[e2][:], psBC[bk][:, 0:CH], AF.Copy, bias=0.0, scale=1.0),
                reads=["BbC:%d" % bk], writes=["Bsb:%d" % e2])
            P.op("scalar", lambda s, bk=bk, e2=e2: s.activation(
                Csb[e2][:], psBC[bk][:, CH:2 * CH], AF.Copy, bias=0.0, scale=1.0),
                reads=["BbC:%d" % bk], writes=["Csb:%d" % e2])

        def emit_dA(i):
            n, q = bodies[i]
            b = i % NDA
            P.op("scalar", lambda s, q=q, n=n, b=b: s.activation(
                dAb[b][:], delta_t[p][:, q * CH:(q + 1) * CH], AF.Exp,
                bias=0.0, scale=alneg[:, q * 16 + n:q * 16 + n + 1]),
                reads=["dl:%d:%d" % (p, q), "alneg"], writes=["dA:%d" % b])

        def emit_dbu(i):
            n, q = bodies[i]
            b3 = i % 3
            e2 = (g0 + n) % 3
            P.op("vector", lambda v, q=q, b3=b3, e2=e2: v.tensor_mul(
                dbub[b3][:], du_t[p][:, q * CH:(q + 1) * CH], Bsb[e2][:]),
                reads=["du:%d:%d" % (p, q), "Bsb:%d" % e2], writes=["dbu:%d" % b3])

        def emit_scan(i):
            b = i % NDA
            b3 = i % 3
            P.op("vector", lambda v, b=b, b3=b3: v.tensor_tensor_scan(
                hbuf[b3][:], dAb[b][:], dbub[b3][:], 0.0, ALU.mult, ALU.add),
                reads=["dA:%d" % b, "dbu:%d" % b3], writes=["h:%d" % b3])

        def emit_yc(i):
            n, q = bodies[i]
            b = i % 4
            b3 = i % 3
            e2 = (g0 + n) % 3
            if n == 0:
                P.op("vector", lambda v, q=q, b3=b3, e2=e2: v.tensor_mul(
                    yA[:, q * CH:(q + 1) * CH], hbuf[b3][:], Csb[e2][:]),
                    reads=["h:%d" % b3, "Csb:%d" % e2], writes=["yA:%d" % q])
            elif n % 2 == 1 and n != 15:
                P.op("vector", lambda v, q=q, b3=b3, e2=e2: v.tensor_mul(
                    pa[q][:], hbuf[b3][:], Csb[e2][:]),
                    reads=["h:%d" % b3, "Csb:%d" % e2], writes=["pa:%d" % q])
            else:
                P.op("vector", lambda v, q=q, b=b, b3=b3, e2=e2: v.tensor_mul(
                    ypb[b][:], hbuf[b3][:], Csb[e2][:]),
                    reads=["h:%d" % b3, "Csb:%d" % e2], writes=["yp:%d" % b])
                if n % 2 == 0:
                    P.op("vector", lambda v, q=q, b=b: v.tensor_add(
                        ypb[b][:], ypb[b][:], pa[q][:]),
                        reads=["yp:%d" % b, "pa:%d" % q], writes=["yp:%d" % b])
                P.op("vector", lambda v, q=q, b=b: v.tensor_add(
                    yA[:, q * CH:(q + 1) * CH],
                    yA[:, q * CH:(q + 1) * CH], ypb[b][:]),
                    reads=["yA:%d" % q, "yp:%d" % b], writes=["yA:%d" % q])

        for i in range(L + 2):
            if i < L:
                n, q = bodies[i]
                if sc == 0 and i == 0:
                    yield lambda: emit_bc(0)
                if q == 2 and n + 1 < 16:
                    yield lambda n=n: emit_bc(g0 + n + 1)
                if i == 226 and sc + 1 < NSC:
                    yield lambda: emit_bc(g0 + 16)
                yield lambda i=i: emit_dA(i)
                yield lambda i=i: emit_dbu(i)
            if 1 <= i <= L:
                yield lambda i=i: emit_scan(i - 1)
            if i >= 2:
                yield lambda i=i: emit_yc(i - 2)

        for q in range(16):
            yield lambda q=q, p=p: P.op("vector", lambda v: v.tensor_mul(
                gated[p][:, q * CH:(q + 1) * CH], yA[:, q * CH:(q + 1) * CH],
                sres[p][:, q * CH:(q + 1) * CH]),
                reads=["yA:%d" % q, "sr:%d:%d" % (p, q)], writes=["gt:%d:%d" % (p, q)])

    def post_out(sc):
        p = sc % 2
        t0 = sc * CH
        for mtok in range(2):
            yield lambda mtok=mtok, t0=t0: P.op("sync", lambda e: e.dma_start(
                xrel[mtok][:], xh.ap()[2 + t0 + mtok * 128:2 + t0 + (mtok + 1) * 128, :]),
                writes=["xrel:%d" % mtok], dma=True)
        for f in range(2):
            for kg in range(4):
                wb2 = (f * 4 + kg) % 2
                yield lambda kg=kg, f=f, wb2=wb2: P.op("sync", lambda e: e.dma_start(
                    wo_buf[wb2][:].rearrange("part (k c) -> part k c", k=4),
                    s_woutT.ap()[kg * 512:(kg + 1) * 512,
                                 f * 512:(f + 1) * 512].rearrange(
                        "(k part) c -> part k c", k=4)),
                    reads=["swout:%d:0" % kq for kq in range(kg * 4, kg * 4 + 4)],
                    writes=["wo:%d" % wb2], dma=True)
                for j in range(4):
                    kq = kg * 4 + j
                    for mtok in range(2):
                        ps_ = psOut if mtok == 0 else psO2
                        pst_ = "psOut" if mtok == 0 else "psO2"
                        yield lambda kq=kq, j=j, mtok=mtok, wb2=wb2, ps_=ps_, pst_=pst_, p=p: P.op(
                            "tensor", lambda t: t.matmul(
                                ps_[:, 0:512],
                                gated[p][:, kq * CH + mtok * 128:kq * CH + (mtok + 1) * 128],
                                wo_buf[wb2][:, j * 512:(j + 1) * 512],
                                start=(kq == 0), stop=(kq == 15)),
                            reads=["gt:%d:%d" % (p, kq), "wo:%d" % wb2],
                            writes=([pst_] if kq == 0 else ()),
                            wr_override=([pst_] if kq == 15 else ()))
            for mtok in range(2):
                ps_ = psOut if mtok == 0 else psO2
                pst_ = "psOut" if mtok == 0 else "psO2"
                yield lambda f=f, mtok=mtok, ps_=ps_, pst_=pst_: P.op(
                    "vector", lambda v: v.tensor_add(
                        mmo[mtok][:, f * 512:(f + 1) * 512], ps_[:, 0:512],
                        xrel[mtok][:, f * 512:(f + 1) * 512]),
                    reads=[pst_, "xrel:%d" % mtok], writes=["mmo:%d:%d" % (mtok, f)])
        for mtok in range(2):
            yield lambda mtok=mtok, t0=t0: P.op("gpsimd", lambda e: e.dma_start(
                s_mm.ap()[t0 + mtok * 128:t0 + (mtok + 1) * 128, :], mmo[mtok][:]),
                reads=["mmo:%d:0" % mtok, "mmo:%d:1" % mtok],
                writes=["smm:%d" % (t0 // 128 + mtok)], dma=True)

    def _drain(it):
        for fn in it:
            fn()

    def _zip(post_it, pre_it, r_post=4, r_pre=1):
        done_a = done_b = False
        while not done_b:
            for _ in range(r_post):
                fn = next(post_it, None)
                if fn is None:
                    done_a = True
                    break
                fn()
            for _ in range(r_pre):
                fn = next(pre_it, None)
                if fn is None:
                    done_b = True
                    break
                fn()
            if done_a and done_b:
                break

    def f0_abs():
        for wi, wsrc in enumerate((wgT, wuT, wdT)):
            rows, cols = (DM, INNER) if wi < 2 else (INNER, DM)
            nt = (rows // 128) * (cols // 1024)
            ncb = cols // 1024
            for t in range(nt):
                r, c = t // ncb, t % ncb
                b = t % 2
                yield lambda wsrc=wsrc, r=r, c=c, b=b: P.op("sync", lambda e: e.dma_start(
                    wab[b][:], wsrc.ap()[r * 128:(r + 1) * 128, c * 1024:(c + 1) * 1024]),
                    writes=["wab:%d" % b], dma=True)
                yield lambda wi=wi, t=t, b=b: P.op("scalar", lambda sx: sx.activation(
                    wab[b][:], wab[b][:], AF.Abs, bias=0.0, scale=1.0,
                    accum_out=acc[:, wi * 32 + t:wi * 32 + t + 1]),
                    reads=["wab:%d" % b], writes=["wab:%d" % b, "acc:%d:%d" % (wi, t)])

    f0_it = iter(f0_abs())

    def f0_take(nmax):
        for _ in range(nmax):
            fn = next(f0_it, None)
            if fn is None:
                return
            yield fn

    def _chain(*its):
        for it in its:
            for x in it:
                yield x

    _drain(pre_a_mm(0))
    _drain(pre_a_conv(0))
    _drain(pre_b(0))
    for sc in range(NSC):
        post_it = iter(post_scan(sc))
        if sc > 0:
            _zip(post_it, iter(post_out(sc - 1)), 1, 1)
        if sc + 1 < NSC:
            _zip(post_it, iter(pre_a_mm(sc + 1)), 2, 3)
            _drain(pre_a_conv(sc + 1))
            _zip(post_it, _chain(iter(pre_b(sc + 1)), f0_take(48)), 4, 1)
        else:
            _zip(post_it, f0_take(1 << 30), 4, 1)
        _drain(post_it)
    _drain(post_out(NSC - 1))
    _drain(f0_it)

    mstk.close()
    pmstk.close()
    P.barrier()

    P.op("tensor", lambda t: t.matmul(psA[0:1, 0:96], ones1[:], acc[:, 0:96],
                                      start=True, stop=True),
         reads=["ones1"] + ["acc:%d:%d" % (wi2, t2) for wi2 in range(3)
                            for t2 in range(32)],
         writes=["psA"], wr_override=["psA"])
    P.op("scalar", lambda s: s.activation(acc[0:1, 0:96], psA[0:1, 0:96], AF.Copy,
                                          bias=0.0, scale=1.0),
         reads=["psA"], writes=["accr"])
    for wi in range(3):
        P.op("vector", lambda v, wi=wi: v.reduce_sum(
            swrow[0:1, 12 + wi:13 + wi], acc[0:1, wi * 32:(wi + 1) * 32], axis=AXX),
            reads=["accr"], writes=["swsum"])
    P.op("vector", lambda v: v.tensor_scalar(swrow[0:1, 0:3], swrow[0:1, 12:15],
                                             1.0 / NW_ELEMS, 1e-5, ALU.mult, ALU.max),
         reads=["swsum"], writes=["swclip"])
    P.op("vector", lambda v: v.tensor_scalar_mul(swrow[0:1, 3:6], swrow[0:1, 0:3], 0.5),
         reads=["swclip"], writes=["swthr"])
    P.op("vector", lambda v: v.tensor_scalar_mul(swrow[0:1, 6:9], swrow[0:1, 0:3], -0.5),
         reads=["swclip"], writes=["swnthr"])
    P.op("tensor", lambda t: t.matmul(psA[:, 0:9], ones2[0:1, 0:128], swrow[0:1, 0:9],
                                      start=True, stop=True),
         reads=["ones2", "swclip", "swthr", "swnthr"], writes=["psA"], wr_override=["psA"])
    P.op("vector", lambda v: v.tensor_copy(swbc[:], psA[:, 0:9]),
         reads=["psA"], writes=["swbc"])


    fboth = ExitStack()
    psT = fboth.enter_context(nc.psum_tensor("psTf", [128, 1024], BF16))
    psC = fboth.enter_context(nc.psum_tensor("psC", [128, 512], F32))
    gu_bf = salloc(fboth, "gu_bf", [128, NT * INNER], BF16)
    fstk = ExitStack()
    xqfm = salloc(fstk, "xqfm", [128, 8 * TOK], BF16)
    wgq = [salloc(fstk, "wgq%d" % i, [128, 8 * 512], BF16) for i in range(2)]
    wuq = [salloc(fstk, "wuq%d" % i, [128, 8 * 512], BF16) for i in range(2)]
    wqf = [salloc(fstk, "wqf%d" % i, [128, 512], F32) for i in range(2)]
    wts_1 = salloc(fstk, "wts", [128, 512], F32)
    wts = [wts_1, wts_1]
    gsb = [salloc(fstk, "gsb%d" % i, [128, 512], BF16) for i in range(2)]
    usb = [salloc(fstk, "usb%d" % i, [128, 512], BF16) for i in range(2)]
    wdstg = salloc(fstk, "wdstg", [128, 512], BF16)
    wqf3_1 = salloc(fstk, "wqf3", [128, 512], F32)
    wqf3 = [wqf3_1, wqf3_1]
    psG2 = fstk.enter_context(nc.psum_tensor("psG2", [128, 512], F32))
    psU2 = fstk.enter_context(nc.psum_tensor("psU2", [128, 512], F32))
    psG3 = fstk.enter_context(nc.psum_tensor("psG3", [128, 512], F32))
    psU3 = fstk.enter_context(nc.psum_tensor("psU3", [128, 512], F32))
    psGpool = [(psB, "psB"), (psG2, "psG2"), (psG3, "psG3")]
    psUpool = [(psC, "psC"), (psU2, "psU2"), (psU3, "psU3")]

    for k2 in range(8):
        P.op("sync", lambda e, k2=k2: e.dma_start(
            xqfm[:, k2 * TOK:(k2 + 1) * TOK], s_xqfm.ap()[k2 * 128:(k2 + 1) * 128, :]),
            reads=["sxq:%d:%d" % (k2, i2) for i2 in range(NT)],
            writes=["xqfm:%d" % k2], dma=True)

    for i in range(NT):
        P.op("vector", lambda v, i=i: v.tensor_mul(
            combog[:, i:i + 1], invsx[:, i:i + 1], swbc[:, 0:1]),
            reads=["invsx:%d" % i, "swbc"], writes=["cg:%d" % i])
        P.op("vector", lambda v, i=i: v.tensor_mul(
            combou[:, i:i + 1], invsx[:, i:i + 1], swbc[:, 1:2]),
            reads=["invsx:%d" % i, "swbc"], writes=["cu:%d" % i])

    def f1_quant(f):
        fb = f % 2
        for (wsrc, wtag, col) in ((wgT, "wgq", 0), (wuT, "wuq", 1)):
            wdst = wgq[fb] if col == 0 else wuq[fb]
            for k2 in range(8):
                b = k2 % 2
                yield lambda wsrc=wsrc, k2=k2, f=f, b=b: P.op("sync", lambda e: e.dma_start(
                    wqf[b][:], wsrc.ap()[k2 * 128:(k2 + 1) * 128, f * 512:(f + 1) * 512]),
                    writes=["wqf:%d" % b], dma=True)
                yield lambda b=b, col=col: P.op("vector", lambda v: v.tensor_single_scalar(
                    wts[b][:], wqf[b][:], swbc[:, 6 + col:7 + col], ALU.is_le),
                    reads=["wqf:%d" % b, "swbc"], writes=["wts"])
                yield lambda b=b, col=col, wdst=wdst, k2=k2, wtag=wtag, fb=fb: P.op(
                    "vector", lambda v: v.scalar_tensor_tensor(
                        wdst[:, k2 * 512:(k2 + 1) * 512], wqf[b][:],
                        swbc[:, 3 + col:4 + col], wts[b][:], ALU.is_ge, ALU.subtract),
                    reads=["wqf:%d" % b, "wts", "swbc"],
                    writes=["%s:%d:%d" % (wtag, fb, k2)])

    def f1_mm(f):
        fb = f % 2
        for mt in range(NT):
            gb = mt % 2
            psg, psgt = psGpool[mt % 3]
            psu, psut = psUpool[mt % 3]
            for (wtag, ps_, pst_) in (("wgq", psg, psgt), ("wuq", psu, psut)):
                wq_ = wgq[fb] if wtag == "wgq" else wuq[fb]
                for k2 in range(8):
                    yield lambda wq_=wq_, k2=k2, mt=mt, ps_=ps_, pst_=pst_, wtag=wtag, fb=fb: P.op(
                        "tensor", lambda t: t.matmul(
                            ps_[:, 0:512],
                            xqfm[:, k2 * TOK + mt * 128:k2 * TOK + (mt + 1) * 128],
                            wq_[:, k2 * 512:(k2 + 1) * 512], start=(k2 == 0), stop=(k2 == 7)),
                        reads=["xqfm:%d" % k2, "%s:%d:%d" % (wtag, fb, k2)],
                        writes=([pst_] if k2 == 0 else ()),
                        wr_override=([pst_] if k2 == 7 else ()))
            yield lambda gb=gb, mt=mt, psg=psg, psgt=psgt: P.op("scalar", lambda s: s.activation(
                gsb[gb][:], psg[:, 0:512], AF.Silu, bias=0.0, scale=combog[:, mt:mt + 1]),
                reads=[psgt, "cg:%d" % mt], writes=["gsb:%d" % gb])
            yield lambda gb=gb, mt=mt, psu=psu, psut=psut: P.op("scalar", lambda s: s.activation(
                usb[gb][:], psu[:, 0:512], AF.Copy, bias=0.0, scale=combou[:, mt:mt + 1]),
                reads=[psut, "cu:%d" % mt], writes=["usb:%d" % gb])
            yield lambda gb=gb, mt=mt, f=f: P.op("vector", lambda v: v.tensor_mul(
                gu_bf[:, mt * INNER + f * 512:mt * INNER + (f + 1) * 512],
                gsb[gb][:], usb[gb][:]),
                reads=["gsb:%d" % gb, "usb:%d" % gb], writes=["gu:%d:%d" % (mt, f)])
            yield lambda gb=gb, mt=mt, f=f: P.op("vector", lambda v: v.reduce_max(
                gmparts[:, mt * 8 + f:mt * 8 + f + 1],
                gu_bf[:, mt * INNER + f * 512:mt * INNER + (f + 1) * 512],
                axis=AXX, apply_absolute_value=True),
                reads=["gu:%d:%d" % (mt, f)], writes=["gmp:%d:%d" % (mt, f)])

    def _zipf(a_it, b_it, ra=2, rb=1):
        done_a = done_b = False
        while not (done_a and done_b):
            for _ in range(ra):
                fn = next(a_it, None)
                if fn is None:
                    done_a = True
                    break
                fn()
            for _ in range(rb):
                fn = next(b_it, None)
                if fn is None:
                    done_b = True
                    break
                fn()

    def f1_wd():
        for t in range(32):
            for hcol in range(2):
                wb9 = (t * 2 + hcol) % 2
                yield lambda t=t, hcol=hcol, wb9=wb9: P.op("sync", lambda e: e.dma_start(
                    wqf3[wb9][:], wdT.ap()[t * 128:(t + 1) * 128, hcol * 512:(hcol + 1) * 512]),
                    writes=["wqf3"], dma=True)
                def _wdq_pair(wb9=wb9):
                    P.op("vector", lambda v: v.tensor_single_scalar(
                        wts[0][:], wqf3[wb9][:], swbc[:, 8:9], ALU.is_le),
                        reads=["wqf3", "swbc"], writes=["wts"])
                    P.op("vector", lambda v: v.scalar_tensor_tensor(
                        wdstg[:], wqf3[wb9][:], swbc[:, 5:6], wts[0][:], ALU.is_ge,
                        ALU.subtract),
                        reads=["wqf3", "wts", "swbc"], writes=["wdstg"])
                yield _wdq_pair
                yield lambda t=t, hcol=hcol: P.op("gpsimd", lambda e: e.dma_start(
                    s_wdq.ap()[t * 128:(t + 1) * 128, hcol * 512:(hcol + 1) * 512],
                    wdstg[:]),
                    reads=["wdstg"], writes=["swdq:%d:%d" % (t, hcol)], dma=True)

    wd_it = iter(f1_wd())

    def _sec(f):
        q_it = iter(f1_quant(f + 1)) if f + 1 < 8 else iter(())
        wd_left = 32 if f < 7 else 1 << 30
        while True:
            got = False
            for _ in range(3):
                fn = next(q_it, None)
                if fn is not None:
                    got = True
                    yield fn
            for _ in range(2):
                if wd_left <= 0:
                    break
                fn = next(wd_it, None)
                if fn is not None:
                    wd_left -= 1
                    got = True
                    yield fn
            if not got:
                return

    for fn in f1_quant(0):
        fn()
    for f in range(8):
        _zipf(iter(f1_mm(f)), _sec(f), 2, 1)
    for fn in wd_it:
        fn()

    for mt in range(NT):
        P.op("vector", lambda v, mt=mt: v.reduce_max(
            sgu[:, mt:mt + 1], gmparts[:, mt * 8:mt * 8 + 8], axis=AXX),
            reads=["gmp:%d:%d" % (mt, f2) for f2 in range(8)], writes=["gmax:%d" % mt])
        P.op("vector", lambda v, mt=mt: v.tensor_single_scalar(
            invsgu[:, mt:mt + 1], sgu[:, mt:mt + 1], 1e-5, ALU.max),
            reads=["gmax:%d" % mt], writes=["gclip:%d" % mt])
        P.op("vector", lambda v, mt=mt: v.reciprocal(
            gmparts[:, mt * 8:mt * 8 + 1], invsgu[:, mt:mt + 1]),
            reads=["gclip:%d" % mt], writes=["grcp:%d" % mt])
        P.op("vector", lambda v, mt=mt: v.tensor_scalar_mul(
            sgu[:, mt:mt + 1], gmparts[:, mt * 8:mt * 8 + 1], 127.0),
            reads=["grcp:%d" % mt], writes=["sgu:%d" % mt])
        P.op("vector", lambda v, mt=mt: v.tensor_scalar_mul(
            invsgu[:, mt:mt + 1], invsgu[:, mt:mt + 1], 1.0 / 127.0),
            reads=["gclip:%d" % mt], writes=["invsgu:%d" % mt])
        P.op("vector", lambda v, mt=mt: v.tensor_mul(
            combod[:, mt:mt + 1], invsgu[:, mt:mt + 1], swbc[:, 2:3]),
            reads=["invsgu:%d" % mt, "swbc"], writes=["cd:%d" % mt])

    fstk.close()
    P.barrier()

    f2stk = ExitStack()
    psP = [[psB, psC]] + [
        [f2stk.enter_context(nc.psum_tensor("psP%d_%d" % (pp, j), [128, 512], F32))
         for j in range(2)] for pp in (1, 2)]
    bob = salloc(f2stk, "bob", [128, DM], F32)
    tqg = [salloc(f2stk, "tqg%d" % i, [128, 1024], F32) for i in range(2)]
    tmg = tqg
    guqT = [[salloc(f2stk, "guqT%d_%d" % (gb, i), [128, INNER], BF16)
             for i in range(2)] for gb in range(2)]
    wdqt = [salloc(f2stk, "wdqt%d" % i, [128, 8 * 512], BF16) for i in range(2)]
    dnsb = [salloc(f2stk, "dnsb%d" % i, [128, DM], F32) for i in range(2)]
    smt = [salloc(f2stk, "smt%d" % i, [128, DM], F32) for i in range(2)]
    outsb = dnsb
    P.op("sync", lambda e: e.dma_start(bob[:], b_out_bc_d.ap()), writes=["bob"], dma=True)

    def f2_a(g):
        gb = g % 2
        mts = (2 * g, 2 * g + 1)
        for j, mt in enumerate(mts):
            for hh in range(4):
                hb = hh % 2
                yield lambda mt=mt, hh=hh, hb=hb: P.op("scalar", lambda s: s.activation(
                    tqg[hb][:], gu_bf[:, mt * INNER + hh * 1024:mt * INNER + (hh + 1) * 1024],
                    AF.Identity, bias=csts[:, 2:3], scale=sgu[:, mt:mt + 1]),
                    reads=["gu:%d:%d" % (mt, f2) for f2 in range(hh * 2, hh * 2 + 2)]
                          + ["sgu:%d" % mt, "cst2"],
                    writes=["tqg:%d" % hb])
                yield lambda hb=hb: P.op("vector", lambda v: v.tensor_single_scalar(
                    tmg[hb][:].bitcast(I32), tqg[hb][:].bitcast(I32), RMASK, ALU.bitwise_and),
                    reads=["tqg:%d" % hb], writes=["tqg:%d" % hb])
                yield lambda mt=mt, hh=hh, hb=hb: P.op("vector", lambda v: v.tensor_single_scalar(
                    gu_bf[:, mt * INNER + hh * 1024:mt * INNER + (hh + 1) * 1024],
                    tmg[hb][:], -384.0, ALU.add),
                    reads=["tqg:%d" % hb] + ["gu:%d:%d" % (mt, f2) for f2 in range(hh * 2, hh * 2 + 2)],
                    writes=["guqq:%d:%d" % (mt, hh)])
            for gb8 in range(4):
                for kb in range(8):
                    kk = gb8 * 8 + kb
                    yield lambda mt=mt, kk=kk, kb=kb: P.op("tensor", lambda t: t.matmul(
                        psT[:, kb * 128:(kb + 1) * 128],
                        gu_bf[:, mt * INNER + kk * 128:mt * INNER + (kk + 1) * 128],
                        eye[:], is_transpose=True, start=(kb == 0), stop=(kb == 7)),
                        reads=["guqq:%d:%d" % (mt, kk // 8), "eye"],
                        writes=(["psT"] if kb == 0 else ()),
                        wr_override=(["psT"] if kb == 7 else ()))
                yield lambda j=j, gb8=gb8, gb=gb: P.op("scalar", lambda s: s.activation(
                    guqT[gb][j][:, gb8 * 1024:(gb8 + 1) * 1024], psT[:, 0:1024], AF.Copy,
                    bias=0.0, scale=1.0),
                    reads=["psT"], writes=["guqT:%d:%d:%d" % (gb, j, gb8)])

    def f2_mm(g):
        gb = g % 2
        for f2 in range(2):
            pp = (2 * g + f2) % 3
            for kc in range(4):
                wb3 = kc % 2
                yield lambda kc=kc, f2=f2, wb3=wb3: P.op("sync", lambda e: e.dma_start(
                    wdqt[wb3][:].rearrange("part (k c) -> part k c", k=8),
                    s_wdq.ap()[kc * 1024:(kc + 1) * 1024,
                               f2 * 512:(f2 + 1) * 512].rearrange(
                        "(k part) c -> part k c", k=8)),
                    reads=["swdq:%d:%d" % (kc * 8 + k3, f2) for k3 in range(8)],
                    writes=["wdqt:%d" % wb3], dma=True)
                for k3 in range(8):
                    kb = kc * 8 + k3
                    for j in range(2):
                        yield lambda j=j, kb=kb, k3=k3, wb3=wb3, pp=pp, gb=gb: P.op(
                            "tensor", lambda t: t.matmul(
                                psP[pp][j][:, 0:512], guqT[gb][j][:, kb * 128:(kb + 1) * 128],
                                wdqt[wb3][:, k3 * 512:(k3 + 1) * 512],
                                start=(kb == 0), stop=(kb == 31)),
                            reads=["guqT:%d:%d:%d" % (gb, j, kb // 8), "wdqt:%d" % wb3],
                            writes=(["psP:%d:%d" % (pp, j)] if kb == 0 else ()),
                            wr_override=(["psP:%d:%d" % (pp, j)] if kb == 31 else ()))

    def f2_out(g):
        mts = (2 * g, 2 * g + 1)
        for f2 in range(2):
            pp = (2 * g + f2) % 3
            for j, mt in enumerate(mts):
                yield lambda j=j, mt=mt, f2=f2, pp=pp: P.op(
                    "scalar", lambda s: s.activation(
                        dnsb[j][:, f2 * 512:(f2 + 1) * 512], psP[pp][j][:, 0:512], AF.Copy,
                        bias=0.0, scale=combod[:, mt:mt + 1]),
                    reads=["psP:%d:%d" % (pp, j), "cd:%d" % mt], writes=["dn:%d:%d" % (j, f2)])
        for j, mt in enumerate(mts):
            yield lambda j=j, mt=mt: P.op("sync", lambda e: e.dma_start(
                smt[j][:], s_mm.ap()[mt * 128:(mt + 1) * 128, :]),
                reads=["smm:%d" % mt], writes=["smt:%d" % j], dma=True)
            yield lambda j=j: P.op("vector", lambda v: v.tensor_add(
                outsb[j][:], dnsb[j][:], smt[j][:]),
                reads=["dn:%d:0" % j, "dn:%d:1" % j, "smt:%d" % j],
                writes=["dn:%d:0" % j, "dn:%d:1" % j])
            yield lambda j=j: P.op("vector", lambda v: v.tensor_add(
                outsb[j][:], outsb[j][:], bob[:]),
                reads=["dn:%d:0" % j, "dn:%d:1" % j, "bob"],
                writes=["dn:%d:0" % j, "dn:%d:1" % j])
            yield lambda j=j, mt=mt: P.op("gpsimd", lambda e: e.dma_start(
                out_d.ap()[mt * 128:(mt + 1) * 128, :], outsb[j][:]),
                reads=["dn:%d:0" % j, "dn:%d:1" % j], writes=["outd:%d" % mt], dma=True)

    for fn in f2_a(0):
        fn()
    for g in range(NT // 2):
        sec = _chain(iter(f2_out(g - 1)) if g > 0 else iter(()),
                     iter(f2_a(g + 1)) if g + 1 < NT // 2 else iter(()))
        _zipf(iter(f2_mm(g)), sec, 3, 2)
    for fn in f2_out(NT // 2 - 1):
        fn()

    f2stk.close()
    fboth.close()

    P.op("sync", None, reads=["outd:%d" % mt for mt in range(NT)])
    P.op("gpsimd", None, reads=["outd:%d" % mt for mt in range(NT)])

    P.emit()
    pstack.close()
    nc.finalize()
    return nc


_NC_CACHE = {}


def _get_nc(TOK):
    if TOK not in _NC_CACHE:
        _NC_CACHE[TOK] = build_program(TOK)
    return _NC_CACHE[TOK]


def _host_inputs(inputs, TOK=2048):
    f32 = np.float32
    x = np.asarray(inputs["L0_attn_x"], dtype=f32)
    B, L, _ = x.shape
    flat = np.ascontiguousarray(x.reshape(B * L, DM))
    W_in = np.asarray(inputs["L0_attn_W_in"], dtype=f32)
    b_in = np.asarray(inputs["L0_attn_b_in"], dtype=f32)
    conv_w = np.asarray(inputs["L0_attn_conv_w"], dtype=f32)
    conv_b = np.asarray(inputs["L0_attn_conv_b"], dtype=f32)
    W_x = np.asarray(inputs["L0_attn_W_x"], dtype=f32)
    W_dt = np.asarray(inputs["L0_attn_W_dt"], dtype=f32)
    b_dt = np.asarray(inputs["L0_attn_b_dt"], dtype=f32)
    A_log = np.asarray(inputs["L0_attn_A_log"], dtype=f32)
    W_out = np.asarray(inputs["L0_attn_W_out"], dtype=f32)
    b_out = np.asarray(inputs["L0_attn_b_out"], dtype=f32)
    w_gate = np.asarray(inputs["L0_ffn_w_gate"], dtype=f32)
    w_up = np.asarray(inputs["L0_ffn_w_up"], dtype=f32)
    w_down = np.asarray(inputs["L0_ffn_w_down"], dtype=f32)

    shared = {
        "w_inT": np.ascontiguousarray(W_in.T),
        "w_xT": np.ascontiguousarray(W_x.T),
        "w_dtT": np.ascontiguousarray(W_dt.T),
        "w_outT": np.ascontiguousarray(W_out.T),
        "wgT": np.ascontiguousarray(w_gate.T),
        "wuT": np.ascontiguousarray(w_up.T),
        "wdT": np.ascontiguousarray(w_down.T),
        "b_in_t": np.ascontiguousarray(b_in.reshape(32, 128).T),
        "conv_w_t": np.ascontiguousarray(
            conv_w[:, 0, :].reshape(16, 128, 3).transpose(1, 2, 0).reshape(128, 48)),
        "conv_b_t": np.ascontiguousarray(conv_b.reshape(16, 128).T),
        "b_dt_t": np.ascontiguousarray(b_dt.reshape(16, 128).T),
        "alog_t": np.ascontiguousarray(
            A_log.reshape(16, 128, 16).transpose(1, 0, 2).reshape(128, 256)),
        "b_out_bc": np.ascontiguousarray(np.broadcast_to(b_out.reshape(1, DM), (128, DM))),
        "eye_in": np.eye(128, dtype=BFNP),
    }
    ncores = (B * L) // TOK
    in_maps = []
    for k in range(ncores):
        start = k * TOK
        seq_start = (start % L) == 0
        halo = np.zeros((2, DM), f32) if seq_start else flat[start - 2:start]
        xh_k = np.ascontiguousarray(np.concatenate([halo, flat[start:start + TOK]], axis=0))
        hm = np.full((128, 2), 0.0 if seq_start else 1.0, dtype=f32)
        m = dict(shared)
        m["xh"] = xh_k
        m["hmask_in"] = hm
        in_maps.append(m)
    return in_maps, (B, L)


def kernel(**inputs):
    TOK = 2048
    in_maps, (B, L) = _host_inputs(inputs, TOK)
    nc = _get_nc(TOK)
    res = run_bass_kernel_spmd(nc, in_maps, core_ids=list(range(len(in_maps))))
    outs = [np.asarray(r["out"]) for r in res.results]
    full = np.concatenate(outs, axis=0).reshape(B, L, DM)
    return full.astype(np.float32)


# version 30
# speedup vs baseline: 1.0370x; 1.0370x over previous
import numpy as np
import ml_dtypes
from contextlib import ExitStack

import concourse.bass as bass
import concourse.mybir as mybir
from concourse.bass_utils import run_bass_kernel_spmd

F32 = mybir.dt.float32
BF16 = mybir.dt.bfloat16
I32 = mybir.dt.int32
AF = mybir.ActivationFunctionType
ALU = mybir.AluOpType
AXX = mybir.AxisListType.X
BFNP = ml_dtypes.bfloat16

DM = 1024
DI = 2048
NS = 16
DTR = 64
INNER = 4096
CH = 256
NCORES = 8
NW_ELEMS = float(INNER * DM)
RMASK = 0xFFFF8000 - (1 << 32)
RBIAS = 384.5


class _Op:
    __slots__ = ("q", "dma", "fn", "waits", "inc", "cum", "skey")

    def __init__(self):
        self.waits = set()
        self.inc = False


class Prog:
    QUEUES = ("sync", "gpsimd", "tensor", "scalar", "vector")
    DMA_SLOTS = 8

    def __init__(self, nc):
        self.nc = nc
        self.qops = {q: [] for q in self.QUEUES}
        self.streams = {}
        self.slotlast = {}
        self.dmacnt = {q: 0 for q in self.QUEUES}
        self.tags = {}
        self.pend = {}

    def op(self, q, fn, reads=(), writes=(), dma=False, wr_override=()):
        o = _Op()
        o.q, o.dma, o.fn = q, dma, fn
        pend = self.pend.pop(q, None)
        deps = set(pend) if pend else set()
        for t in reads:
            e = self.tags.get(t)
            if e is not None and e[0] is not None:
                deps.add(e[0])
        for t in writes:
            e = self.tags.get(t)
            if e is not None:
                if e[0] is not None:
                    deps.add(e[0])
                deps.update(e[1])
        if dma:
            slot = self.dmacnt[q] % self.DMA_SLOTS
            self.dmacnt[q] += 1
            o.skey = (q, "d%d" % slot)
            prev = self.slotlast.get(o.skey)
            if prev is not None:
                deps.add(prev)
            self.slotlast[o.skey] = o
            o.inc = True
        else:
            o.skey = (q, "c")
        for d in deps:
            if d is o:
                continue
            d.inc = True
            o.waits.add(d)
        for t in reads:
            self.tags.setdefault(t, [None, []])[1].append(o)
        for t in writes:
            self.tags[t] = [o, []]
        for t in wr_override:
            self.tags[t] = [o, []]
        c = self.streams.get(o.skey, 0) + 1
        self.streams[o.skey] = c
        o.cum = c * (16 if dma else 1)
        self.qops[q].append(o)
        return o

    def barrier(self):
        last = set()
        for q in self.QUEUES:
            for o in reversed(self.qops[q]):
                if not o.dma:
                    last.add(o)
                    o.inc = True
                    break
        for o in self.slotlast.values():
            last.add(o)
            o.inc = True
        for q in self.QUEUES:
            self.pend[q] = self.pend.get(q, set()) | set(last)

    def emit(self):
        nc = self.nc
        es = ExitStack()
        sems = {}
        for key in self.streams:
            sems[key] = es.enter_context(nc.semaphore("sem_%s_%s" % key))
        with nc.Block() as block:
            for q in self.QUEUES:
                lst = self.qops[q]
                if not lst:
                    continue

                def body(eng, lst=lst, q=q):
                    sat = {}
                    last_inc = {}
                    for o in lst:
                        wl = {}
                        for d in o.waits:
                            if d.cum > wl.get(d.skey, 0):
                                wl[d.skey] = d.cum
                        for k in sorted(wl, key=lambda t: (t[0], t[1])):
                            v = wl[k]
                            if v > sat.get(k, 0):
                                eng.wait_ge(sems[k], v)
                                sat[k] = v
                        inst = o.fn(eng) if o.fn is not None else None
                        if o.inc:
                            k = o.skey
                            amt = o.cum - last_inc.get(k, 0)
                            inst.then_inc(sems[k], amt)
                            last_inc[k] = o.cum

                getattr(block, q)(body)
        es.close()


def build_program(TOK):
    NT = TOK // 128
    NSC = TOK // CH
    nc = bass.Bass("TRN2", target_bir_lowering=False)

    xh = nc.dram_tensor("xh", [TOK + 2, DM], F32, kind="ExternalInput")
    w_inT = nc.dram_tensor("w_inT", [DM, 2 * DI], F32, kind="ExternalInput")
    w_xT = nc.dram_tensor("w_xT", [DI, 96], F32, kind="ExternalInput")
    w_dtT = nc.dram_tensor("w_dtT", [DTR, DI], F32, kind="ExternalInput")
    w_outT = nc.dram_tensor("w_outT", [DI, DM], F32, kind="ExternalInput")
    wgT = nc.dram_tensor("wgT", [DM, INNER], F32, kind="ExternalInput")
    wuT = nc.dram_tensor("wuT", [DM, INNER], F32, kind="ExternalInput")
    wdT = nc.dram_tensor("wdT", [INNER, DM], F32, kind="ExternalInput")
    b_in_t = nc.dram_tensor("b_in_t", [128, 32], F32, kind="ExternalInput")
    conv_w_t = nc.dram_tensor("conv_w_t", [128, 48], F32, kind="ExternalInput")
    conv_b_t = nc.dram_tensor("conv_b_t", [128, 16], F32, kind="ExternalInput")
    b_dt_t = nc.dram_tensor("b_dt_t", [128, 16], F32, kind="ExternalInput")
    alog_t = nc.dram_tensor("alog_t", [128, 256], F32, kind="ExternalInput")
    b_out_bc_d = nc.dram_tensor("b_out_bc", [128, DM], F32, kind="ExternalInput")
    eye_in = nc.dram_tensor("eye_in", [128, 128], BF16, kind="ExternalInput")
    hmask_in = nc.dram_tensor("hmask_in", [128, 2], F32, kind="ExternalInput")
    out_d = nc.dram_tensor("out", [TOK, DM], F32, kind="ExternalOutput")

    s_winT = nc.dram_tensor("s_winT", [DM, 2 * DI], BF16)
    s_woutT = nc.dram_tensor("s_woutT", [DI, DM], BF16)
    s_xnfm = nc.dram_tensor("s_xnfm", [DM, TOK], BF16)
    s_xqfm = nc.dram_tensor("s_xqfm", [DM, TOK], BF16)
    s_wdq = nc.dram_tensor("s_wdq", [INNER, DM], BF16)
    s_mm = nc.dram_tensor("s_mm", [TOK, DM], F32)

    P = Prog(nc)
    half = 0.5

    pstack = ExitStack()

    def salloc(stack, name, shape, dt):
        return stack.enter_context(nc.sbuf_tensor(name, shape, dt))

    hmask = salloc(pstack, "hmask", [128, 2], F32)
    eye = salloc(pstack, "eye", [128, 128], BF16)
    invsx = salloc(pstack, "invsx", [128, NT], F32)
    swrow = salloc(pstack, "swrow", [1, 16], F32)
    swbc = salloc(pstack, "swbc", [128, 9], F32)
    ones2 = salloc(pstack, "ones2", [128, 128], F32)
    combog = salloc(pstack, "combog", [128, NT], F32)
    combou = salloc(pstack, "combou", [128, NT], F32)
    combod = salloc(pstack, "combod", [128, NT], F32)
    gmparts = salloc(pstack, "gmparts", [128, NT * 8], F32)
    sgu = salloc(pstack, "sgu", [128, NT], F32)
    invsgu = salloc(pstack, "invsgu", [128, NT], F32)
    ones1 = salloc(pstack, "ones1", [128, 1], F32)
    csts = salloc(pstack, "csts", [128, 3], F32)
    acc = salloc(pstack, "acc", [128, 96], F32)

    psA = pstack.enter_context(nc.psum_tensor("psA", [128, 512], F32))
    psB = pstack.enter_context(nc.psum_tensor("psB", [128, 512], F32))

    pmstk = ExitStack()
    psI = [pmstk.enter_context(nc.psum_tensor("psI%d" % i, [128, 256], F32)) for i in range(2)]
    bint = salloc(pmstk, "bint", [128, 32], F32)
    cwt = salloc(pmstk, "cwt", [128, 48], F32)
    cbt = salloc(pmstk, "cbt", [128, 16], F32)
    bdtt = salloc(pmstk, "bdtt", [128, 16], F32)
    alneg = salloc(pmstk, "alneg", [128, 256], F32)
    wxT_bf = salloc(pmstk, "wxT_bf", [128, 16 * 96], BF16)
    wdtT_bf = salloc(pmstk, "wdtT_bf", [64, DI], BF16)
    xnh_fm = salloc(pmstk, "xnh_fm", [128, 16], BF16)

    p0 = ExitStack()
    psT0 = [p0.enter_context(nc.psum_tensor("psT%d" % i, [128, 1024], BF16))
            for i in range(2)]
    ROT = 3
    sxs = salloc(p0, "sxs", [128, NT], F32)
    xt = [salloc(p0, "xt%d" % i, [128, DM], F32) for i in range(4)]
    dump = [salloc(p0, "dump%d" % i, [128, DM], F32) for i in range(2)]
    stats = [salloc(p0, "stats%d" % i, [128, 16], F32) for i in range(ROT)]
    xn = [salloc(p0, "xn%d" % i, [128, DM], F32) for i in range(ROT)]
    xnb = [salloc(p0, "xnb%d" % i, [128, DM], BF16) for i in range(ROT)]
    tq0 = [salloc(p0, "tq0%d" % i, [128, DM], F32) for i in range(ROT)]
    tm0 = [salloc(p0, "tm0%d" % i, [128, DM], F32) for i in range(ROT)]
    xqb = [salloc(p0, "xqb%d" % i, [128, DM], BF16) for i in range(ROT)]
    trev = [salloc(p0, "trev%d" % i, [128, DM], BF16) for i in range(ROT)]
    wcf = [salloc(p0, "wcf%d" % i, [128, 4096], F32) for i in range(2)]
    wcb = [salloc(p0, "wcb%d" % i, [128, 4096], BF16) for i in range(2)]
    xhalo = salloc(p0, "xhalo", [2, DM], F32)
    hstat = salloc(p0, "hstat", [2, 16], F32)
    xnhb = salloc(p0, "xnhb", [2, DM], BF16)
    asum = salloc(p0, "asum", [128, 3], F32)

    P.op("sync", lambda e: e.dma_start(bint[:], b_in_t.ap()), writes=["bint"], dma=True)
    P.op("sync", lambda e: e.dma_start(cwt[:], conv_w_t.ap()), writes=["cwt"], dma=True)
    P.op("sync", lambda e: e.dma_start(cbt[:], conv_b_t.ap()), writes=["cbt"], dma=True)
    P.op("sync", lambda e: e.dma_start(bdtt[:], b_dt_t.ap()), writes=["bdtt"], dma=True)
    P.op("sync", lambda e: e.dma_start(alneg[:], alog_t.ap()), writes=["alraw"], dma=True)
    P.op("sync", lambda e: e.dma_start(hmask[:], hmask_in.ap()), writes=["hmask"], dma=True)
    P.op("sync", lambda e: e.dma_start(eye[:], eye_in.ap()), writes=["eye"], dma=True)
    P.op("vector", lambda v: v.memset(ones1[:], 1.0), writes=["ones1"])
    P.op("vector", lambda v: v.memset(csts[:, 0:1], 1e-5), writes=["cst0"])
    P.op("vector", lambda v: v.memset(csts[:, 1:2], 1.0), writes=["cst1"])
    P.op("vector", lambda v: v.memset(csts[:, 2:3], RBIAS), writes=["cst2"])
    P.op("vector", lambda v: v.memset(ones2[:], 1.0), writes=["ones2"])

    P.op("scalar", lambda s: s.activation(alneg[:], alneg[:], AF.Exp, bias=0.0, scale=1.0),
         reads=["alraw"], writes=["alneg_t"])
    P.op("vector", lambda v: v.tensor_scalar_mul(alneg[:], alneg[:], -1.0),
         reads=["alneg_t"], writes=["alneg"])

    P.op("sync", lambda e: e.dma_start(
        wcf[0][:, 0:16 * 96].rearrange("part (k c) -> part k c", k=16),
        w_xT.ap().rearrange("(k part) c -> part k c", k=16)),
        writes=["wcf:0"], dma=True)
    P.op("vector", lambda v: v.tensor_copy(wxT_bf[:], wcf[0][:, 0:16 * 96]),
         reads=["wcf:0"], writes=["wxT_bf"])
    P.op("sync", lambda e: e.dma_start(wcf[1][0:64, 0:DI], w_dtT.ap()),
         writes=["wcf:1"], dma=True)
    P.op("vector", lambda v: v.tensor_copy(wdtT_bf[:], wcf[1][0:64, 0:DI]),
         reads=["wcf:1"], writes=["wdtT_bf"])

    def wcast_gen():
        wcnt = 0
        for (src, dst, rows, cols, tagbase) in ((w_inT, s_winT, DM, 2 * DI, "swin"),
                                                (w_outT, s_woutT, DI, DM, "swout")):
            ntile = rows // 128
            ncol = cols // 4096 if cols >= 4096 else 1
            cw = 4096 if cols >= 4096 else cols
            for r in range(ntile):
                for c in range(ncol):
                    b = wcnt % 2
                    lq = "sync" if wcnt % 2 == 0 else "scalar"
                    wcnt += 1
                    yield lambda src=src, r=r, c=c, b=b, cw=cw, lq=lq: P.op(
                        lq, lambda e: e.dma_start(
                            wcf[b][:, 0:cw],
                            src.ap()[r * 128:(r + 1) * 128, c * cw:(c + 1) * cw]),
                        writes=["wcf:%d" % b], dma=True)
                    yield lambda b=b, cw=cw: P.op("vector", lambda v: v.tensor_copy(
                        wcb[b][:, 0:cw], wcf[b][:, 0:cw]),
                        reads=["wcf:%d" % b], writes=["wcb:%d" % b])
                    yield lambda dst=dst, r=r, c=c, b=b, cw=cw, tagbase=tagbase: P.op(
                        "gpsimd", lambda e: e.dma_start(
                            dst.ap()[r * 128:(r + 1) * 128, c * cw:(c + 1) * cw],
                            wcb[b][:, 0:cw]),
                        reads=["wcb:%d" % b],
                        writes=["%s:%d:%d" % (tagbase, r, c)], dma=True)

    def _tile_ops(i):
        if True:
            ishalo = (i == NT)
            part = 2 if ishalo else 128
            b = i % ROT
            bx = i % 4
            db = i % 2
            xtt = xhalo if ishalo else xt[bx]
            stt = hstat if ishalo else stats[b]
            xnt = xn[b][0:part, :] if not ishalo else xn[0][0:2, :]
            xnbt = xnhb if ishalo else xnb[b]
            rt = "h" if ishalo else str(bx)
            strt = "hs" if ishalo else "s%d" % b
            txn = ("xn:%d" % b) if not ishalo else "xn:0"
            txnb = ("xnb:%d" % b) if not ishalo else "xnb:h"

            if ishalo:
                yield lambda: P.op("sync", lambda e: e.dma_start(
                    xhalo[:], xh.ap()[0:2, :]),
                    writes=["xt:h"], dma=True)
            else:
                yield lambda i=i, bx=bx, rt=rt: P.op("sync", lambda e: e.dma_start(
                    xt[bx][:], xh.ap()[2 + i * 128:2 + (i + 1) * 128, :]),
                    writes=["xt:" + rt], dma=True)
            xv = xtt[0:part, :]
            sv_ = stt[0:part, :]
            yield lambda xv=xv, sv_=sv_, rt=rt, strt=strt: P.op(
                "vector", lambda v: v.reduce_sum(sv_[:, 0:1], xv, axis=AXX),
                reads=["xt:" + rt], writes=[strt + ":sum"])
            yield lambda xv=xv, sv_=sv_, db=db, part=part, rt=rt, strt=strt: P.op(
                "scalar", lambda s: s.activation(
                    dump[db][0:part, :], xv, AF.Square, bias=0.0, scale=1.0,
                    accum_out=sv_[:, 1:2]),
                reads=["xt:" + rt], writes=["dump:%d" % db, strt + ":sq"])
            yield lambda sv_=sv_, strt=strt: P.op("vector", lambda v: v.tensor_scalar_mul(
                sv_[:, 2:3], sv_[:, 0:1], 1.0 / DM),
                reads=[strt + ":sum"], writes=[strt + ":mean"])
            yield lambda sv_=sv_, strt=strt: P.op("vector", lambda v: v.tensor_scalar_mul(
                sv_[:, 3:4], sv_[:, 1:2], 1.0 / DM),
                reads=[strt + ":sq"], writes=[strt + ":ex2"])
            yield lambda sv_=sv_, strt=strt: P.op("vector", lambda v: v.tensor_mul(
                sv_[:, 4:5], sv_[:, 2:3], sv_[:, 2:3]),
                reads=[strt + ":mean"], writes=[strt + ":msq"])
            yield lambda sv_=sv_, strt=strt: P.op("vector", lambda v: v.tensor_sub(
                sv_[:, 5:6], sv_[:, 3:4], sv_[:, 4:5]),
                reads=[strt + ":ex2", strt + ":msq"], writes=[strt + ":var"])
            yield lambda sv_=sv_, part=part, strt=strt: P.op("scalar", lambda s: s.activation(
                sv_[:, 11:12], sv_[:, 5:6], AF.Sqrt, bias=csts[0:part, 0:1], scale=1.0),
                reads=[strt + ":var", "cst0"], writes=[strt + ":std"])
            yield lambda sv_=sv_, strt=strt: P.op("vector", lambda v: v.reciprocal(
                sv_[:, 6:7], sv_[:, 11:12]),
                reads=[strt + ":std"], writes=[strt + ":rstd"])
            yield lambda sv_=sv_, strt=strt: P.op("vector", lambda v: v.tensor_scalar(
                sv_[:, 7:8], sv_[:, 2:3], sv_[:, 6:7], -1.0, ALU.mult, ALU.mult),
                reads=[strt + ":mean", strt + ":rstd"], writes=[strt + ":mrs"])
            yield lambda xv=xv, sv_=sv_, xnt=xnt, rt=rt, strt=strt, txn=txn: P.op(
                "scalar", lambda s: s.activation(
                    xnt, xv, AF.Identity, bias=sv_[:, 7:8], scale=sv_[:, 6:7]),
                reads=["xt:" + rt, strt + ":rstd", strt + ":mrs"], writes=[txn])
            yield lambda xnt=xnt, xnbt=xnbt, part=part, txn=txn, txnb=txnb: P.op(
                "vector", lambda v: v.tensor_copy(xnbt[0:part, :], xnt),
                reads=[txn], writes=[txnb])
            if not ishalo:
                yield lambda xnt=xnt, sv_=sv_, txn=txn, strt=strt: P.op(
                    "vector", lambda v: v.reduce_max(
                        sv_[:, 9:10], xnt, axis=AXX, apply_absolute_value=True),
                    reads=[txn], writes=[strt + ":rmax"])
                yield lambda sv_=sv_, strt=strt: P.op("vector", lambda v: v.tensor_single_scalar(
                    sv_[:, 8:9], sv_[:, 9:10], 1e-5, ALU.max),
                    reads=[strt + ":rmax"], writes=[strt + ":clip"])
                yield lambda sv_=sv_, i=i, strt=strt: P.op("vector", lambda v: v.tensor_scalar_mul(
                    invsx[:, i:i + 1], sv_[:, 8:9], 1.0 / 127.0),
                    reads=[strt + ":clip"], writes=["invsx:%d" % i])
                yield lambda sv_=sv_, strt=strt: P.op("vector", lambda v: v.reciprocal(
                    sv_[:, 10:11], sv_[:, 8:9]),
                    reads=[strt + ":clip"], writes=[strt + ":rcp"])
                yield lambda sv_=sv_, i=i, strt=strt: P.op("vector", lambda v: v.tensor_scalar_mul(
                    sxs[:, i:i + 1], sv_[:, 10:11], 127.0),
                    reads=[strt + ":rcp"], writes=["sx:%d" % i])
                yield lambda xnt=xnt, b=b, i=i, txn=txn: P.op("scalar", lambda s: s.activation(
                    tq0[b][:], xnt, AF.Identity, bias=csts[:, 2:3], scale=sxs[:, i:i + 1]),
                    reads=[txn, "sx:%d" % i, "cst2"], writes=["tq0:%d" % b])
                yield lambda b=b: P.op("vector", lambda v: v.tensor_single_scalar(
                    tm0[b][:].bitcast(I32), tq0[b][:].bitcast(I32), RMASK, ALU.bitwise_and),
                    reads=["tq0:%d" % b], writes=["tm0:%d" % b])
                yield lambda b=b: P.op("vector", lambda v: v.tensor_single_scalar(
                    xqb[b][:], tm0[b][:], -384.0, ALU.add),
                    reads=["tm0:%d" % b], writes=["xqb:%d" % b])
                for wi, (srcb, srct, dstd, tg) in enumerate(
                        ((xnb[b], txnb, s_xnfm, "sxn"),
                         (xqb[b], "xqb:%d" % b, s_xqfm, "sxq"))):
                    tp = (i * 2 + wi) % 2
                    for kb in range(8):
                        yield lambda srcb=srcb, kb=kb, srct=srct, tp=tp: P.op(
                            "tensor", lambda t: t.matmul(
                                psT0[tp][:, kb * 128:(kb + 1) * 128],
                                srcb[:, kb * 128:(kb + 1) * 128],
                                eye[:], is_transpose=True, start=(kb == 0), stop=(kb == 7)),
                            reads=[srct, "eye"],
                            writes=(["psT:%d" % tp] if kb == 0 else ()),
                            wr_override=(["psT:%d" % tp] if kb == 7 else ()))
                    yield lambda b=b, tp=tp: P.op("scalar", lambda s: s.activation(
                        trev[b][:], psT0[tp][:, 0:1024], AF.Copy, bias=0.0, scale=1.0),
                        reads=["psT:%d" % tp], writes=["trev:%d" % b])
                    yield lambda b=b, dstd=dstd, i=i, tg=tg: P.op(
                        "gpsimd", lambda e: e.dma_start(
                            dstd.ap()[:, i * 128:(i + 1) * 128].rearrange(
                                "(k part) c -> part k c", k=8),
                            trev[b][:].rearrange("part (k c) -> part k c", k=8)),
                        reads=["trev:%d" % b],
                        writes=["%s:%d:%d" % (tg, kb, i) for kb in range(8)], dma=True)
            else:
                for kb in range(8):
                    yield lambda kb=kb: P.op("tensor", lambda t: t.matmul(
                        psT0[0][:, kb * 2:(kb + 1) * 2], xnhb[0:2, kb * 128:(kb + 1) * 128],
                        eye[0:2, 0:2], is_transpose=True, start=(kb == 0), stop=(kb == 7)),
                        reads=["xnb:h", "eye"],
                        writes=(["psT:0"] if kb == 0 else ()),
                        wr_override=(["psT:0"] if kb == 7 else ()))
                yield lambda: P.op("scalar", lambda s: s.activation(
                    xnh_fm[:], psT0[0][:, 0:16], AF.Copy, bias=0.0, scale=1.0),
                    reads=["psT:0"], writes=["xnh_fm"])

    def ln_skewed():
        tiles = []
        for i in range(NT):
            it = _tile_ops(i)
            tiles.append([fn for fn in it])
        cuts = (0, 1, 3, 10, 20, None)
        for j in range(NT + 4):
            for st in range(5):
                i = j - st
                if 0 <= i < NT:
                    a, b = cuts[st], cuts[st + 1]
                    for fn in tiles[i][a:b]:
                        yield fn
        for fn in _tile_ops(NT):
            yield fn

    _wc = iter(wcast_gen())
    _ln = iter(ln_skewed())
    while True:
        got = False
        for _ in range(5):
            fn = next(_ln, None)
            if fn is not None:
                got = True
                fn()
        fn = next(_wc, None)
        if fn is not None:
            got = True
            fn()
        if not got:
            break

    p0.close()
    P.barrier()

    mstk = ExitStack()
    psBC = [mstk.enter_context(nc.psum_tensor("psBC%d" % i, [128, 512], F32))
            for i in range(2)]
    psO2 = mstk.enter_context(nc.psum_tensor("psO2", [128, 512], F32))
    psOut = mstk.enter_context(nc.psum_tensor("psOut", [128, 512], F32))
    xnsc = [salloc(mstk, "xnsc%d" % i, [128, 8 * CH], BF16) for i in range(2)]
    wch = [salloc(mstk, "wch%d" % i, [128, 2048], BF16) for i in range(2)]
    CH2 = CH + 2
    xc_t = salloc(mstk, "xc_t", [128, 16 * CH2], F32)
    sres = [salloc(mstk, "sres%d" % i, [128, 16 * CH], BF16) for i in range(2)]
    delta_t = [salloc(mstk, "delta_t%d" % i, [128, 16 * CH], BF16) for i in range(2)]
    du_t = [salloc(mstk, "du_t%d" % i, [128, 16 * CH], BF16) for i in range(2)]
    dltbf = [salloc(mstk, "dltbf%d" % i, [64, CH], BF16) for i in range(2)]
    BCt = [salloc(mstk, "BCt%d" % i, [128, CH], F32) for i in range(2)]
    ubf = salloc(mstk, "ubf", [128, 16 * CH], BF16)
    yA = salloc(mstk, "yA", [128, 16 * CH], F32)
    gated = [salloc(mstk, "gated%d" % i, [128, 16 * CH], BF16) for i in range(2)]
    shstB = salloc(mstk, "shstB", [128, CH], F32)
    shstC = salloc(mstk, "shstC", [128, CH], F32)
    NDA = 26
    dAb = [salloc(mstk, "dAb%d" % i, [128, CH], BF16) for i in range(NDA)]
    dbub = [salloc(mstk, "dbub%d" % i, [128, CH], BF16) for i in range(3)]
    hbuf = [salloc(mstk, "hbuf%d" % i, [128, CH], BF16) for i in range(3)]
    ypb = [salloc(mstk, "ypb%d" % i, [128, CH], BF16) for i in range(4)]
    pa = [salloc(mstk, "pa%d" % i, [128, CH], BF16) for i in range(16)]
    Bsb = [salloc(mstk, "Bsb%d" % i, [128, CH], BF16) for i in range(3)]
    Csb = [salloc(mstk, "Csb%d" % i, [128, CH], BF16) for i in range(3)]
    cva = [salloc(mstk, "cva%d" % i, [128, CH], F32) for i in range(2)]
    cvb = [salloc(mstk, "cvb%d" % i, [128, CH], F32) for i in range(2)]
    cvc = cva
    xpb = [salloc(mstk, "xpb%d" % i, [128, CH], F32) for i in range(2)]
    axb = [salloc(mstk, "axb0", [128, CH], F32)] * 2
    eab = [salloc(mstk, "eab0", [128, CH], F32)] * 2
    lab = [salloc(mstk, "lab%d" % i, [128, CH], F32) for i in range(2)]
    hv = salloc(mstk, "hv", [128, 2], F32)
    hv2 = salloc(mstk, "hv2", [128, 2], F32)
    wo_buf = [salloc(mstk, "wo_buf%d" % i, [128, 4 * 512], BF16) for i in range(2)]
    xrel = [salloc(mstk, "xrel%d" % i, [128, DM], F32) for i in range(2)]
    mmo = [salloc(mstk, "mmo%d" % i, [128, DM], F32) for i in range(2)]
    wab = [salloc(mstk, "wab%d" % i, [128, 1024], F32) for i in range(2)]

    def pre_a_mm(sc):
        p = sc % 2
        t0 = sc * CH
        yield lambda: P.op("sync", lambda e: e.dma_start(
            xnsc[p][:].rearrange("part (k c) -> part k c", k=8),
            s_xnfm.ap()[:, t0:t0 + CH].rearrange("(k part) c -> part k c", k=8)),
            reads=["sxn:%d:%d" % (k2, (t0 // 128) + j) for k2 in range(8)
                   for j in range(CH // 128)],
            writes=["xnsc:%d" % p], dma=True)
        for mg in range(16):
            wb = mg % 2
            yield lambda mg=mg, wb=wb: P.op("sync", lambda e: e.dma_start(
                wch[wb][:].rearrange("part (k c) -> part k c", k=8),
                s_winT.ap()[:, mg * 256:(mg + 1) * 256].rearrange(
                    "(k part) c -> part k c", k=8)),
                reads=["swin:%d:0" % k2 for k2 in range(8)],
                writes=["wch:%d" % wb], dma=True)
            for m in range(2):
                mt = mg * 2 + m
                mp = m % 2
                for k2 in range(8):
                    yield lambda wb=wb, k2=k2, m=m, mp=mp, p=p: P.op(
                        "tensor", lambda t: t.matmul(
                            psI[mp][:, 0:CH],
                            wch[wb][:, k2 * 256 + m * 128:k2 * 256 + (m + 1) * 128],
                            xnsc[p][:, k2 * CH:(k2 + 1) * CH],
                            start=(k2 == 0), stop=(k2 == 7)),
                        reads=["wch:%d" % wb, "xnsc:%d" % p],
                        writes=(["psI:%d" % mp] if k2 == 0 else ()),
                        wr_override=(["psI:%d" % mp] if k2 == 7 else ()))
                if sc == 0 and mt < 16:
                    for k2 in range(8):
                        yield lambda wb=wb, k2=k2, m=m: P.op(
                            "tensor", lambda t: t.matmul(
                                psA[:, 0:2],
                                wch[wb][:, k2 * 256 + m * 128:k2 * 256 + (m + 1) * 128],
                                xnh_fm[:, k2 * 2:(k2 + 1) * 2],
                                start=(k2 == 0), stop=(k2 == 7)),
                            reads=["wch:%d" % wb, "xnh_fm"],
                            writes=(["psA"] if k2 == 0 else ()),
                            wr_override=(["psA"] if k2 == 7 else ()))
                    yield lambda mt=mt: P.op("scalar", lambda s: s.activation(
                        hv[:], psA[:, 0:2], AF.Identity, bias=bint[:, mt:mt + 1],
                        scale=1.0),
                        reads=["psA", "bint"], writes=["hv"])
                    yield lambda: P.op("vector", lambda v: v.tensor_mul(
                        hv2[:], hv[:], hmask[:]),
                        reads=["hv", "hmask"], writes=["hv2"])
                    yield lambda mt=mt: P.op("vector", lambda v: v.tensor_copy(
                        xc_t[:, mt * CH2:mt * CH2 + 2], hv2[:]),
                        reads=["hv2"], writes=["xch:%d" % mt])
                if mt < 16:
                    yield lambda mp=mp, mt=mt: P.op("scalar", lambda s: s.activation(
                        xc_t[:, mt * CH2 + 2:mt * CH2 + 2 + CH], psI[mp][:, 0:CH],
                        AF.Identity, bias=bint[:, mt:mt + 1], scale=1.0),
                        reads=["psI:%d" % mp, "bint"], writes=["xcb:%d" % mt])
                else:
                    yield lambda mp=mp, mt=mt, p=p: P.op("scalar", lambda s: s.activation(
                        sres[p][:, (mt - 16) * CH:(mt - 15) * CH], psI[mp][:, 0:CH],
                        AF.Identity, bias=bint[:, mt:mt + 1], scale=1.0),
                        reads=["psI:%d" % mp, "bint"], writes=["sr:%d:%d" % (p, mt - 16)])

    def pre_a_conv(sc):
        p = sc % 2
        for q in range(16):
            cb2 = q % 2
            yield lambda q=q, cb2=cb2: P.op("vector", lambda v: v.tensor_single_scalar(
                cva[cb2][:], xc_t[:, q * CH2 + 0:q * CH2 + CH],
                cwt[:, 0 + q:0 + q + 1], ALU.mult),
                reads=["xch:%d" % q, "xcb:%d" % q, "cwt"],
                writes=["cva:%d" % cb2])
            yield lambda q=q, cb2=cb2: P.op("vector", lambda v: v.scalar_tensor_tensor(
                cvb[cb2][:], xc_t[:, q * CH2 + 1:q * CH2 + 1 + CH],
                cwt[:, 16 + q:16 + q + 1], cva[cb2][:], ALU.mult, ALU.add),
                reads=["xch:%d" % q, "xcb:%d" % q, "cva:%d" % cb2, "cwt"],
                writes=["cvb:%d" % cb2])
            yield lambda q=q, cb2=cb2: P.op("vector", lambda v: v.scalar_tensor_tensor(
                cvc[cb2][:], xc_t[:, q * CH2 + 2:q * CH2 + 2 + CH],
                cwt[:, 32 + q:32 + q + 1], cvb[cb2][:], ALU.mult, ALU.add),
                reads=["xcb:%d" % q, "cvb:%d" % cb2, "cwt"],
                writes=["cva:%d" % cb2])
            yield lambda q=q, cb2=cb2: P.op("scalar", lambda s: s.activation(
                ubf[:, q * CH:(q + 1) * CH], cvc[cb2][:], AF.Silu,
                bias=cbt[:, q:q + 1], scale=1.0),
                reads=["cva:%d" % cb2, "cbt"], writes=["ubf:%d" % q])
            if sc + 1 < NSC:
                yield lambda q=q: P.op("vector", lambda v: v.tensor_copy(
                    xc_t[:, q * CH2:q * CH2 + 2],
                    xc_t[:, q * CH2 + CH:q * CH2 + CH + 2]),
                    reads=["xcb:%d" % q], writes=["xch:%d" % q])
        for q in range(16):
            yield lambda q=q, p=p: P.op("scalar", lambda s: s.activation(
                sres[p][:, q * CH:(q + 1) * CH], sres[p][:, q * CH:(q + 1) * CH],
                AF.Silu, bias=0.0, scale=1.0),
                reads=["sr:%d:%d" % (p, q)], writes=["sr:%d:%d" % (p, q)])

    def pre_b(sc):
        p = sc % 2
        for k in range(16):
            yield lambda k=k: P.op("tensor", lambda t: t.matmul(
                psA[0:96, 0:CH], wxT_bf[:, k * 96:(k + 1) * 96],
                ubf[:, k * CH:(k + 1) * CH], start=(k == 0), stop=(k == 15)),
                reads=["wxT_bf", "ubf:%d" % k],
                writes=(["psA"] if k == 0 else ()),
                wr_override=(["psA"] if k == 15 else ()))
        yield lambda p=p: P.op("scalar", lambda s: s.activation(
            dltbf[p][:, :], psA[0:64, 0:CH], AF.Copy, bias=0.0, scale=1.0),
            reads=["psA"], writes=["dltbf:%d" % p])
        yield lambda p=p: P.op("scalar", lambda s: s.activation(
            BCt[p][64:96, :], psA[64:96, 0:CH], AF.Copy, bias=0.0, scale=1.0),
            reads=["psA"], writes=["BCt:%d" % p])
        for mt in range(16):
            b = mt % 2
            yield lambda mt=mt, p=p: P.op("tensor", lambda t: t.matmul(
                psB[:, 0:CH], wdtT_bf[:, mt * 128:(mt + 1) * 128], dltbf[p][:, :],
                start=True, stop=True),
                reads=["wdtT_bf", "dltbf:%d" % p], writes=["psB"], wr_override=["psB"])
            yield lambda b=b, mt=mt: P.op("scalar", lambda s: s.activation(
                xpb[b][:], psB[:, 0:CH], AF.Identity, bias=bdtt[:, mt:mt + 1], scale=1.0),
                reads=["psB", "bdtt"], writes=["xpb:%d" % b])
            yield lambda b=b: P.op("scalar", lambda s: s.activation(
                axb[b][:], xpb[b][:], AF.Abs, bias=0.0, scale=1.0),
                reads=["xpb:%d" % b], writes=["axb:%d" % b])
            yield lambda b=b: P.op("scalar", lambda s: s.activation(
                eab[b][:], axb[b][:], AF.Exp, bias=0.0, scale=-1.0),
                reads=["axb:%d" % b], writes=["eab:%d" % b])
            yield lambda b=b: P.op("scalar", lambda s: s.activation(
                lab[b][:], eab[b][:], AF.Ln, bias=csts[:, 1:2], scale=1.0),
                reads=["eab:%d" % b, "cst1"], writes=["lab:%d" % b])
            yield lambda b=b, mt=mt, p=p: P.op("vector", lambda v: v.scalar_tensor_tensor(
                delta_t[p][:, mt * CH:(mt + 1) * CH], xpb[b][:], 0.0, lab[b][:],
                ALU.max, ALU.add),
                reads=["xpb:%d" % b, "lab:%d" % b], writes=["dl:%d:%d" % (p, mt)])
            yield lambda mt=mt, p=p: P.op("vector", lambda v: v.tensor_mul(
                du_t[p][:, mt * CH:(mt + 1) * CH],
                delta_t[p][:, mt * CH:(mt + 1) * CH], ubf[:, mt * CH:(mt + 1) * CH]),
                reads=["dl:%d:%d" % (p, mt), "ubf:%d" % mt], writes=["du:%d:%d" % (p, mt)])

    def post_scan(sc):
        p = sc % 2
        g0 = sc * 16
        bodies = [(n, q) for n in range(16) for q in range(16)]
        L = len(bodies)

        def emit_bc(gg):
            n = gg % 16
            pp = (gg // 16) % 2
            bk = gg % 2
            e2 = gg % 3
            P.op("vector", lambda v, n=n, pp=pp: v.stream_shuffle(
                shstB[64:96, :], BCt[pp][64:96, :], [n] * 32),
                reads=["BCt:%d" % pp], writes=["shB"])
            P.op("tensor", lambda t, bk=bk: t.matmul(
                psBC[bk][:, 0:CH], ones2[64:65, 0:128], shstB[64:65, :],
                start=True, stop=True),
                reads=["shB", "ones2"], writes=["BbC:%d" % bk])
            P.op("vector", lambda v, n=n, pp=pp: v.stream_shuffle(
                shstC[64:96, :], BCt[pp][64:96, :], [16 + n] * 32),
                reads=["BCt:%d" % pp], writes=["shC"])
            P.op("tensor", lambda t, bk=bk: t.matmul(
                psBC[bk][:, CH:2 * CH], ones2[64:65, 0:128], shstC[64:65, :],
                start=True, stop=True),
                reads=["shC", "ones2"], wr_override=["BbC:%d" % bk])
            P.op("scalar", lambda s, bk=bk, e2=e2: s.activation(
                Bsb[e2][:], psBC[bk][:, 0:CH], AF.Copy, bias=0.0, scale=1.0),
                reads=["BbC:%d" % bk], writes=["Bsb:%d" % e2])
            P.op("scalar", lambda s, bk=bk, e2=e2: s.activation(
                Csb[e2][:], psBC[bk][:, CH:2 * CH], AF.Copy, bias=0.0, scale=1.0),
                reads=["BbC:%d" % bk], writes=["Csb:%d" % e2])

        def emit_dA(i):
            n, q = bodies[i]
            b = i % NDA
            P.op("scalar", lambda s, q=q, n=n, b=b: s.activation(
                dAb[b][:], delta_t[p][:, q * CH:(q + 1) * CH], AF.Exp,
                bias=0.0, scale=alneg[:, q * 16 + n:q * 16 + n + 1]),
                reads=["dl:%d:%d" % (p, q), "alneg"], writes=["dA:%d" % b])

        def emit_dbu(i):
            n, q = bodies[i]
            b3 = i % 3
            e2 = (g0 + n) % 3
            P.op("vector", lambda v, q=q, b3=b3, e2=e2: v.tensor_mul(
                dbub[b3][:], du_t[p][:, q * CH:(q + 1) * CH], Bsb[e2][:]),
                reads=["du:%d:%d" % (p, q), "Bsb:%d" % e2], writes=["dbu:%d" % b3])

        def emit_scan(i):
            b = i % NDA
            b3 = i % 3
            P.op("vector", lambda v, b=b, b3=b3: v.tensor_tensor_scan(
                hbuf[b3][:], dAb[b][:], dbub[b3][:], 0.0, ALU.mult, ALU.add),
                reads=["dA:%d" % b, "dbu:%d" % b3], writes=["h:%d" % b3])

        def emit_yc(i):
            n, q = bodies[i]
            b = i % 4
            b3 = i % 3
            e2 = (g0 + n) % 3
            if n == 0:
                P.op("vector", lambda v, q=q, b3=b3, e2=e2: v.tensor_mul(
                    yA[:, q * CH:(q + 1) * CH], hbuf[b3][:], Csb[e2][:]),
                    reads=["h:%d" % b3, "Csb:%d" % e2], writes=["yA:%d" % q])
            elif n % 2 == 1 and n != 15:
                P.op("vector", lambda v, q=q, b3=b3, e2=e2: v.tensor_mul(
                    pa[q][:], hbuf[b3][:], Csb[e2][:]),
                    reads=["h:%d" % b3, "Csb:%d" % e2], writes=["pa:%d" % q])
            else:
                P.op("vector", lambda v, q=q, b=b, b3=b3, e2=e2: v.tensor_mul(
                    ypb[b][:], hbuf[b3][:], Csb[e2][:]),
                    reads=["h:%d" % b3, "Csb:%d" % e2], writes=["yp:%d" % b])
                if n % 2 == 0:
                    P.op("vector", lambda v, q=q, b=b: v.tensor_add(
                        ypb[b][:], ypb[b][:], pa[q][:]),
                        reads=["yp:%d" % b, "pa:%d" % q], writes=["yp:%d" % b])
                P.op("vector", lambda v, q=q, b=b: v.tensor_add(
                    yA[:, q * CH:(q + 1) * CH],
                    yA[:, q * CH:(q + 1) * CH], ypb[b][:]),
                    reads=["yA:%d" % q, "yp:%d" % b], writes=["yA:%d" % q])

        for i in range(L + 2):
            if i < L:
                n, q = bodies[i]
                if sc == 0 and i == 0:
                    yield lambda: emit_bc(0)
                if q == 2 and n + 1 < 16:
                    yield lambda n=n: emit_bc(g0 + n + 1)
                if i == 226 and sc + 1 < NSC:
                    yield lambda: emit_bc(g0 + 16)
                yield lambda i=i: emit_dA(i)
                yield lambda i=i: emit_dbu(i)
            if 1 <= i <= L:
                yield lambda i=i: emit_scan(i - 1)
            if i >= 2:
                yield lambda i=i: emit_yc(i - 2)

        for q in range(16):
            yield lambda q=q, p=p: P.op("vector", lambda v: v.tensor_mul(
                gated[p][:, q * CH:(q + 1) * CH], yA[:, q * CH:(q + 1) * CH],
                sres[p][:, q * CH:(q + 1) * CH]),
                reads=["yA:%d" % q, "sr:%d:%d" % (p, q)], writes=["gt:%d:%d" % (p, q)])

    def post_out(sc):
        p = sc % 2
        t0 = sc * CH
        for mtok in range(2):
            yield lambda mtok=mtok, t0=t0: P.op("sync", lambda e: e.dma_start(
                xrel[mtok][:], xh.ap()[2 + t0 + mtok * 128:2 + t0 + (mtok + 1) * 128, :]),
                writes=["xrel:%d" % mtok], dma=True)
        for f in range(2):
            for kg in range(4):
                wb2 = (f * 4 + kg) % 2
                yield lambda kg=kg, f=f, wb2=wb2: P.op("sync", lambda e: e.dma_start(
                    wo_buf[wb2][:].rearrange("part (k c) -> part k c", k=4),
                    s_woutT.ap()[kg * 512:(kg + 1) * 512,
                                 f * 512:(f + 1) * 512].rearrange(
                        "(k part) c -> part k c", k=4)),
                    reads=["swout:%d:0" % kq for kq in range(kg * 4, kg * 4 + 4)],
                    writes=["wo:%d" % wb2], dma=True)
                for j in range(4):
                    kq = kg * 4 + j
                    for mtok in range(2):
                        ps_ = psOut if mtok == 0 else psO2
                        pst_ = "psOut" if mtok == 0 else "psO2"
                        yield lambda kq=kq, j=j, mtok=mtok, wb2=wb2, ps_=ps_, pst_=pst_, p=p: P.op(
                            "tensor", lambda t: t.matmul(
                                ps_[:, 0:512],
                                gated[p][:, kq * CH + mtok * 128:kq * CH + (mtok + 1) * 128],
                                wo_buf[wb2][:, j * 512:(j + 1) * 512],
                                start=(kq == 0), stop=(kq == 15)),
                            reads=["gt:%d:%d" % (p, kq), "wo:%d" % wb2],
                            writes=([pst_] if kq == 0 else ()),
                            wr_override=([pst_] if kq == 15 else ()))
            for mtok in range(2):
                ps_ = psOut if mtok == 0 else psO2
                pst_ = "psOut" if mtok == 0 else "psO2"
                yield lambda f=f, mtok=mtok, ps_=ps_, pst_=pst_: P.op(
                    "vector", lambda v: v.tensor_add(
                        mmo[mtok][:, f * 512:(f + 1) * 512], ps_[:, 0:512],
                        xrel[mtok][:, f * 512:(f + 1) * 512]),
                    reads=[pst_, "xrel:%d" % mtok], writes=["mmo:%d:%d" % (mtok, f)])
        for mtok in range(2):
            yield lambda mtok=mtok, t0=t0: P.op("gpsimd", lambda e: e.dma_start(
                s_mm.ap()[t0 + mtok * 128:t0 + (mtok + 1) * 128, :], mmo[mtok][:]),
                reads=["mmo:%d:0" % mtok, "mmo:%d:1" % mtok],
                writes=["smm:%d" % (t0 // 128 + mtok)], dma=True)

    def _drain(it):
        for fn in it:
            fn()

    def _zip(post_it, pre_it, r_post=4, r_pre=1):
        done_a = done_b = False
        while not done_b:
            for _ in range(r_post):
                fn = next(post_it, None)
                if fn is None:
                    done_a = True
                    break
                fn()
            for _ in range(r_pre):
                fn = next(pre_it, None)
                if fn is None:
                    done_b = True
                    break
                fn()
            if done_a and done_b:
                break

    def f0_abs():
        for wi, wsrc in enumerate((wgT, wuT, wdT)):
            rows, cols = (DM, INNER) if wi < 2 else (INNER, DM)
            nt = (rows // 128) * (cols // 1024)
            ncb = cols // 1024
            for t in range(nt):
                r, c = t // ncb, t % ncb
                b = t % 2
                yield lambda wsrc=wsrc, r=r, c=c, b=b: P.op("gpsimd", lambda e: e.dma_start(
                    wab[b][:], wsrc.ap()[r * 128:(r + 1) * 128, c * 1024:(c + 1) * 1024]),
                    writes=["wab:%d" % b], dma=True)
                yield lambda wi=wi, t=t, b=b: P.op("scalar", lambda sx: sx.activation(
                    wab[b][:], wab[b][:], AF.Abs, bias=0.0, scale=1.0,
                    accum_out=acc[:, wi * 32 + t:wi * 32 + t + 1]),
                    reads=["wab:%d" % b], writes=["wab:%d" % b, "acc:%d:%d" % (wi, t)])

    f0_it = iter(f0_abs())

    def f0_take(nmax):
        for _ in range(nmax):
            fn = next(f0_it, None)
            if fn is None:
                return
            yield fn

    def _chain(*its):
        for it in its:
            for x in it:
                yield x

    _drain(pre_a_mm(0))
    _drain(pre_a_conv(0))
    _drain(pre_b(0))
    for sc in range(NSC):
        post_it = iter(post_scan(sc))
        if sc > 0:
            _zip(post_it, iter(post_out(sc - 1)), 1, 1)
        if sc + 1 < NSC:
            _zip(post_it, iter(pre_a_mm(sc + 1)), 2, 3)
            _drain(pre_a_conv(sc + 1))
            _zip(post_it, _chain(iter(pre_b(sc + 1)), f0_take(48)), 4, 1)
        else:
            _zip(post_it, f0_take(1 << 30), 4, 1)
        _drain(post_it)
    _drain(post_out(NSC - 1))
    _drain(f0_it)

    mstk.close()
    pmstk.close()
    P.barrier()

    P.op("tensor", lambda t: t.matmul(psA[0:1, 0:96], ones1[:], acc[:, 0:96],
                                      start=True, stop=True),
         reads=["ones1"] + ["acc:%d:%d" % (wi2, t2) for wi2 in range(3)
                            for t2 in range(32)],
         writes=["psA"], wr_override=["psA"])
    P.op("scalar", lambda s: s.activation(acc[0:1, 0:96], psA[0:1, 0:96], AF.Copy,
                                          bias=0.0, scale=1.0),
         reads=["psA"], writes=["accr"])
    for wi in range(3):
        P.op("vector", lambda v, wi=wi: v.reduce_sum(
            swrow[0:1, 12 + wi:13 + wi], acc[0:1, wi * 32:(wi + 1) * 32], axis=AXX),
            reads=["accr"], writes=["swsum"])
    P.op("vector", lambda v: v.tensor_scalar(swrow[0:1, 0:3], swrow[0:1, 12:15],
                                             1.0 / NW_ELEMS, 1e-5, ALU.mult, ALU.max),
         reads=["swsum"], writes=["swclip"])
    P.op("vector", lambda v: v.tensor_scalar_mul(swrow[0:1, 3:6], swrow[0:1, 0:3], 0.5),
         reads=["swclip"], writes=["swthr"])
    P.op("vector", lambda v: v.tensor_scalar_mul(swrow[0:1, 6:9], swrow[0:1, 0:3], -0.5),
         reads=["swclip"], writes=["swnthr"])
    P.op("tensor", lambda t: t.matmul(psA[:, 0:9], ones2[0:1, 0:128], swrow[0:1, 0:9],
                                      start=True, stop=True),
         reads=["ones2", "swclip", "swthr", "swnthr"], writes=["psA"], wr_override=["psA"])
    P.op("vector", lambda v: v.tensor_copy(swbc[:], psA[:, 0:9]),
         reads=["psA"], writes=["swbc"])


    fboth = ExitStack()
    psT = fboth.enter_context(nc.psum_tensor("psTf", [128, 1024], BF16))
    psC = fboth.enter_context(nc.psum_tensor("psC", [128, 512], F32))
    gu_bf = salloc(fboth, "gu_bf", [128, NT * INNER], BF16)
    fstk = ExitStack()
    xqfm = salloc(fstk, "xqfm", [128, 8 * TOK], BF16)
    wgq = [salloc(fstk, "wgq%d" % i, [128, 8 * 512], BF16) for i in range(2)]
    wuq = [salloc(fstk, "wuq%d" % i, [128, 8 * 512], BF16) for i in range(2)]
    wqf = [salloc(fstk, "wqf%d" % i, [128, 512], F32) for i in range(2)]
    wts_1 = salloc(fstk, "wts", [128, 512], F32)
    wts = [wts_1, wts_1]
    gsb = [salloc(fstk, "gsb%d" % i, [128, 512], BF16) for i in range(2)]
    usb = [salloc(fstk, "usb%d" % i, [128, 512], BF16) for i in range(2)]
    wdstg = salloc(fstk, "wdstg", [128, 512], BF16)
    wqf3_1 = salloc(fstk, "wqf3", [128, 512], F32)
    wqf3 = [wqf3_1, wqf3_1]
    psG2 = fstk.enter_context(nc.psum_tensor("psG2", [128, 512], F32))
    psU2 = fstk.enter_context(nc.psum_tensor("psU2", [128, 512], F32))
    psG3 = fstk.enter_context(nc.psum_tensor("psG3", [128, 512], F32))
    psU3 = fstk.enter_context(nc.psum_tensor("psU3", [128, 512], F32))
    psGpool = [(psB, "psB"), (psG2, "psG2"), (psG3, "psG3")]
    psUpool = [(psC, "psC"), (psU2, "psU2"), (psU3, "psU3")]

    for k2 in range(8):
        P.op("sync", lambda e, k2=k2: e.dma_start(
            xqfm[:, k2 * TOK:(k2 + 1) * TOK], s_xqfm.ap()[k2 * 128:(k2 + 1) * 128, :]),
            reads=["sxq:%d:%d" % (k2, i2) for i2 in range(NT)],
            writes=["xqfm:%d" % k2], dma=True)

    for i in range(NT):
        P.op("vector", lambda v, i=i: v.tensor_mul(
            combog[:, i:i + 1], invsx[:, i:i + 1], swbc[:, 0:1]),
            reads=["invsx:%d" % i, "swbc"], writes=["cg:%d" % i])
        P.op("vector", lambda v, i=i: v.tensor_mul(
            combou[:, i:i + 1], invsx[:, i:i + 1], swbc[:, 1:2]),
            reads=["invsx:%d" % i, "swbc"], writes=["cu:%d" % i])

    def f1_quant(f):
        fb = f % 2
        for (wsrc, wtag, col) in ((wgT, "wgq", 0), (wuT, "wuq", 1)):
            wdst = wgq[fb] if col == 0 else wuq[fb]
            for k2 in range(8):
                b = k2 % 2
                yield lambda wsrc=wsrc, k2=k2, f=f, b=b: P.op("sync", lambda e: e.dma_start(
                    wqf[b][:], wsrc.ap()[k2 * 128:(k2 + 1) * 128, f * 512:(f + 1) * 512]),
                    writes=["wqf:%d" % b], dma=True)
                yield lambda b=b, col=col: P.op("vector", lambda v: v.tensor_single_scalar(
                    wts[b][:], wqf[b][:], swbc[:, 6 + col:7 + col], ALU.is_le),
                    reads=["wqf:%d" % b, "swbc"], writes=["wts"])
                yield lambda b=b, col=col, wdst=wdst, k2=k2, wtag=wtag, fb=fb: P.op(
                    "vector", lambda v: v.scalar_tensor_tensor(
                        wdst[:, k2 * 512:(k2 + 1) * 512], wqf[b][:],
                        swbc[:, 3 + col:4 + col], wts[b][:], ALU.is_ge, ALU.subtract),
                    reads=["wqf:%d" % b, "wts", "swbc"],
                    writes=["%s:%d:%d" % (wtag, fb, k2)])

    def f1_mm(f):
        fb = f % 2
        for mt in range(NT):
            gb = mt % 2
            psg, psgt = psGpool[mt % 3]
            psu, psut = psUpool[mt % 3]
            for (wtag, ps_, pst_) in (("wgq", psg, psgt), ("wuq", psu, psut)):
                wq_ = wgq[fb] if wtag == "wgq" else wuq[fb]
                for k2 in range(8):
                    yield lambda wq_=wq_, k2=k2, mt=mt, ps_=ps_, pst_=pst_, wtag=wtag, fb=fb: P.op(
                        "tensor", lambda t: t.matmul(
                            ps_[:, 0:512],
                            xqfm[:, k2 * TOK + mt * 128:k2 * TOK + (mt + 1) * 128],
                            wq_[:, k2 * 512:(k2 + 1) * 512], start=(k2 == 0), stop=(k2 == 7)),
                        reads=["xqfm:%d" % k2, "%s:%d:%d" % (wtag, fb, k2)],
                        writes=([pst_] if k2 == 0 else ()),
                        wr_override=([pst_] if k2 == 7 else ()))
            yield lambda gb=gb, mt=mt, psg=psg, psgt=psgt: P.op("scalar", lambda s: s.activation(
                gsb[gb][:], psg[:, 0:512], AF.Silu, bias=0.0, scale=combog[:, mt:mt + 1]),
                reads=[psgt, "cg:%d" % mt], writes=["gsb:%d" % gb])
            yield lambda gb=gb, mt=mt, psu=psu, psut=psut: P.op("scalar", lambda s: s.activation(
                usb[gb][:], psu[:, 0:512], AF.Copy, bias=0.0, scale=combou[:, mt:mt + 1]),
                reads=[psut, "cu:%d" % mt], writes=["usb:%d" % gb])
            yield lambda gb=gb, mt=mt, f=f: P.op("vector", lambda v: v.tensor_mul(
                gu_bf[:, mt * INNER + f * 512:mt * INNER + (f + 1) * 512],
                gsb[gb][:], usb[gb][:]),
                reads=["gsb:%d" % gb, "usb:%d" % gb], writes=["gu:%d:%d" % (mt, f)])
            yield lambda gb=gb, mt=mt, f=f: P.op("vector", lambda v: v.reduce_max(
                gmparts[:, mt * 8 + f:mt * 8 + f + 1],
                gu_bf[:, mt * INNER + f * 512:mt * INNER + (f + 1) * 512],
                axis=AXX, apply_absolute_value=True),
                reads=["gu:%d:%d" % (mt, f)], writes=["gmp:%d:%d" % (mt, f)])

    def _zipf(a_it, b_it, ra=2, rb=1):
        done_a = done_b = False
        while not (done_a and done_b):
            for _ in range(ra):
                fn = next(a_it, None)
                if fn is None:
                    done_a = True
                    break
                fn()
            for _ in range(rb):
                fn = next(b_it, None)
                if fn is None:
                    done_b = True
                    break
                fn()

    def f1_wd():
        for t in range(32):
            for hcol in range(2):
                wb9 = (t * 2 + hcol) % 2
                yield lambda t=t, hcol=hcol, wb9=wb9: P.op("sync", lambda e: e.dma_start(
                    wqf3[wb9][:], wdT.ap()[t * 128:(t + 1) * 128, hcol * 512:(hcol + 1) * 512]),
                    writes=["wqf3"], dma=True)
                def _wdq_pair(wb9=wb9):
                    P.op("vector", lambda v: v.tensor_single_scalar(
                        wts[0][:], wqf3[wb9][:], swbc[:, 8:9], ALU.is_le),
                        reads=["wqf3", "swbc"], writes=["wts"])
                    P.op("vector", lambda v: v.scalar_tensor_tensor(
                        wdstg[:], wqf3[wb9][:], swbc[:, 5:6], wts[0][:], ALU.is_ge,
                        ALU.subtract),
                        reads=["wqf3", "wts", "swbc"], writes=["wdstg"])
                yield _wdq_pair
                yield lambda t=t, hcol=hcol: P.op("gpsimd", lambda e: e.dma_start(
                    s_wdq.ap()[t * 128:(t + 1) * 128, hcol * 512:(hcol + 1) * 512],
                    wdstg[:]),
                    reads=["wdstg"], writes=["swdq:%d:%d" % (t, hcol)], dma=True)

    wd_it = iter(f1_wd())

    def _sec(f):
        q_it = iter(f1_quant(f + 1)) if f + 1 < 8 else iter(())
        wd_left = 32 if f < 7 else 1 << 30
        while True:
            got = False
            for _ in range(3):
                fn = next(q_it, None)
                if fn is not None:
                    got = True
                    yield fn
            for _ in range(2):
                if wd_left <= 0:
                    break
                fn = next(wd_it, None)
                if fn is not None:
                    wd_left -= 1
                    got = True
                    yield fn
            if not got:
                return

    for fn in f1_quant(0):
        fn()
    for f in range(8):
        _zipf(iter(f1_mm(f)), _sec(f), 2, 1)
    for fn in wd_it:
        fn()

    for mt in range(NT):
        P.op("vector", lambda v, mt=mt: v.reduce_max(
            sgu[:, mt:mt + 1], gmparts[:, mt * 8:mt * 8 + 8], axis=AXX),
            reads=["gmp:%d:%d" % (mt, f2) for f2 in range(8)], writes=["gmax:%d" % mt])
        P.op("vector", lambda v, mt=mt: v.tensor_single_scalar(
            invsgu[:, mt:mt + 1], sgu[:, mt:mt + 1], 1e-5, ALU.max),
            reads=["gmax:%d" % mt], writes=["gclip:%d" % mt])
        P.op("vector", lambda v, mt=mt: v.reciprocal(
            gmparts[:, mt * 8:mt * 8 + 1], invsgu[:, mt:mt + 1]),
            reads=["gclip:%d" % mt], writes=["grcp:%d" % mt])
        P.op("vector", lambda v, mt=mt: v.tensor_scalar_mul(
            sgu[:, mt:mt + 1], gmparts[:, mt * 8:mt * 8 + 1], 127.0),
            reads=["grcp:%d" % mt], writes=["sgu:%d" % mt])
        P.op("vector", lambda v, mt=mt: v.tensor_scalar_mul(
            invsgu[:, mt:mt + 1], invsgu[:, mt:mt + 1], 1.0 / 127.0),
            reads=["gclip:%d" % mt], writes=["invsgu:%d" % mt])
        P.op("vector", lambda v, mt=mt: v.tensor_mul(
            combod[:, mt:mt + 1], invsgu[:, mt:mt + 1], swbc[:, 2:3]),
            reads=["invsgu:%d" % mt, "swbc"], writes=["cd:%d" % mt])

    fstk.close()
    P.barrier()

    f2stk = ExitStack()
    psP = [[psB, psC]] + [
        [f2stk.enter_context(nc.psum_tensor("psP%d_%d" % (pp, j), [128, 512], F32))
         for j in range(2)] for pp in (1, 2)]
    bob = salloc(f2stk, "bob", [128, DM], F32)
    tqg = [salloc(f2stk, "tqg%d" % i, [128, 1024], F32) for i in range(2)]
    tmg = tqg
    guqT = [[salloc(f2stk, "guqT%d_%d" % (gb, i), [128, INNER], BF16)
             for i in range(2)] for gb in range(2)]
    wdqt = [salloc(f2stk, "wdqt%d" % i, [128, 8 * 512], BF16) for i in range(2)]
    dnsb = [salloc(f2stk, "dnsb%d" % i, [128, DM], F32) for i in range(2)]
    smt = [salloc(f2stk, "smt%d" % i, [128, DM], F32) for i in range(2)]
    outsb = dnsb
    P.op("sync", lambda e: e.dma_start(bob[:], b_out_bc_d.ap()), writes=["bob"], dma=True)

    def f2_a(g):
        gb = g % 2
        mts = (2 * g, 2 * g + 1)
        for j, mt in enumerate(mts):
            for hh in range(4):
                hb = hh % 2
                yield lambda mt=mt, hh=hh, hb=hb: P.op("scalar", lambda s: s.activation(
                    tqg[hb][:], gu_bf[:, mt * INNER + hh * 1024:mt * INNER + (hh + 1) * 1024],
                    AF.Identity, bias=csts[:, 2:3], scale=sgu[:, mt:mt + 1]),
                    reads=["gu:%d:%d" % (mt, f2) for f2 in range(hh * 2, hh * 2 + 2)]
                          + ["sgu:%d" % mt, "cst2"],
                    writes=["tqg:%d" % hb])
                yield lambda hb=hb: P.op("vector", lambda v: v.tensor_single_scalar(
                    tmg[hb][:].bitcast(I32), tqg[hb][:].bitcast(I32), RMASK, ALU.bitwise_and),
                    reads=["tqg:%d" % hb], writes=["tqg:%d" % hb])
                yield lambda mt=mt, hh=hh, hb=hb: P.op("vector", lambda v: v.tensor_single_scalar(
                    gu_bf[:, mt * INNER + hh * 1024:mt * INNER + (hh + 1) * 1024],
                    tmg[hb][:], -384.0, ALU.add),
                    reads=["tqg:%d" % hb] + ["gu:%d:%d" % (mt, f2) for f2 in range(hh * 2, hh * 2 + 2)],
                    writes=["guqq:%d:%d" % (mt, hh)])
            for gb8 in range(4):
                for kb in range(8):
                    kk = gb8 * 8 + kb
                    yield lambda mt=mt, kk=kk, kb=kb: P.op("tensor", lambda t: t.matmul(
                        psT[:, kb * 128:(kb + 1) * 128],
                        gu_bf[:, mt * INNER + kk * 128:mt * INNER + (kk + 1) * 128],
                        eye[:], is_transpose=True, start=(kb == 0), stop=(kb == 7)),
                        reads=["guqq:%d:%d" % (mt, kk // 8), "eye"],
                        writes=(["psT"] if kb == 0 else ()),
                        wr_override=(["psT"] if kb == 7 else ()))
                yield lambda j=j, gb8=gb8, gb=gb: P.op("scalar", lambda s: s.activation(
                    guqT[gb][j][:, gb8 * 1024:(gb8 + 1) * 1024], psT[:, 0:1024], AF.Copy,
                    bias=0.0, scale=1.0),
                    reads=["psT"], writes=["guqT:%d:%d:%d" % (gb, j, gb8)])

    def f2_mm(g):
        gb = g % 2
        for f2 in range(2):
            pp = (2 * g + f2) % 3
            for kc in range(4):
                wb3 = kc % 2
                yield lambda kc=kc, f2=f2, wb3=wb3: P.op("sync", lambda e: e.dma_start(
                    wdqt[wb3][:].rearrange("part (k c) -> part k c", k=8),
                    s_wdq.ap()[kc * 1024:(kc + 1) * 1024,
                               f2 * 512:(f2 + 1) * 512].rearrange(
                        "(k part) c -> part k c", k=8)),
                    reads=["swdq:%d:%d" % (kc * 8 + k3, f2) for k3 in range(8)],
                    writes=["wdqt:%d" % wb3], dma=True)
                for k3 in range(8):
                    kb = kc * 8 + k3
                    for j in range(2):
                        yield lambda j=j, kb=kb, k3=k3, wb3=wb3, pp=pp, gb=gb: P.op(
                            "tensor", lambda t: t.matmul(
                                psP[pp][j][:, 0:512], guqT[gb][j][:, kb * 128:(kb + 1) * 128],
                                wdqt[wb3][:, k3 * 512:(k3 + 1) * 512],
                                start=(kb == 0), stop=(kb == 31)),
                            reads=["guqT:%d:%d:%d" % (gb, j, kb // 8), "wdqt:%d" % wb3],
                            writes=(["psP:%d:%d" % (pp, j)] if kb == 0 else ()),
                            wr_override=(["psP:%d:%d" % (pp, j)] if kb == 31 else ()))

    def f2_out(g):
        mts = (2 * g, 2 * g + 1)
        for f2 in range(2):
            pp = (2 * g + f2) % 3
            for j, mt in enumerate(mts):
                yield lambda j=j, mt=mt, f2=f2, pp=pp: P.op(
                    "scalar", lambda s: s.activation(
                        dnsb[j][:, f2 * 512:(f2 + 1) * 512], psP[pp][j][:, 0:512], AF.Copy,
                        bias=0.0, scale=combod[:, mt:mt + 1]),
                    reads=["psP:%d:%d" % (pp, j), "cd:%d" % mt], writes=["dn:%d:%d" % (j, f2)])
        for j, mt in enumerate(mts):
            yield lambda j=j, mt=mt: P.op("sync", lambda e: e.dma_start(
                smt[j][:], s_mm.ap()[mt * 128:(mt + 1) * 128, :]),
                reads=["smm:%d" % mt], writes=["smt:%d" % j], dma=True)
            yield lambda j=j: P.op("vector", lambda v: v.tensor_add(
                outsb[j][:], dnsb[j][:], smt[j][:]),
                reads=["dn:%d:0" % j, "dn:%d:1" % j, "smt:%d" % j],
                writes=["dn:%d:0" % j, "dn:%d:1" % j])
            yield lambda j=j: P.op("vector", lambda v: v.tensor_add(
                outsb[j][:], outsb[j][:], bob[:]),
                reads=["dn:%d:0" % j, "dn:%d:1" % j, "bob"],
                writes=["dn:%d:0" % j, "dn:%d:1" % j])
            yield lambda j=j, mt=mt: P.op("gpsimd", lambda e: e.dma_start(
                out_d.ap()[mt * 128:(mt + 1) * 128, :], outsb[j][:]),
                reads=["dn:%d:0" % j, "dn:%d:1" % j], writes=["outd:%d" % mt], dma=True)

    for fn in f2_a(0):
        fn()
    for g in range(NT // 2):
        sec = _chain(iter(f2_out(g - 1)) if g > 0 else iter(()),
                     iter(f2_a(g + 1)) if g + 1 < NT // 2 else iter(()))
        _zipf(iter(f2_mm(g)), sec, 3, 2)
    for fn in f2_out(NT // 2 - 1):
        fn()

    f2stk.close()
    fboth.close()

    P.op("sync", None, reads=["outd:%d" % mt for mt in range(NT)])
    P.op("gpsimd", None, reads=["outd:%d" % mt for mt in range(NT)])

    P.emit()
    pstack.close()
    nc.finalize()
    return nc


_NC_CACHE = {}


def _get_nc(TOK):
    if TOK not in _NC_CACHE:
        _NC_CACHE[TOK] = build_program(TOK)
    return _NC_CACHE[TOK]


def _host_inputs(inputs, TOK=2048):
    f32 = np.float32
    x = np.asarray(inputs["L0_attn_x"], dtype=f32)
    B, L, _ = x.shape
    flat = np.ascontiguousarray(x.reshape(B * L, DM))
    W_in = np.asarray(inputs["L0_attn_W_in"], dtype=f32)
    b_in = np.asarray(inputs["L0_attn_b_in"], dtype=f32)
    conv_w = np.asarray(inputs["L0_attn_conv_w"], dtype=f32)
    conv_b = np.asarray(inputs["L0_attn_conv_b"], dtype=f32)
    W_x = np.asarray(inputs["L0_attn_W_x"], dtype=f32)
    W_dt = np.asarray(inputs["L0_attn_W_dt"], dtype=f32)
    b_dt = np.asarray(inputs["L0_attn_b_dt"], dtype=f32)
    A_log = np.asarray(inputs["L0_attn_A_log"], dtype=f32)
    W_out = np.asarray(inputs["L0_attn_W_out"], dtype=f32)
    b_out = np.asarray(inputs["L0_attn_b_out"], dtype=f32)
    w_gate = np.asarray(inputs["L0_ffn_w_gate"], dtype=f32)
    w_up = np.asarray(inputs["L0_ffn_w_up"], dtype=f32)
    w_down = np.asarray(inputs["L0_ffn_w_down"], dtype=f32)

    shared = {
        "w_inT": np.ascontiguousarray(W_in.T),
        "w_xT": np.ascontiguousarray(W_x.T),
        "w_dtT": np.ascontiguousarray(W_dt.T),
        "w_outT": np.ascontiguousarray(W_out.T),
        "wgT": np.ascontiguousarray(w_gate.T),
        "wuT": np.ascontiguousarray(w_up.T),
        "wdT": np.ascontiguousarray(w_down.T),
        "b_in_t": np.ascontiguousarray(b_in.reshape(32, 128).T),
        "conv_w_t": np.ascontiguousarray(
            conv_w[:, 0, :].reshape(16, 128, 3).transpose(1, 2, 0).reshape(128, 48)),
        "conv_b_t": np.ascontiguousarray(conv_b.reshape(16, 128).T),
        "b_dt_t": np.ascontiguousarray(b_dt.reshape(16, 128).T),
        "alog_t": np.ascontiguousarray(
            A_log.reshape(16, 128, 16).transpose(1, 0, 2).reshape(128, 256)),
        "b_out_bc": np.ascontiguousarray(np.broadcast_to(b_out.reshape(1, DM), (128, DM))),
        "eye_in": np.eye(128, dtype=BFNP),
    }
    ncores = (B * L) // TOK
    in_maps = []
    for k in range(ncores):
        start = k * TOK
        seq_start = (start % L) == 0
        halo = np.zeros((2, DM), f32) if seq_start else flat[start - 2:start]
        xh_k = np.ascontiguousarray(np.concatenate([halo, flat[start:start + TOK]], axis=0))
        hm = np.full((128, 2), 0.0 if seq_start else 1.0, dtype=f32)
        m = dict(shared)
        m["xh"] = xh_k
        m["hmask_in"] = hm
        in_maps.append(m)
    return in_maps, (B, L)


def kernel(**inputs):
    TOK = 2048
    in_maps, (B, L) = _host_inputs(inputs, TOK)
    nc = _get_nc(TOK)
    res = run_bass_kernel_spmd(nc, in_maps, core_ids=list(range(len(in_maps))))
    outs = [np.asarray(r["out"]) for r in res.results]
    full = np.concatenate(outs, axis=0).reshape(B, L, DM)
    return full.astype(np.float32)


# version 32
# speedup vs baseline: 1.0501x; 1.0126x over previous
import numpy as np
import ml_dtypes
from contextlib import ExitStack

import concourse.bass as bass
import concourse.mybir as mybir
from concourse.bass_utils import run_bass_kernel_spmd

F32 = mybir.dt.float32
BF16 = mybir.dt.bfloat16
I32 = mybir.dt.int32
AF = mybir.ActivationFunctionType
ALU = mybir.AluOpType
AXX = mybir.AxisListType.X
BFNP = ml_dtypes.bfloat16

DM = 1024
DI = 2048
NS = 16
DTR = 64
INNER = 4096
CH = 256
NCORES = 8
NW_ELEMS = float(INNER * DM)
RMASK = 0xFFFF8000 - (1 << 32)
RBIAS = 384.5


class _Op:
    __slots__ = ("q", "dma", "fn", "waits", "inc", "cum", "skey")

    def __init__(self):
        self.waits = set()
        self.inc = False


class Prog:
    QUEUES = ("sync", "gpsimd", "tensor", "scalar", "vector")
    DMA_SLOTS = 8

    def __init__(self, nc):
        self.nc = nc
        self.qops = {q: [] for q in self.QUEUES}
        self.streams = {}
        self.slotlast = {}
        self.dmacnt = {q: 0 for q in self.QUEUES}
        self.tags = {}
        self.pend = {}

    def op(self, q, fn, reads=(), writes=(), dma=False, wr_override=()):
        o = _Op()
        o.q, o.dma, o.fn = q, dma, fn
        pend = self.pend.pop(q, None)
        deps = set(pend) if pend else set()
        for t in reads:
            e = self.tags.get(t)
            if e is not None and e[0] is not None:
                deps.add(e[0])
        for t in writes:
            e = self.tags.get(t)
            if e is not None:
                if e[0] is not None:
                    deps.add(e[0])
                deps.update(e[1])
        if dma:
            slot = self.dmacnt[q] % self.DMA_SLOTS
            self.dmacnt[q] += 1
            o.skey = (q, "d%d" % slot)
            prev = self.slotlast.get(o.skey)
            if prev is not None:
                deps.add(prev)
            self.slotlast[o.skey] = o
            o.inc = True
        else:
            o.skey = (q, "c")
        for d in deps:
            if d is o:
                continue
            d.inc = True
            o.waits.add(d)
        for t in reads:
            self.tags.setdefault(t, [None, []])[1].append(o)
        for t in writes:
            self.tags[t] = [o, []]
        for t in wr_override:
            self.tags[t] = [o, []]
        c = self.streams.get(o.skey, 0) + 1
        self.streams[o.skey] = c
        o.cum = c * (16 if dma else 1)
        self.qops[q].append(o)
        return o

    def barrier(self):
        last = set()
        for q in self.QUEUES:
            for o in reversed(self.qops[q]):
                if not o.dma:
                    last.add(o)
                    o.inc = True
                    break
        for o in self.slotlast.values():
            last.add(o)
            o.inc = True
        for q in self.QUEUES:
            self.pend[q] = self.pend.get(q, set()) | set(last)

    def emit(self):
        nc = self.nc
        es = ExitStack()
        sems = {}
        for key in self.streams:
            sems[key] = es.enter_context(nc.semaphore("sem_%s_%s" % key))
        with nc.Block() as block:
            for q in self.QUEUES:
                lst = self.qops[q]
                if not lst:
                    continue

                def body(eng, lst=lst, q=q):
                    sat = {}
                    last_inc = {}
                    for o in lst:
                        wl = {}
                        for d in o.waits:
                            if d.cum > wl.get(d.skey, 0):
                                wl[d.skey] = d.cum
                        for k in sorted(wl, key=lambda t: (t[0], t[1])):
                            v = wl[k]
                            if v > sat.get(k, 0):
                                eng.wait_ge(sems[k], v)
                                sat[k] = v
                        inst = o.fn(eng) if o.fn is not None else None
                        if o.inc:
                            k = o.skey
                            amt = o.cum - last_inc.get(k, 0)
                            inst.then_inc(sems[k], amt)
                            last_inc[k] = o.cum

                getattr(block, q)(body)
        es.close()


def build_program(TOK):
    NT = TOK // 128
    NSC = TOK // CH
    nc = bass.Bass("TRN2", target_bir_lowering=False)

    xh = nc.dram_tensor("xh", [TOK + 2, DM], F32, kind="ExternalInput")
    w_inT = nc.dram_tensor("w_inT", [DM, 2 * DI], F32, kind="ExternalInput")
    w_xT = nc.dram_tensor("w_xT", [DI, 96], F32, kind="ExternalInput")
    w_dtT = nc.dram_tensor("w_dtT", [DTR, DI], F32, kind="ExternalInput")
    w_outT = nc.dram_tensor("w_outT", [DI, DM], F32, kind="ExternalInput")
    wgT = nc.dram_tensor("wgT", [DM, INNER], F32, kind="ExternalInput")
    wuT = nc.dram_tensor("wuT", [DM, INNER], F32, kind="ExternalInput")
    wdT = nc.dram_tensor("wdT", [INNER, DM], F32, kind="ExternalInput")
    b_in_t = nc.dram_tensor("b_in_t", [128, 32], F32, kind="ExternalInput")
    conv_w_t = nc.dram_tensor("conv_w_t", [128, 48], F32, kind="ExternalInput")
    conv_b_t = nc.dram_tensor("conv_b_t", [128, 16], F32, kind="ExternalInput")
    b_dt_t = nc.dram_tensor("b_dt_t", [128, 16], F32, kind="ExternalInput")
    alog_t = nc.dram_tensor("alog_t", [128, 256], F32, kind="ExternalInput")
    b_out_bc_d = nc.dram_tensor("b_out_bc", [128, DM], F32, kind="ExternalInput")
    eye_in = nc.dram_tensor("eye_in", [128, 128], BF16, kind="ExternalInput")
    hmask_in = nc.dram_tensor("hmask_in", [128, 2], F32, kind="ExternalInput")
    out_d = nc.dram_tensor("out", [TOK, DM], F32, kind="ExternalOutput")

    s_winT = nc.dram_tensor("s_winT", [DM, 2 * DI], BF16)
    s_woutT = nc.dram_tensor("s_woutT", [DI, DM], BF16)
    s_xnfm = nc.dram_tensor("s_xnfm", [DM, TOK], BF16)
    s_xqfm = nc.dram_tensor("s_xqfm", [DM, TOK], BF16)
    s_wdq = nc.dram_tensor("s_wdq", [INNER, DM], BF16)
    s_mm = nc.dram_tensor("s_mm", [TOK, DM], F32)

    P = Prog(nc)
    half = 0.5

    pstack = ExitStack()

    def salloc(stack, name, shape, dt):
        return stack.enter_context(nc.sbuf_tensor(name, shape, dt))

    hmask = salloc(pstack, "hmask", [128, 2], F32)
    eye = salloc(pstack, "eye", [128, 128], BF16)
    invsx = salloc(pstack, "invsx", [128, NT], F32)
    swrow = salloc(pstack, "swrow", [1, 16], F32)
    swbc = salloc(pstack, "swbc", [128, 9], F32)
    ones2 = salloc(pstack, "ones2", [128, 128], F32)
    combog = salloc(pstack, "combog", [128, NT], F32)
    combou = salloc(pstack, "combou", [128, NT], F32)
    combod = salloc(pstack, "combod", [128, NT], F32)
    gmparts = salloc(pstack, "gmparts", [128, NT * 8], F32)
    sgu = salloc(pstack, "sgu", [128, NT], F32)
    invsgu = salloc(pstack, "invsgu", [128, NT], F32)
    ones1 = salloc(pstack, "ones1", [128, 1], F32)
    csts = salloc(pstack, "csts", [128, 3], F32)
    acc = salloc(pstack, "acc", [128, 96], F32)

    psA = pstack.enter_context(nc.psum_tensor("psA", [128, 512], F32))
    psB = pstack.enter_context(nc.psum_tensor("psB", [128, 512], F32))

    pmstk = ExitStack()
    psI = [pmstk.enter_context(nc.psum_tensor("psI%d" % i, [128, 256], F32)) for i in range(2)]
    bint = salloc(pmstk, "bint", [128, 32], F32)
    cwt = salloc(pmstk, "cwt", [128, 48], F32)
    cbt = salloc(pmstk, "cbt", [128, 16], F32)
    bdtt = salloc(pmstk, "bdtt", [128, 16], F32)
    alneg = salloc(pmstk, "alneg", [128, 256], F32)
    wxT_bf = salloc(pmstk, "wxT_bf", [128, 16 * 96], BF16)
    wdtT_bf = salloc(pmstk, "wdtT_bf", [64, DI], BF16)
    xnh_fm = salloc(pmstk, "xnh_fm", [128, 16], BF16)

    p0 = ExitStack()
    psT0 = [p0.enter_context(nc.psum_tensor("psT%d" % i, [128, 1024], BF16))
            for i in range(2)]
    ROT = 3
    sxs = salloc(p0, "sxs", [128, NT], F32)
    xt = [salloc(p0, "xt%d" % i, [128, DM], F32) for i in range(5)]
    dump = [salloc(p0, "dump%d" % i, [128, DM], F32) for i in range(2)]
    stats = [salloc(p0, "stats%d" % i, [128, 16], F32) for i in range(ROT)]
    xn = [salloc(p0, "xn%d" % i, [128, DM], F32) for i in range(ROT)]
    xnb = [salloc(p0, "xnb%d" % i, [128, DM], BF16) for i in range(ROT)]
    tq0 = [salloc(p0, "tq0%d" % i, [128, DM], F32) for i in range(ROT)]
    tm0 = [salloc(p0, "tm0%d" % i, [128, DM], F32) for i in range(ROT)]
    xqb = [salloc(p0, "xqb%d" % i, [128, DM], BF16) for i in range(ROT)]
    trev = [salloc(p0, "trev%d" % i, [128, DM], BF16) for i in range(ROT)]
    wcf = [salloc(p0, "wcf%d" % i, [128, 4096], F32) for i in range(2)]
    wcb = [salloc(p0, "wcb%d" % i, [128, 4096], BF16) for i in range(2)]
    xhalo = salloc(p0, "xhalo", [2, DM], F32)
    hstat = salloc(p0, "hstat", [2, 16], F32)
    xnhb = salloc(p0, "xnhb", [2, DM], BF16)
    asum = salloc(p0, "asum", [128, 3], F32)

    P.op("sync", lambda e: e.dma_start(bint[:], b_in_t.ap()), writes=["bint"], dma=True)
    P.op("sync", lambda e: e.dma_start(cwt[:], conv_w_t.ap()), writes=["cwt"], dma=True)
    P.op("sync", lambda e: e.dma_start(cbt[:], conv_b_t.ap()), writes=["cbt"], dma=True)
    P.op("sync", lambda e: e.dma_start(bdtt[:], b_dt_t.ap()), writes=["bdtt"], dma=True)
    P.op("sync", lambda e: e.dma_start(alneg[:], alog_t.ap()), writes=["alraw"], dma=True)
    P.op("sync", lambda e: e.dma_start(hmask[:], hmask_in.ap()), writes=["hmask"], dma=True)
    P.op("sync", lambda e: e.dma_start(eye[:], eye_in.ap()), writes=["eye"], dma=True)
    P.op("vector", lambda v: v.memset(ones1[:], 1.0), writes=["ones1"])
    P.op("vector", lambda v: v.memset(csts[:, 0:1], 1e-5), writes=["cst0"])
    P.op("vector", lambda v: v.memset(csts[:, 1:2], 1.0), writes=["cst1"])
    P.op("vector", lambda v: v.memset(csts[:, 2:3], RBIAS), writes=["cst2"])
    P.op("vector", lambda v: v.memset(ones2[:], 1.0), writes=["ones2"])

    P.op("scalar", lambda s: s.activation(alneg[:], alneg[:], AF.Exp, bias=0.0, scale=1.0),
         reads=["alraw"], writes=["alneg_t"])
    P.op("vector", lambda v: v.tensor_scalar_mul(alneg[:], alneg[:], -1.0),
         reads=["alneg_t"], writes=["alneg"])

    P.op("sync", lambda e: e.dma_start(
        wcf[0][:, 0:16 * 96].rearrange("part (k c) -> part k c", k=16),
        w_xT.ap().rearrange("(k part) c -> part k c", k=16)),
        writes=["wcf:0"], dma=True)
    P.op("vector", lambda v: v.tensor_copy(wxT_bf[:], wcf[0][:, 0:16 * 96]),
         reads=["wcf:0"], writes=["wxT_bf"])
    P.op("sync", lambda e: e.dma_start(wcf[1][0:64, 0:DI], w_dtT.ap()),
         writes=["wcf:1"], dma=True)
    P.op("vector", lambda v: v.tensor_copy(wdtT_bf[:], wcf[1][0:64, 0:DI]),
         reads=["wcf:1"], writes=["wdtT_bf"])

    def wcast_gen():
        wcnt = 0
        for (src, dst, rows, cols, tagbase) in ((w_inT, s_winT, DM, 2 * DI, "swin"),
                                                (w_outT, s_woutT, DI, DM, "swout")):
            ntile = rows // 128
            ncol = cols // 4096 if cols >= 4096 else 1
            cw = 4096 if cols >= 4096 else cols
            for r in range(ntile):
                for c in range(ncol):
                    b = wcnt % 2
                    lq = "sync" if wcnt % 2 == 0 else "scalar"
                    wcnt += 1
                    yield lambda src=src, r=r, c=c, b=b, cw=cw, lq=lq: P.op(
                        lq, lambda e: e.dma_start(
                            wcf[b][:, 0:cw],
                            src.ap()[r * 128:(r + 1) * 128, c * cw:(c + 1) * cw]),
                        writes=["wcf:%d" % b], dma=True)
                    yield lambda b=b, cw=cw: P.op("vector", lambda v: v.tensor_copy(
                        wcb[b][:, 0:cw], wcf[b][:, 0:cw]),
                        reads=["wcf:%d" % b], writes=["wcb:%d" % b])
                    yield lambda dst=dst, r=r, c=c, b=b, cw=cw, tagbase=tagbase: P.op(
                        "gpsimd", lambda e: e.dma_start(
                            dst.ap()[r * 128:(r + 1) * 128, c * cw:(c + 1) * cw],
                            wcb[b][:, 0:cw]),
                        reads=["wcb:%d" % b],
                        writes=["%s:%d:%d" % (tagbase, r, c)], dma=True)

    def _tile_ops(i):
        if True:
            ishalo = (i == NT)
            part = 2 if ishalo else 128
            b = i % ROT
            bx = i % 5
            db = i % 2
            xtt = xhalo if ishalo else xt[bx]
            stt = hstat if ishalo else stats[b]
            xnt = xn[b][0:part, :] if not ishalo else xn[0][0:2, :]
            xnbt = xnhb if ishalo else xnb[b]
            rt = "h" if ishalo else str(bx)
            strt = "hs" if ishalo else "s%d" % b
            txn = ("xn:%d" % b) if not ishalo else "xn:0"
            txnb = ("xnb:%d" % b) if not ishalo else "xnb:h"

            if ishalo:
                yield lambda: P.op("sync", lambda e: e.dma_start(
                    xhalo[:], xh.ap()[0:2, :]),
                    writes=["xt:h"], dma=True)
            else:
                yield lambda i=i, bx=bx, rt=rt: P.op("sync", lambda e: e.dma_start(
                    xt[bx][:], xh.ap()[2 + i * 128:2 + (i + 1) * 128, :]),
                    writes=["xt:" + rt], dma=True)
            xv = xtt[0:part, :]
            sv_ = stt[0:part, :]
            yield lambda xv=xv, sv_=sv_, rt=rt, strt=strt: P.op(
                "vector", lambda v: v.reduce_sum(sv_[:, 0:1], xv, axis=AXX),
                reads=["xt:" + rt], writes=[strt + ":sum"])
            yield lambda xv=xv, sv_=sv_, db=db, part=part, rt=rt, strt=strt: P.op(
                "scalar", lambda s: s.activation(
                    dump[db][0:part, :], xv, AF.Square, bias=0.0, scale=1.0,
                    accum_out=sv_[:, 1:2]),
                reads=["xt:" + rt], writes=["dump:%d" % db, strt + ":sq"])
            yield lambda sv_=sv_, strt=strt: P.op("vector", lambda v: v.tensor_scalar_mul(
                sv_[:, 2:3], sv_[:, 0:1], 1.0 / DM),
                reads=[strt + ":sum"], writes=[strt + ":mean"])
            yield lambda sv_=sv_, strt=strt: P.op("vector", lambda v: v.tensor_scalar_mul(
                sv_[:, 3:4], sv_[:, 1:2], 1.0 / DM),
                reads=[strt + ":sq"], writes=[strt + ":ex2"])
            yield lambda sv_=sv_, strt=strt: P.op("vector", lambda v: v.tensor_mul(
                sv_[:, 4:5], sv_[:, 2:3], sv_[:, 2:3]),
                reads=[strt + ":mean"], writes=[strt + ":msq"])
            yield lambda sv_=sv_, strt=strt: P.op("vector", lambda v: v.tensor_sub(
                sv_[:, 5:6], sv_[:, 3:4], sv_[:, 4:5]),
                reads=[strt + ":ex2", strt + ":msq"], writes=[strt + ":var"])
            yield lambda sv_=sv_, part=part, strt=strt: P.op("scalar", lambda s: s.activation(
                sv_[:, 11:12], sv_[:, 5:6], AF.Sqrt, bias=csts[0:part, 0:1], scale=1.0),
                reads=[strt + ":var", "cst0"], writes=[strt + ":std"])
            yield lambda sv_=sv_, strt=strt: P.op("vector", lambda v: v.reciprocal(
                sv_[:, 6:7], sv_[:, 11:12]),
                reads=[strt + ":std"], writes=[strt + ":rstd"])
            yield lambda sv_=sv_, strt=strt: P.op("vector", lambda v: v.tensor_scalar(
                sv_[:, 7:8], sv_[:, 2:3], sv_[:, 6:7], -1.0, ALU.mult, ALU.mult),
                reads=[strt + ":mean", strt + ":rstd"], writes=[strt + ":mrs"])
            yield lambda xv=xv, sv_=sv_, xnt=xnt, rt=rt, strt=strt, txn=txn: P.op(
                "scalar", lambda s: s.activation(
                    xnt, xv, AF.Identity, bias=sv_[:, 7:8], scale=sv_[:, 6:7]),
                reads=["xt:" + rt, strt + ":rstd", strt + ":mrs"], writes=[txn])
            yield lambda xnt=xnt, xnbt=xnbt, part=part, txn=txn, txnb=txnb: P.op(
                "vector", lambda v: v.tensor_copy(xnbt[0:part, :], xnt),
                reads=[txn], writes=[txnb])
            if not ishalo:
                yield lambda xnt=xnt, sv_=sv_, txn=txn, strt=strt: P.op(
                    "vector", lambda v: v.reduce_max(
                        sv_[:, 9:10], xnt, axis=AXX, apply_absolute_value=True),
                    reads=[txn], writes=[strt + ":rmax"])
                yield lambda sv_=sv_, strt=strt: P.op("vector", lambda v: v.tensor_single_scalar(
                    sv_[:, 8:9], sv_[:, 9:10], 1e-5, ALU.max),
                    reads=[strt + ":rmax"], writes=[strt + ":clip"])
                yield lambda sv_=sv_, i=i, strt=strt: P.op("vector", lambda v: v.tensor_scalar_mul(
                    invsx[:, i:i + 1], sv_[:, 8:9], 1.0 / 127.0),
                    reads=[strt + ":clip"], writes=["invsx:%d" % i])
                yield lambda sv_=sv_, strt=strt: P.op("vector", lambda v: v.reciprocal(
                    sv_[:, 10:11], sv_[:, 8:9]),
                    reads=[strt + ":clip"], writes=[strt + ":rcp"])
                yield lambda sv_=sv_, i=i, strt=strt: P.op("vector", lambda v: v.tensor_scalar_mul(
                    sxs[:, i:i + 1], sv_[:, 10:11], 127.0),
                    reads=[strt + ":rcp"], writes=["sx:%d" % i])
                yield lambda xnt=xnt, b=b, i=i, txn=txn: P.op("scalar", lambda s: s.activation(
                    tq0[b][:], xnt, AF.Identity, bias=csts[:, 2:3], scale=sxs[:, i:i + 1]),
                    reads=[txn, "sx:%d" % i, "cst2"], writes=["tq0:%d" % b])
                yield lambda b=b: P.op("vector", lambda v: v.tensor_single_scalar(
                    tm0[b][:].bitcast(I32), tq0[b][:].bitcast(I32), RMASK, ALU.bitwise_and),
                    reads=["tq0:%d" % b], writes=["tm0:%d" % b])
                yield lambda b=b: P.op("vector", lambda v: v.tensor_single_scalar(
                    xqb[b][:], tm0[b][:], -384.0, ALU.add),
                    reads=["tm0:%d" % b], writes=["xqb:%d" % b])
                for wi, (srcb, srct, dstd, tg) in enumerate(
                        ((xnb[b], txnb, s_xnfm, "sxn"),
                         (xqb[b], "xqb:%d" % b, s_xqfm, "sxq"))):
                    tp = (i * 2 + wi) % 2
                    for kb in range(8):
                        yield lambda srcb=srcb, kb=kb, srct=srct, tp=tp: P.op(
                            "tensor", lambda t: t.matmul(
                                psT0[tp][:, kb * 128:(kb + 1) * 128],
                                srcb[:, kb * 128:(kb + 1) * 128],
                                eye[:], is_transpose=True, start=(kb == 0), stop=(kb == 7)),
                            reads=[srct, "eye"],
                            writes=(["psT:%d" % tp] if kb == 0 else ()),
                            wr_override=(["psT:%d" % tp] if kb == 7 else ()))
                    yield lambda b=b, tp=tp: P.op("scalar", lambda s: s.activation(
                        trev[b][:], psT0[tp][:, 0:1024], AF.Copy, bias=0.0, scale=1.0),
                        reads=["psT:%d" % tp], writes=["trev:%d" % b])
                    yield lambda b=b, dstd=dstd, i=i, tg=tg: P.op(
                        "gpsimd", lambda e: e.dma_start(
                            dstd.ap()[:, i * 128:(i + 1) * 128].rearrange(
                                "(k part) c -> part k c", k=8),
                            trev[b][:].rearrange("part (k c) -> part k c", k=8)),
                        reads=["trev:%d" % b],
                        writes=["%s:%d:%d" % (tg, kb, i) for kb in range(8)], dma=True)
            else:
                for kb in range(8):
                    yield lambda kb=kb: P.op("tensor", lambda t: t.matmul(
                        psT0[0][:, kb * 2:(kb + 1) * 2], xnhb[0:2, kb * 128:(kb + 1) * 128],
                        eye[0:2, 0:2], is_transpose=True, start=(kb == 0), stop=(kb == 7)),
                        reads=["xnb:h", "eye"],
                        writes=(["psT:0"] if kb == 0 else ()),
                        wr_override=(["psT:0"] if kb == 7 else ()))
                yield lambda: P.op("scalar", lambda s: s.activation(
                    xnh_fm[:], psT0[0][:, 0:16], AF.Copy, bias=0.0, scale=1.0),
                    reads=["psT:0"], writes=["xnh_fm"])

    def ln_skewed():
        tiles = []
        for i in range(NT):
            it = _tile_ops(i)
            tiles.append([fn for fn in it])
        cuts = (0, 1, 3, 10, 20, None)
        offs = (0, 2, 3, 4, 5)
        for j in range(NT + 5):
            for st in range(5):
                i = j - offs[st]
                if 0 <= i < NT:
                    a, b = cuts[st], cuts[st + 1]
                    for fn in tiles[i][a:b]:
                        yield fn
        for fn in _tile_ops(NT):
            yield fn

    _wc = iter(wcast_gen())
    _ln = iter(ln_skewed())
    while True:
        got = False
        for _ in range(5):
            fn = next(_ln, None)
            if fn is not None:
                got = True
                fn()
        fn = next(_wc, None)
        if fn is not None:
            got = True
            fn()
        if not got:
            break

    p0.close()
    P.barrier()

    mstk = ExitStack()
    psBC = [mstk.enter_context(nc.psum_tensor("psBC%d" % i, [128, 512], F32))
            for i in range(2)]
    psO2 = mstk.enter_context(nc.psum_tensor("psO2", [128, 512], F32))
    psOut = mstk.enter_context(nc.psum_tensor("psOut", [128, 512], F32))
    xnsc = [salloc(mstk, "xnsc%d" % i, [128, 8 * CH], BF16) for i in range(2)]
    wch = [salloc(mstk, "wch%d" % i, [128, 2048], BF16) for i in range(3)]
    CH2 = CH + 2
    xc_t = salloc(mstk, "xc_t", [128, 16 * CH2], F32)
    sres = [salloc(mstk, "sres%d" % i, [128, 16 * CH], BF16) for i in range(2)]
    delta_t = [salloc(mstk, "delta_t%d" % i, [128, 16 * CH], BF16) for i in range(2)]
    du_t = [salloc(mstk, "du_t%d" % i, [128, 16 * CH], BF16) for i in range(2)]
    dltbf = [salloc(mstk, "dltbf%d" % i, [64, CH], BF16) for i in range(2)]
    BCt = [salloc(mstk, "BCt%d" % i, [128, CH], F32) for i in range(2)]
    ubf = salloc(mstk, "ubf", [128, 16 * CH], BF16)
    yA = salloc(mstk, "yA", [128, 16 * CH], F32)
    gated = [salloc(mstk, "gated%d" % i, [128, 16 * CH], BF16) for i in range(2)]
    shstB = salloc(mstk, "shstB", [128, CH], F32)
    shstC = salloc(mstk, "shstC", [128, CH], F32)
    NDA = 20
    dAb = [salloc(mstk, "dAb%d" % i, [128, CH], BF16) for i in range(NDA)]
    dbub = [salloc(mstk, "dbub%d" % i, [128, CH], BF16) for i in range(3)]
    hbuf = [salloc(mstk, "hbuf%d" % i, [128, CH], BF16) for i in range(3)]
    ypb = [salloc(mstk, "ypb%d" % i, [128, CH], BF16) for i in range(2)]
    pa = [salloc(mstk, "pa%d" % i, [128, CH], BF16) for i in range(16)]
    Bsb = [salloc(mstk, "Bsb%d" % i, [128, CH], BF16) for i in range(3)]
    Csb = [salloc(mstk, "Csb%d" % i, [128, CH], BF16) for i in range(3)]
    cva = [salloc(mstk, "cva%d" % i, [128, CH], F32) for i in range(2)]
    cvb = [salloc(mstk, "cvb%d" % i, [128, CH], F32) for i in range(2)]
    cvc = cva
    xpb = [salloc(mstk, "xpb%d" % i, [128, CH], F32) for i in range(2)]
    axb = [salloc(mstk, "axb0", [128, CH], F32)] * 2
    eab = [salloc(mstk, "eab0", [128, CH], F32)] * 2
    lab = [salloc(mstk, "lab%d" % i, [128, CH], F32) for i in range(2)]
    hv = salloc(mstk, "hv", [128, 2], F32)
    hv2 = salloc(mstk, "hv2", [128, 2], F32)
    wo_buf = [salloc(mstk, "wo_buf%d" % i, [128, 4 * 512], BF16) for i in range(2)]
    xrel = [salloc(mstk, "xrel%d" % i, [128, DM], F32) for i in range(2)]
    mmo = [salloc(mstk, "mmo%d" % i, [128, DM], F32) for i in range(2)]
    wab = [salloc(mstk, "wab%d" % i, [128, 1024], F32) for i in range(2)]

    def pre_a_mm(sc):
        p = sc % 2
        t0 = sc * CH
        yield lambda: P.op("sync", lambda e: e.dma_start(
            xnsc[p][:].rearrange("part (k c) -> part k c", k=8),
            s_xnfm.ap()[:, t0:t0 + CH].rearrange("(k part) c -> part k c", k=8)),
            reads=["sxn:%d:%d" % (k2, (t0 // 128) + j) for k2 in range(8)
                   for j in range(CH // 128)],
            writes=["xnsc:%d" % p], dma=True)
        for mg in range(16):
            wb = mg % 3
            yield lambda mg=mg, wb=wb: P.op("sync", lambda e: e.dma_start(
                wch[wb][:].rearrange("part (k c) -> part k c", k=8),
                s_winT.ap()[:, mg * 256:(mg + 1) * 256].rearrange(
                    "(k part) c -> part k c", k=8)),
                reads=["swin:%d:0" % k2 for k2 in range(8)],
                writes=["wch:%d" % wb], dma=True)
            for m in range(2):
                mt = mg * 2 + m
                mp = m % 2
                for k2 in range(8):
                    yield lambda wb=wb, k2=k2, m=m, mp=mp, p=p: P.op(
                        "tensor", lambda t: t.matmul(
                            psI[mp][:, 0:CH],
                            wch[wb][:, k2 * 256 + m * 128:k2 * 256 + (m + 1) * 128],
                            xnsc[p][:, k2 * CH:(k2 + 1) * CH],
                            start=(k2 == 0), stop=(k2 == 7)),
                        reads=["wch:%d" % wb, "xnsc:%d" % p],
                        writes=(["psI:%d" % mp] if k2 == 0 else ()),
                        wr_override=(["psI:%d" % mp] if k2 == 7 else ()))
                if sc == 0 and mt < 16:
                    for k2 in range(8):
                        yield lambda wb=wb, k2=k2, m=m: P.op(
                            "tensor", lambda t: t.matmul(
                                psA[:, 0:2],
                                wch[wb][:, k2 * 256 + m * 128:k2 * 256 + (m + 1) * 128],
                                xnh_fm[:, k2 * 2:(k2 + 1) * 2],
                                start=(k2 == 0), stop=(k2 == 7)),
                            reads=["wch:%d" % wb, "xnh_fm"],
                            writes=(["psA"] if k2 == 0 else ()),
                            wr_override=(["psA"] if k2 == 7 else ()))
                    yield lambda mt=mt: P.op("scalar", lambda s: s.activation(
                        hv[:], psA[:, 0:2], AF.Identity, bias=bint[:, mt:mt + 1],
                        scale=1.0),
                        reads=["psA", "bint"], writes=["hv"])
                    yield lambda: P.op("vector", lambda v: v.tensor_mul(
                        hv2[:], hv[:], hmask[:]),
                        reads=["hv", "hmask"], writes=["hv2"])
                    yield lambda mt=mt: P.op("vector", lambda v: v.tensor_copy(
                        xc_t[:, mt * CH2:mt * CH2 + 2], hv2[:]),
                        reads=["hv2"], writes=["xch:%d" % mt])
                if mt < 16:
                    yield lambda mp=mp, mt=mt: P.op("scalar", lambda s: s.activation(
                        xc_t[:, mt * CH2 + 2:mt * CH2 + 2 + CH], psI[mp][:, 0:CH],
                        AF.Identity, bias=bint[:, mt:mt + 1], scale=1.0),
                        reads=["psI:%d" % mp, "bint"], writes=["xcb:%d" % mt])
                else:
                    yield lambda mp=mp, mt=mt, p=p: P.op("scalar", lambda s: s.activation(
                        sres[p][:, (mt - 16) * CH:(mt - 15) * CH], psI[mp][:, 0:CH],
                        AF.Identity, bias=bint[:, mt:mt + 1], scale=1.0),
                        reads=["psI:%d" % mp, "bint"], writes=["sr:%d:%d" % (p, mt - 16)])

    def pre_a_conv(sc):
        p = sc % 2
        for q in range(16):
            cb2 = q % 2
            yield lambda q=q, cb2=cb2: P.op("vector", lambda v: v.tensor_single_scalar(
                cva[cb2][:], xc_t[:, q * CH2 + 0:q * CH2 + CH],
                cwt[:, 0 + q:0 + q + 1], ALU.mult),
                reads=["xch:%d" % q, "xcb:%d" % q, "cwt"],
                writes=["cva:%d" % cb2])
            yield lambda q=q, cb2=cb2: P.op("vector", lambda v: v.scalar_tensor_tensor(
                cvb[cb2][:], xc_t[:, q * CH2 + 1:q * CH2 + 1 + CH],
                cwt[:, 16 + q:16 + q + 1], cva[cb2][:], ALU.mult, ALU.add),
                reads=["xch:%d" % q, "xcb:%d" % q, "cva:%d" % cb2, "cwt"],
                writes=["cvb:%d" % cb2])
            yield lambda q=q, cb2=cb2: P.op("vector", lambda v: v.scalar_tensor_tensor(
                cvc[cb2][:], xc_t[:, q * CH2 + 2:q * CH2 + 2 + CH],
                cwt[:, 32 + q:32 + q + 1], cvb[cb2][:], ALU.mult, ALU.add),
                reads=["xcb:%d" % q, "cvb:%d" % cb2, "cwt"],
                writes=["cva:%d" % cb2])
            yield lambda q=q, cb2=cb2: P.op("scalar", lambda s: s.activation(
                ubf[:, q * CH:(q + 1) * CH], cvc[cb2][:], AF.Silu,
                bias=cbt[:, q:q + 1], scale=1.0),
                reads=["cva:%d" % cb2, "cbt"], writes=["ubf:%d" % q])
            if sc + 1 < NSC:
                yield lambda q=q: P.op("vector", lambda v: v.tensor_copy(
                    xc_t[:, q * CH2:q * CH2 + 2],
                    xc_t[:, q * CH2 + CH:q * CH2 + CH + 2]),
                    reads=["xcb:%d" % q], writes=["xch:%d" % q])
        for q in range(16):
            yield lambda q=q, p=p: P.op("scalar", lambda s: s.activation(
                sres[p][:, q * CH:(q + 1) * CH], sres[p][:, q * CH:(q + 1) * CH],
                AF.Silu, bias=0.0, scale=1.0),
                reads=["sr:%d:%d" % (p, q)], writes=["sr:%d:%d" % (p, q)])

    def pre_b(sc):
        p = sc % 2
        for k in range(16):
            yield lambda k=k: P.op("tensor", lambda t: t.matmul(
                psA[0:96, 0:CH], wxT_bf[:, k * 96:(k + 1) * 96],
                ubf[:, k * CH:(k + 1) * CH], start=(k == 0), stop=(k == 15)),
                reads=["wxT_bf", "ubf:%d" % k],
                writes=(["psA"] if k == 0 else ()),
                wr_override=(["psA"] if k == 15 else ()))
        yield lambda p=p: P.op("scalar", lambda s: s.activation(
            dltbf[p][:, :], psA[0:64, 0:CH], AF.Copy, bias=0.0, scale=1.0),
            reads=["psA"], writes=["dltbf:%d" % p])
        yield lambda p=p: P.op("scalar", lambda s: s.activation(
            BCt[p][64:96, :], psA[64:96, 0:CH], AF.Copy, bias=0.0, scale=1.0),
            reads=["psA"], writes=["BCt:%d" % p])
        for mt in range(16):
            b = mt % 2
            yield lambda mt=mt, p=p: P.op("tensor", lambda t: t.matmul(
                psB[:, 0:CH], wdtT_bf[:, mt * 128:(mt + 1) * 128], dltbf[p][:, :],
                start=True, stop=True),
                reads=["wdtT_bf", "dltbf:%d" % p], writes=["psB"], wr_override=["psB"])
            yield lambda b=b, mt=mt: P.op("scalar", lambda s: s.activation(
                xpb[b][:], psB[:, 0:CH], AF.Identity, bias=bdtt[:, mt:mt + 1], scale=1.0),
                reads=["psB", "bdtt"], writes=["xpb:%d" % b])
            yield lambda b=b: P.op("scalar", lambda s: s.activation(
                axb[b][:], xpb[b][:], AF.Abs, bias=0.0, scale=1.0),
                reads=["xpb:%d" % b], writes=["axb:%d" % b])
            yield lambda b=b: P.op("scalar", lambda s: s.activation(
                eab[b][:], axb[b][:], AF.Exp, bias=0.0, scale=-1.0),
                reads=["axb:%d" % b], writes=["eab:%d" % b])
            yield lambda b=b: P.op("scalar", lambda s: s.activation(
                lab[b][:], eab[b][:], AF.Ln, bias=csts[:, 1:2], scale=1.0),
                reads=["eab:%d" % b, "cst1"], writes=["lab:%d" % b])
            yield lambda b=b, mt=mt, p=p: P.op("vector", lambda v: v.scalar_tensor_tensor(
                delta_t[p][:, mt * CH:(mt + 1) * CH], xpb[b][:], 0.0, lab[b][:],
                ALU.max, ALU.add),
                reads=["xpb:%d" % b, "lab:%d" % b], writes=["dl:%d:%d" % (p, mt)])
            yield lambda mt=mt, p=p: P.op("vector", lambda v: v.tensor_mul(
                du_t[p][:, mt * CH:(mt + 1) * CH],
                delta_t[p][:, mt * CH:(mt + 1) * CH], ubf[:, mt * CH:(mt + 1) * CH]),
                reads=["dl:%d:%d" % (p, mt), "ubf:%d" % mt], writes=["du:%d:%d" % (p, mt)])

    def post_scan(sc):
        p = sc % 2
        g0 = sc * 16
        bodies = [(n, q) for n in range(16) for q in range(16)]
        L = len(bodies)

        def emit_bc(gg):
            n = gg % 16
            pp = (gg // 16) % 2
            bk = gg % 2
            e2 = gg % 3
            P.op("vector", lambda v, n=n, pp=pp: v.stream_shuffle(
                shstB[64:96, :], BCt[pp][64:96, :], [n] * 32),
                reads=["BCt:%d" % pp], writes=["shB"])
            P.op("tensor", lambda t, bk=bk: t.matmul(
                psBC[bk][:, 0:CH], ones2[64:65, 0:128], shstB[64:65, :],
                start=True, stop=True),
                reads=["shB", "ones2"], writes=["BbC:%d" % bk])
            P.op("vector", lambda v, n=n, pp=pp: v.stream_shuffle(
                shstC[64:96, :], BCt[pp][64:96, :], [16 + n] * 32),
                reads=["BCt:%d" % pp], writes=["shC"])
            P.op("tensor", lambda t, bk=bk: t.matmul(
                psBC[bk][:, CH:2 * CH], ones2[64:65, 0:128], shstC[64:65, :],
                start=True, stop=True),
                reads=["shC", "ones2"], wr_override=["BbC:%d" % bk])
            P.op("scalar", lambda s, bk=bk, e2=e2: s.activation(
                Bsb[e2][:], psBC[bk][:, 0:CH], AF.Copy, bias=0.0, scale=1.0),
                reads=["BbC:%d" % bk], writes=["Bsb:%d" % e2])
            P.op("scalar", lambda s, bk=bk, e2=e2: s.activation(
                Csb[e2][:], psBC[bk][:, CH:2 * CH], AF.Copy, bias=0.0, scale=1.0),
                reads=["BbC:%d" % bk], writes=["Csb:%d" % e2])

        def emit_dA(i):
            n, q = bodies[i]
            b = i % NDA
            P.op("scalar", lambda s, q=q, n=n, b=b: s.activation(
                dAb[b][:], delta_t[p][:, q * CH:(q + 1) * CH], AF.Exp,
                bias=0.0, scale=alneg[:, q * 16 + n:q * 16 + n + 1]),
                reads=["dl:%d:%d" % (p, q), "alneg"], writes=["dA:%d" % b])

        def emit_dbu(i):
            n, q = bodies[i]
            b3 = i % 3
            e2 = (g0 + n) % 3
            P.op("vector", lambda v, q=q, b3=b3, e2=e2: v.tensor_mul(
                dbub[b3][:], du_t[p][:, q * CH:(q + 1) * CH], Bsb[e2][:]),
                reads=["du:%d:%d" % (p, q), "Bsb:%d" % e2], writes=["dbu:%d" % b3])

        def emit_scan(i):
            b = i % NDA
            b3 = i % 3
            P.op("vector", lambda v, b=b, b3=b3: v.tensor_tensor_scan(
                hbuf[b3][:], dAb[b][:], dbub[b3][:], 0.0, ALU.mult, ALU.add),
                reads=["dA:%d" % b, "dbu:%d" % b3], writes=["h:%d" % b3])

        def emit_yc(i):
            n, q = bodies[i]
            b = i % 2
            b3 = i % 3
            e2 = (g0 + n) % 3
            if n == 0:
                P.op("vector", lambda v, q=q, b3=b3, e2=e2: v.tensor_mul(
                    yA[:, q * CH:(q + 1) * CH], hbuf[b3][:], Csb[e2][:]),
                    reads=["h:%d" % b3, "Csb:%d" % e2], writes=["yA:%d" % q])
            elif n % 2 == 1 and n != 15:
                P.op("vector", lambda v, q=q, b3=b3, e2=e2: v.tensor_mul(
                    pa[q][:], hbuf[b3][:], Csb[e2][:]),
                    reads=["h:%d" % b3, "Csb:%d" % e2], writes=["pa:%d" % q])
            else:
                P.op("vector", lambda v, q=q, b=b, b3=b3, e2=e2: v.tensor_mul(
                    ypb[b][:], hbuf[b3][:], Csb[e2][:]),
                    reads=["h:%d" % b3, "Csb:%d" % e2], writes=["yp:%d" % b])
                if n % 2 == 0:
                    P.op("vector", lambda v, q=q, b=b: v.tensor_add(
                        ypb[b][:], ypb[b][:], pa[q][:]),
                        reads=["yp:%d" % b, "pa:%d" % q], writes=["yp:%d" % b])
                P.op("vector", lambda v, q=q, b=b: v.tensor_add(
                    yA[:, q * CH:(q + 1) * CH],
                    yA[:, q * CH:(q + 1) * CH], ypb[b][:]),
                    reads=["yA:%d" % q, "yp:%d" % b], writes=["yA:%d" % q])

        for i in range(L + 2):
            if i < L:
                n, q = bodies[i]
                if sc == 0 and i == 0:
                    yield lambda: emit_bc(0)
                if q == 2 and n + 1 < 16:
                    yield lambda n=n: emit_bc(g0 + n + 1)
                if i == 226 and sc + 1 < NSC:
                    yield lambda: emit_bc(g0 + 16)
                yield lambda i=i: emit_dA(i)
                yield lambda i=i: emit_dbu(i)
            if 1 <= i <= L:
                yield lambda i=i: emit_scan(i - 1)
            if i >= 2:
                yield lambda i=i: emit_yc(i - 2)

        for q in range(16):
            yield lambda q=q, p=p: P.op("vector", lambda v: v.tensor_mul(
                gated[p][:, q * CH:(q + 1) * CH], yA[:, q * CH:(q + 1) * CH],
                sres[p][:, q * CH:(q + 1) * CH]),
                reads=["yA:%d" % q, "sr:%d:%d" % (p, q)], writes=["gt:%d:%d" % (p, q)])

    def post_out(sc):
        p = sc % 2
        t0 = sc * CH
        for mtok in range(2):
            yield lambda mtok=mtok, t0=t0: P.op("sync", lambda e: e.dma_start(
                xrel[mtok][:], xh.ap()[2 + t0 + mtok * 128:2 + t0 + (mtok + 1) * 128, :]),
                writes=["xrel:%d" % mtok], dma=True)
        for f in range(2):
            for kg in range(4):
                wb2 = (f * 4 + kg) % 2
                yield lambda kg=kg, f=f, wb2=wb2: P.op("sync", lambda e: e.dma_start(
                    wo_buf[wb2][:].rearrange("part (k c) -> part k c", k=4),
                    s_woutT.ap()[kg * 512:(kg + 1) * 512,
                                 f * 512:(f + 1) * 512].rearrange(
                        "(k part) c -> part k c", k=4)),
                    reads=["swout:%d:0" % kq for kq in range(kg * 4, kg * 4 + 4)],
                    writes=["wo:%d" % wb2], dma=True)
                for j in range(4):
                    kq = kg * 4 + j
                    for mtok in range(2):
                        ps_ = psOut if mtok == 0 else psO2
                        pst_ = "psOut" if mtok == 0 else "psO2"
                        yield lambda kq=kq, j=j, mtok=mtok, wb2=wb2, ps_=ps_, pst_=pst_, p=p: P.op(
                            "tensor", lambda t: t.matmul(
                                ps_[:, 0:512],
                                gated[p][:, kq * CH + mtok * 128:kq * CH + (mtok + 1) * 128],
                                wo_buf[wb2][:, j * 512:(j + 1) * 512],
                                start=(kq == 0), stop=(kq == 15)),
                            reads=["gt:%d:%d" % (p, kq), "wo:%d" % wb2],
                            writes=([pst_] if kq == 0 else ()),
                            wr_override=([pst_] if kq == 15 else ()))
            for mtok in range(2):
                ps_ = psOut if mtok == 0 else psO2
                pst_ = "psOut" if mtok == 0 else "psO2"
                yield lambda f=f, mtok=mtok, ps_=ps_, pst_=pst_: P.op(
                    "vector", lambda v: v.tensor_add(
                        mmo[mtok][:, f * 512:(f + 1) * 512], ps_[:, 0:512],
                        xrel[mtok][:, f * 512:(f + 1) * 512]),
                    reads=[pst_, "xrel:%d" % mtok], writes=["mmo:%d:%d" % (mtok, f)])
        for mtok in range(2):
            yield lambda mtok=mtok, t0=t0: P.op("gpsimd", lambda e: e.dma_start(
                s_mm.ap()[t0 + mtok * 128:t0 + (mtok + 1) * 128, :], mmo[mtok][:]),
                reads=["mmo:%d:0" % mtok, "mmo:%d:1" % mtok],
                writes=["smm:%d" % (t0 // 128 + mtok)], dma=True)

    def _drain(it):
        for fn in it:
            fn()

    def _zip(post_it, pre_it, r_post=4, r_pre=1):
        done_a = done_b = False
        while not done_b:
            for _ in range(r_post):
                fn = next(post_it, None)
                if fn is None:
                    done_a = True
                    break
                fn()
            for _ in range(r_pre):
                fn = next(pre_it, None)
                if fn is None:
                    done_b = True
                    break
                fn()
            if done_a and done_b:
                break

    def f0_abs():
        for wi, wsrc in enumerate((wgT, wuT, wdT)):
            rows, cols = (DM, INNER) if wi < 2 else (INNER, DM)
            nt = (rows // 128) * (cols // 1024)
            ncb = cols // 1024
            for t in range(nt):
                r, c = t // ncb, t % ncb
                b = t % 2
                yield lambda wsrc=wsrc, r=r, c=c, b=b: P.op("gpsimd", lambda e: e.dma_start(
                    wab[b][:], wsrc.ap()[r * 128:(r + 1) * 128, c * 1024:(c + 1) * 1024]),
                    writes=["wab:%d" % b], dma=True)
                yield lambda wi=wi, t=t, b=b: P.op("scalar", lambda sx: sx.activation(
                    wab[b][:], wab[b][:], AF.Abs, bias=0.0, scale=1.0,
                    accum_out=acc[:, wi * 32 + t:wi * 32 + t + 1]),
                    reads=["wab:%d" % b], writes=["wab:%d" % b, "acc:%d:%d" % (wi, t)])

    f0_it = iter(f0_abs())

    def f0_take(nmax):
        for _ in range(nmax):
            fn = next(f0_it, None)
            if fn is None:
                return
            yield fn

    def _chain(*its):
        for it in its:
            for x in it:
                yield x

    _drain(pre_a_mm(0))
    _drain(pre_a_conv(0))
    _drain(pre_b(0))
    for sc in range(NSC):
        post_it = iter(post_scan(sc))
        if sc > 0:
            _zip(post_it, iter(post_out(sc - 1)), 1, 1)
        if sc + 1 < NSC:
            _zip(post_it, iter(pre_a_mm(sc + 1)), 2, 3)
            _drain(pre_a_conv(sc + 1))
            _zip(post_it, _chain(iter(pre_b(sc + 1)), f0_take(48)), 4, 1)
        else:
            _zip(post_it, f0_take(1 << 30), 4, 1)
        _drain(post_it)
    _drain(post_out(NSC - 1))
    _drain(f0_it)

    mstk.close()
    pmstk.close()
    P.barrier()

    P.op("tensor", lambda t: t.matmul(psA[0:1, 0:96], ones1[:], acc[:, 0:96],
                                      start=True, stop=True),
         reads=["ones1"] + ["acc:%d:%d" % (wi2, t2) for wi2 in range(3)
                            for t2 in range(32)],
         writes=["psA"], wr_override=["psA"])
    P.op("scalar", lambda s: s.activation(acc[0:1, 0:96], psA[0:1, 0:96], AF.Copy,
                                          bias=0.0, scale=1.0),
         reads=["psA"], writes=["accr"])
    for wi in range(3):
        P.op("vector", lambda v, wi=wi: v.reduce_sum(
            swrow[0:1, 12 + wi:13 + wi], acc[0:1, wi * 32:(wi + 1) * 32], axis=AXX),
            reads=["accr"], writes=["swsum"])
    P.op("vector", lambda v: v.tensor_scalar(swrow[0:1, 0:3], swrow[0:1, 12:15],
                                             1.0 / NW_ELEMS, 1e-5, ALU.mult, ALU.max),
         reads=["swsum"], writes=["swclip"])
    P.op("vector", lambda v: v.tensor_scalar_mul(swrow[0:1, 3:6], swrow[0:1, 0:3], 0.5),
         reads=["swclip"], writes=["swthr"])
    P.op("vector", lambda v: v.tensor_scalar_mul(swrow[0:1, 6:9], swrow[0:1, 0:3], -0.5),
         reads=["swclip"], writes=["swnthr"])
    P.op("tensor", lambda t: t.matmul(psA[:, 0:9], ones2[0:1, 0:128], swrow[0:1, 0:9],
                                      start=True, stop=True),
         reads=["ones2", "swclip", "swthr", "swnthr"], writes=["psA"], wr_override=["psA"])
    P.op("vector", lambda v: v.tensor_copy(swbc[:], psA[:, 0:9]),
         reads=["psA"], writes=["swbc"])


    fboth = ExitStack()
    psT = fboth.enter_context(nc.psum_tensor("psTf", [128, 1024], BF16))
    psC = fboth.enter_context(nc.psum_tensor("psC", [128, 512], F32))
    gu_bf = salloc(fboth, "gu_bf", [128, NT * INNER], BF16)
    fstk = ExitStack()
    xqfm = salloc(fstk, "xqfm", [128, 8 * TOK], BF16)
    wgq = [salloc(fstk, "wgq%d" % i, [128, 8 * 512], BF16) for i in range(2)]
    wuq = [salloc(fstk, "wuq%d" % i, [128, 8 * 512], BF16) for i in range(2)]
    wqf = [salloc(fstk, "wqf%d" % i, [128, 512], F32) for i in range(2)]
    wts_1 = salloc(fstk, "wts", [128, 512], F32)
    wts = [wts_1, wts_1]
    gsb = [salloc(fstk, "gsb%d" % i, [128, 512], BF16) for i in range(2)]
    usb = [salloc(fstk, "usb%d" % i, [128, 512], BF16) for i in range(2)]
    wdstg = salloc(fstk, "wdstg", [128, 512], BF16)
    wqf3_1 = salloc(fstk, "wqf3", [128, 512], F32)
    wqf3 = [wqf3_1, wqf3_1]
    psG2 = fstk.enter_context(nc.psum_tensor("psG2", [128, 512], F32))
    psU2 = fstk.enter_context(nc.psum_tensor("psU2", [128, 512], F32))
    psG3 = fstk.enter_context(nc.psum_tensor("psG3", [128, 512], F32))
    psU3 = fstk.enter_context(nc.psum_tensor("psU3", [128, 512], F32))
    psGpool = [(psB, "psB"), (psG2, "psG2"), (psG3, "psG3")]
    psUpool = [(psC, "psC"), (psU2, "psU2"), (psU3, "psU3")]

    for k2 in range(8):
        P.op("sync", lambda e, k2=k2: e.dma_start(
            xqfm[:, k2 * TOK:(k2 + 1) * TOK], s_xqfm.ap()[k2 * 128:(k2 + 1) * 128, :]),
            reads=["sxq:%d:%d" % (k2, i2) for i2 in range(NT)],
            writes=["xqfm:%d" % k2], dma=True)

    for i in range(NT):
        P.op("vector", lambda v, i=i: v.tensor_mul(
            combog[:, i:i + 1], invsx[:, i:i + 1], swbc[:, 0:1]),
            reads=["invsx:%d" % i, "swbc"], writes=["cg:%d" % i])
        P.op("vector", lambda v, i=i: v.tensor_mul(
            combou[:, i:i + 1], invsx[:, i:i + 1], swbc[:, 1:2]),
            reads=["invsx:%d" % i, "swbc"], writes=["cu:%d" % i])

    def f1_quant(f):
        fb = f % 2
        for (wsrc, wtag, col) in ((wgT, "wgq", 0), (wuT, "wuq", 1)):
            wdst = wgq[fb] if col == 0 else wuq[fb]
            for k2 in range(8):
                b = k2 % 2
                yield lambda wsrc=wsrc, k2=k2, f=f, b=b: P.op("sync", lambda e: e.dma_start(
                    wqf[b][:], wsrc.ap()[k2 * 128:(k2 + 1) * 128, f * 512:(f + 1) * 512]),
                    writes=["wqf:%d" % b], dma=True)
                yield lambda b=b, col=col: P.op("vector", lambda v: v.tensor_single_scalar(
                    wts[b][:], wqf[b][:], swbc[:, 6 + col:7 + col], ALU.is_le),
                    reads=["wqf:%d" % b, "swbc"], writes=["wts"])
                yield lambda b=b, col=col, wdst=wdst, k2=k2, wtag=wtag, fb=fb: P.op(
                    "vector", lambda v: v.scalar_tensor_tensor(
                        wdst[:, k2 * 512:(k2 + 1) * 512], wqf[b][:],
                        swbc[:, 3 + col:4 + col], wts[b][:], ALU.is_ge, ALU.subtract),
                    reads=["wqf:%d" % b, "wts", "swbc"],
                    writes=["%s:%d:%d" % (wtag, fb, k2)])

    def f1_mm(f):
        fb = f % 2
        for mt in range(NT):
            gb = mt % 2
            psg, psgt = psGpool[mt % 3]
            psu, psut = psUpool[mt % 3]
            for (wtag, ps_, pst_) in (("wgq", psg, psgt), ("wuq", psu, psut)):
                wq_ = wgq[fb] if wtag == "wgq" else wuq[fb]
                for k2 in range(8):
                    yield lambda wq_=wq_, k2=k2, mt=mt, ps_=ps_, pst_=pst_, wtag=wtag, fb=fb: P.op(
                        "tensor", lambda t: t.matmul(
                            ps_[:, 0:512],
                            xqfm[:, k2 * TOK + mt * 128:k2 * TOK + (mt + 1) * 128],
                            wq_[:, k2 * 512:(k2 + 1) * 512], start=(k2 == 0), stop=(k2 == 7)),
                        reads=["xqfm:%d" % k2, "%s:%d:%d" % (wtag, fb, k2)],
                        writes=([pst_] if k2 == 0 else ()),
                        wr_override=([pst_] if k2 == 7 else ()))
            yield lambda gb=gb, mt=mt, psg=psg, psgt=psgt: P.op("scalar", lambda s: s.activation(
                gsb[gb][:], psg[:, 0:512], AF.Silu, bias=0.0, scale=combog[:, mt:mt + 1]),
                reads=[psgt, "cg:%d" % mt], writes=["gsb:%d" % gb])
            yield lambda gb=gb, mt=mt, psu=psu, psut=psut: P.op("scalar", lambda s: s.activation(
                usb[gb][:], psu[:, 0:512], AF.Copy, bias=0.0, scale=combou[:, mt:mt + 1]),
                reads=[psut, "cu:%d" % mt], writes=["usb:%d" % gb])
            yield lambda gb=gb, mt=mt, f=f: P.op("vector", lambda v: v.tensor_mul(
                gu_bf[:, mt * INNER + f * 512:mt * INNER + (f + 1) * 512],
                gsb[gb][:], usb[gb][:]),
                reads=["gsb:%d" % gb, "usb:%d" % gb], writes=["gu:%d:%d" % (mt, f)])
            yield lambda gb=gb, mt=mt, f=f: P.op("vector", lambda v: v.reduce_max(
                gmparts[:, mt * 8 + f:mt * 8 + f + 1],
                gu_bf[:, mt * INNER + f * 512:mt * INNER + (f + 1) * 512],
                axis=AXX, apply_absolute_value=True),
                reads=["gu:%d:%d" % (mt, f)], writes=["gmp:%d:%d" % (mt, f)])

    def _zipf(a_it, b_it, ra=2, rb=1):
        done_a = done_b = False
        while not (done_a and done_b):
            for _ in range(ra):
                fn = next(a_it, None)
                if fn is None:
                    done_a = True
                    break
                fn()
            for _ in range(rb):
                fn = next(b_it, None)
                if fn is None:
                    done_b = True
                    break
                fn()

    def f1_wd():
        for t in range(32):
            for hcol in range(2):
                wb9 = (t * 2 + hcol) % 2
                yield lambda t=t, hcol=hcol, wb9=wb9: P.op("sync", lambda e: e.dma_start(
                    wqf3[wb9][:], wdT.ap()[t * 128:(t + 1) * 128, hcol * 512:(hcol + 1) * 512]),
                    writes=["wqf3"], dma=True)
                def _wdq_pair(wb9=wb9):
                    P.op("vector", lambda v: v.tensor_single_scalar(
                        wts[0][:], wqf3[wb9][:], swbc[:, 8:9], ALU.is_le),
                        reads=["wqf3", "swbc"], writes=["wts"])
                    P.op("vector", lambda v: v.scalar_tensor_tensor(
                        wdstg[:], wqf3[wb9][:], swbc[:, 5:6], wts[0][:], ALU.is_ge,
                        ALU.subtract),
                        reads=["wqf3", "wts", "swbc"], writes=["wdstg"])
                yield _wdq_pair
                yield lambda t=t, hcol=hcol: P.op("gpsimd", lambda e: e.dma_start(
                    s_wdq.ap()[t * 128:(t + 1) * 128, hcol * 512:(hcol + 1) * 512],
                    wdstg[:]),
                    reads=["wdstg"], writes=["swdq:%d:%d" % (t, hcol)], dma=True)

    wd_it = iter(f1_wd())

    def _sec(f):
        q_it = iter(f1_quant(f + 1)) if f + 1 < 8 else iter(())
        wd_left = 32 if f < 7 else 1 << 30
        while True:
            got = False
            for _ in range(3):
                fn = next(q_it, None)
                if fn is not None:
                    got = True
                    yield fn
            for _ in range(2):
                if wd_left <= 0:
                    break
                fn = next(wd_it, None)
                if fn is not None:
                    wd_left -= 1
                    got = True
                    yield fn
            if not got:
                return

    for fn in f1_quant(0):
        fn()
    for f in range(8):
        _zipf(iter(f1_mm(f)), _sec(f), 2, 1)
    for fn in wd_it:
        fn()

    for mt in range(NT):
        P.op("vector", lambda v, mt=mt: v.reduce_max(
            sgu[:, mt:mt + 1], gmparts[:, mt * 8:mt * 8 + 8], axis=AXX),
            reads=["gmp:%d:%d" % (mt, f2) for f2 in range(8)], writes=["gmax:%d" % mt])
        P.op("vector", lambda v, mt=mt: v.tensor_single_scalar(
            invsgu[:, mt:mt + 1], sgu[:, mt:mt + 1], 1e-5, ALU.max),
            reads=["gmax:%d" % mt], writes=["gclip:%d" % mt])
        P.op("vector", lambda v, mt=mt: v.reciprocal(
            gmparts[:, mt * 8:mt * 8 + 1], invsgu[:, mt:mt + 1]),
            reads=["gclip:%d" % mt], writes=["grcp:%d" % mt])
        P.op("vector", lambda v, mt=mt: v.tensor_scalar_mul(
            sgu[:, mt:mt + 1], gmparts[:, mt * 8:mt * 8 + 1], 127.0),
            reads=["grcp:%d" % mt], writes=["sgu:%d" % mt])
        P.op("vector", lambda v, mt=mt: v.tensor_scalar_mul(
            invsgu[:, mt:mt + 1], invsgu[:, mt:mt + 1], 1.0 / 127.0),
            reads=["gclip:%d" % mt], writes=["invsgu:%d" % mt])
        P.op("vector", lambda v, mt=mt: v.tensor_mul(
            combod[:, mt:mt + 1], invsgu[:, mt:mt + 1], swbc[:, 2:3]),
            reads=["invsgu:%d" % mt, "swbc"], writes=["cd:%d" % mt])

    fstk.close()
    P.barrier()

    f2stk = ExitStack()
    psP = [[psB, psC]] + [
        [f2stk.enter_context(nc.psum_tensor("psP%d_%d" % (pp, j), [128, 512], F32))
         for j in range(2)] for pp in (1, 2)]
    bob = salloc(f2stk, "bob", [128, DM], F32)
    tqg = [salloc(f2stk, "tqg%d" % i, [128, 1024], F32) for i in range(2)]
    tmg = tqg
    guqT = [[salloc(f2stk, "guqT%d_%d" % (gb, i), [128, INNER], BF16)
             for i in range(2)] for gb in range(2)]
    wdqt = [salloc(f2stk, "wdqt%d" % i, [128, 8 * 512], BF16) for i in range(2)]
    dnsb = [salloc(f2stk, "dnsb%d" % i, [128, DM], F32) for i in range(2)]
    smt = [salloc(f2stk, "smt%d" % i, [128, DM], F32) for i in range(2)]
    outsb = dnsb
    P.op("sync", lambda e: e.dma_start(bob[:], b_out_bc_d.ap()), writes=["bob"], dma=True)

    def f2_a(g):
        gb = g % 2
        mts = (2 * g, 2 * g + 1)
        for j, mt in enumerate(mts):
            for hh in range(4):
                hb = hh % 2
                yield lambda mt=mt, hh=hh, hb=hb: P.op("scalar", lambda s: s.activation(
                    tqg[hb][:], gu_bf[:, mt * INNER + hh * 1024:mt * INNER + (hh + 1) * 1024],
                    AF.Identity, bias=csts[:, 2:3], scale=sgu[:, mt:mt + 1]),
                    reads=["gu:%d:%d" % (mt, f2) for f2 in range(hh * 2, hh * 2 + 2)]
                          + ["sgu:%d" % mt, "cst2"],
                    writes=["tqg:%d" % hb])
                yield lambda hb=hb: P.op("vector", lambda v: v.tensor_single_scalar(
                    tmg[hb][:].bitcast(I32), tqg[hb][:].bitcast(I32), RMASK, ALU.bitwise_and),
                    reads=["tqg:%d" % hb], writes=["tqg:%d" % hb])
                yield lambda mt=mt, hh=hh, hb=hb: P.op("vector", lambda v: v.tensor_single_scalar(
                    gu_bf[:, mt * INNER + hh * 1024:mt * INNER + (hh + 1) * 1024],
                    tmg[hb][:], -384.0, ALU.add),
                    reads=["tqg:%d" % hb] + ["gu:%d:%d" % (mt, f2) for f2 in range(hh * 2, hh * 2 + 2)],
                    writes=["guqq:%d:%d" % (mt, hh)])
            for gb8 in range(4):
                for kb in range(8):
                    kk = gb8 * 8 + kb
                    yield lambda mt=mt, kk=kk, kb=kb: P.op("tensor", lambda t: t.matmul(
                        psT[:, kb * 128:(kb + 1) * 128],
                        gu_bf[:, mt * INNER + kk * 128:mt * INNER + (kk + 1) * 128],
                        eye[:], is_transpose=True, start=(kb == 0), stop=(kb == 7)),
                        reads=["guqq:%d:%d" % (mt, kk // 8), "eye"],
                        writes=(["psT"] if kb == 0 else ()),
                        wr_override=(["psT"] if kb == 7 else ()))
                yield lambda j=j, gb8=gb8, gb=gb: P.op("scalar", lambda s: s.activation(
                    guqT[gb][j][:, gb8 * 1024:(gb8 + 1) * 1024], psT[:, 0:1024], AF.Copy,
                    bias=0.0, scale=1.0),
                    reads=["psT"], writes=["guqT:%d:%d:%d" % (gb, j, gb8)])

    def f2_mm(g):
        gb = g % 2
        for f2 in range(2):
            pp = (2 * g + f2) % 3
            for kc in range(4):
                wb3 = kc % 2
                yield lambda kc=kc, f2=f2, wb3=wb3: P.op("sync", lambda e: e.dma_start(
                    wdqt[wb3][:].rearrange("part (k c) -> part k c", k=8),
                    s_wdq.ap()[kc * 1024:(kc + 1) * 1024,
                               f2 * 512:(f2 + 1) * 512].rearrange(
                        "(k part) c -> part k c", k=8)),
                    reads=["swdq:%d:%d" % (kc * 8 + k3, f2) for k3 in range(8)],
                    writes=["wdqt:%d" % wb3], dma=True)
                for k3 in range(8):
                    kb = kc * 8 + k3
                    for j in range(2):
                        yield lambda j=j, kb=kb, k3=k3, wb3=wb3, pp=pp, gb=gb: P.op(
                            "tensor", lambda t: t.matmul(
                                psP[pp][j][:, 0:512], guqT[gb][j][:, kb * 128:(kb + 1) * 128],
                                wdqt[wb3][:, k3 * 512:(k3 + 1) * 512],
                                start=(kb == 0), stop=(kb == 31)),
                            reads=["guqT:%d:%d:%d" % (gb, j, kb // 8), "wdqt:%d" % wb3],
                            writes=(["psP:%d:%d" % (pp, j)] if kb == 0 else ()),
                            wr_override=(["psP:%d:%d" % (pp, j)] if kb == 31 else ()))

    def f2_out(g):
        mts = (2 * g, 2 * g + 1)
        for f2 in range(2):
            pp = (2 * g + f2) % 3
            for j, mt in enumerate(mts):
                yield lambda j=j, mt=mt, f2=f2, pp=pp: P.op(
                    "scalar", lambda s: s.activation(
                        dnsb[j][:, f2 * 512:(f2 + 1) * 512], psP[pp][j][:, 0:512], AF.Copy,
                        bias=0.0, scale=combod[:, mt:mt + 1]),
                    reads=["psP:%d:%d" % (pp, j), "cd:%d" % mt], writes=["dn:%d:%d" % (j, f2)])
        for j, mt in enumerate(mts):
            yield lambda j=j, mt=mt: P.op("sync", lambda e: e.dma_start(
                smt[j][:], s_mm.ap()[mt * 128:(mt + 1) * 128, :]),
                reads=["smm:%d" % mt], writes=["smt:%d" % j], dma=True)
            yield lambda j=j: P.op("vector", lambda v: v.tensor_add(
                outsb[j][:], dnsb[j][:], smt[j][:]),
                reads=["dn:%d:0" % j, "dn:%d:1" % j, "smt:%d" % j],
                writes=["dn:%d:0" % j, "dn:%d:1" % j])
            yield lambda j=j: P.op("vector", lambda v: v.tensor_add(
                outsb[j][:], outsb[j][:], bob[:]),
                reads=["dn:%d:0" % j, "dn:%d:1" % j, "bob"],
                writes=["dn:%d:0" % j, "dn:%d:1" % j])
            yield lambda j=j, mt=mt: P.op("gpsimd", lambda e: e.dma_start(
                out_d.ap()[mt * 128:(mt + 1) * 128, :], outsb[j][:]),
                reads=["dn:%d:0" % j, "dn:%d:1" % j], writes=["outd:%d" % mt], dma=True)

    for fn in f2_a(0):
        fn()
    for g in range(NT // 2):
        sec = _chain(iter(f2_out(g - 1)) if g > 0 else iter(()),
                     iter(f2_a(g + 1)) if g + 1 < NT // 2 else iter(()))
        _zipf(iter(f2_mm(g)), sec, 3, 2)
    for fn in f2_out(NT // 2 - 1):
        fn()

    f2stk.close()
    fboth.close()

    P.op("sync", None, reads=["outd:%d" % mt for mt in range(NT)])
    P.op("gpsimd", None, reads=["outd:%d" % mt for mt in range(NT)])

    P.emit()
    pstack.close()
    nc.finalize()
    return nc


_NC_CACHE = {}


def _get_nc(TOK):
    if TOK not in _NC_CACHE:
        _NC_CACHE[TOK] = build_program(TOK)
    return _NC_CACHE[TOK]


def _host_inputs(inputs, TOK=2048):
    f32 = np.float32
    x = np.asarray(inputs["L0_attn_x"], dtype=f32)
    B, L, _ = x.shape
    flat = np.ascontiguousarray(x.reshape(B * L, DM))
    W_in = np.asarray(inputs["L0_attn_W_in"], dtype=f32)
    b_in = np.asarray(inputs["L0_attn_b_in"], dtype=f32)
    conv_w = np.asarray(inputs["L0_attn_conv_w"], dtype=f32)
    conv_b = np.asarray(inputs["L0_attn_conv_b"], dtype=f32)
    W_x = np.asarray(inputs["L0_attn_W_x"], dtype=f32)
    W_dt = np.asarray(inputs["L0_attn_W_dt"], dtype=f32)
    b_dt = np.asarray(inputs["L0_attn_b_dt"], dtype=f32)
    A_log = np.asarray(inputs["L0_attn_A_log"], dtype=f32)
    W_out = np.asarray(inputs["L0_attn_W_out"], dtype=f32)
    b_out = np.asarray(inputs["L0_attn_b_out"], dtype=f32)
    w_gate = np.asarray(inputs["L0_ffn_w_gate"], dtype=f32)
    w_up = np.asarray(inputs["L0_ffn_w_up"], dtype=f32)
    w_down = np.asarray(inputs["L0_ffn_w_down"], dtype=f32)

    shared = {
        "w_inT": np.ascontiguousarray(W_in.T),
        "w_xT": np.ascontiguousarray(W_x.T),
        "w_dtT": np.ascontiguousarray(W_dt.T),
        "w_outT": np.ascontiguousarray(W_out.T),
        "wgT": np.ascontiguousarray(w_gate.T),
        "wuT": np.ascontiguousarray(w_up.T),
        "wdT": np.ascontiguousarray(w_down.T),
        "b_in_t": np.ascontiguousarray(b_in.reshape(32, 128).T),
        "conv_w_t": np.ascontiguousarray(
            conv_w[:, 0, :].reshape(16, 128, 3).transpose(1, 2, 0).reshape(128, 48)),
        "conv_b_t": np.ascontiguousarray(conv_b.reshape(16, 128).T),
        "b_dt_t": np.ascontiguousarray(b_dt.reshape(16, 128).T),
        "alog_t": np.ascontiguousarray(
            A_log.reshape(16, 128, 16).transpose(1, 0, 2).reshape(128, 256)),
        "b_out_bc": np.ascontiguousarray(np.broadcast_to(b_out.reshape(1, DM), (128, DM))),
        "eye_in": np.eye(128, dtype=BFNP),
    }
    ncores = (B * L) // TOK
    in_maps = []
    for k in range(ncores):
        start = k * TOK
        seq_start = (start % L) == 0
        halo = np.zeros((2, DM), f32) if seq_start else flat[start - 2:start]
        xh_k = np.ascontiguousarray(np.concatenate([halo, flat[start:start + TOK]], axis=0))
        hm = np.full((128, 2), 0.0 if seq_start else 1.0, dtype=f32)
        m = dict(shared)
        m["xh"] = xh_k
        m["hmask_in"] = hm
        in_maps.append(m)
    return in_maps, (B, L)


def kernel(**inputs):
    TOK = 2048
    in_maps, (B, L) = _host_inputs(inputs, TOK)
    nc = _get_nc(TOK)
    res = run_bass_kernel_spmd(nc, in_maps, core_ids=list(range(len(in_maps))))
    outs = [np.asarray(r["out"]) for r in res.results]
    full = np.concatenate(outs, axis=0).reshape(B, L, DM)
    return full.astype(np.float32)


# version 34 (final)
# speedup vs baseline: 1.0577x; 1.0073x over previous
import numpy as np
import ml_dtypes
from contextlib import ExitStack

import concourse.bass as bass
import concourse.mybir as mybir
from concourse.bass_utils import run_bass_kernel_spmd

F32 = mybir.dt.float32
BF16 = mybir.dt.bfloat16
I32 = mybir.dt.int32
AF = mybir.ActivationFunctionType
ALU = mybir.AluOpType
AXX = mybir.AxisListType.X
BFNP = ml_dtypes.bfloat16

DM = 1024
DI = 2048
NS = 16
DTR = 64
INNER = 4096
CH = 256
NCORES = 8
NW_ELEMS = float(INNER * DM)
RMASK = 0xFFFF8000 - (1 << 32)
RBIAS = 384.5


class _Op:
    __slots__ = ("q", "dma", "fn", "waits", "inc", "cum", "skey")

    def __init__(self):
        self.waits = set()
        self.inc = False


class Prog:
    QUEUES = ("sync", "gpsimd", "tensor", "scalar", "vector")
    DMA_SLOTS = 8

    def __init__(self, nc):
        self.nc = nc
        self.qops = {q: [] for q in self.QUEUES}
        self.streams = {}
        self.slotlast = {}
        self.dmacnt = {q: 0 for q in self.QUEUES}
        self.tags = {}
        self.pend = {}

    def op(self, q, fn, reads=(), writes=(), dma=False, wr_override=()):
        o = _Op()
        o.q, o.dma, o.fn = q, dma, fn
        pend = self.pend.pop(q, None)
        deps = set(pend) if pend else set()
        for t in reads:
            e = self.tags.get(t)
            if e is not None and e[0] is not None:
                deps.add(e[0])
        for t in writes:
            e = self.tags.get(t)
            if e is not None:
                if e[0] is not None:
                    deps.add(e[0])
                deps.update(e[1])
        if dma:
            slot = self.dmacnt[q] % self.DMA_SLOTS
            self.dmacnt[q] += 1
            o.skey = (q, "d%d" % slot)
            prev = self.slotlast.get(o.skey)
            if prev is not None:
                deps.add(prev)
            self.slotlast[o.skey] = o
            o.inc = True
        else:
            o.skey = (q, "c")
        for d in deps:
            if d is o:
                continue
            d.inc = True
            o.waits.add(d)
        for t in reads:
            self.tags.setdefault(t, [None, []])[1].append(o)
        for t in writes:
            self.tags[t] = [o, []]
        for t in wr_override:
            self.tags[t] = [o, []]
        c = self.streams.get(o.skey, 0) + 1
        self.streams[o.skey] = c
        o.cum = c * (16 if dma else 1)
        self.qops[q].append(o)
        return o

    def barrier(self):
        last = set()
        for q in self.QUEUES:
            for o in reversed(self.qops[q]):
                if not o.dma:
                    last.add(o)
                    o.inc = True
                    break
        for o in self.slotlast.values():
            last.add(o)
            o.inc = True
        for q in self.QUEUES:
            self.pend[q] = self.pend.get(q, set()) | set(last)

    def emit(self):
        nc = self.nc
        es = ExitStack()
        sems = {}
        for key in self.streams:
            sems[key] = es.enter_context(nc.semaphore("sem_%s_%s" % key))
        with nc.Block() as block:
            for q in self.QUEUES:
                lst = self.qops[q]
                if not lst:
                    continue

                def body(eng, lst=lst, q=q):
                    sat = {}
                    last_inc = {}
                    for o in lst:
                        wl = {}
                        for d in o.waits:
                            if d.cum > wl.get(d.skey, 0):
                                wl[d.skey] = d.cum
                        for k in sorted(wl, key=lambda t: (t[0], t[1])):
                            v = wl[k]
                            if v > sat.get(k, 0):
                                eng.wait_ge(sems[k], v)
                                sat[k] = v
                        inst = o.fn(eng) if o.fn is not None else None
                        if o.inc:
                            k = o.skey
                            amt = o.cum - last_inc.get(k, 0)
                            inst.then_inc(sems[k], amt)
                            last_inc[k] = o.cum

                getattr(block, q)(body)
        es.close()


def build_program(TOK):
    NT = TOK // 128
    NSC = TOK // CH
    nc = bass.Bass("TRN2", target_bir_lowering=False)

    xh = nc.dram_tensor("xh", [TOK + 2, DM], F32, kind="ExternalInput")
    w_inT = nc.dram_tensor("w_inT", [DM, 2 * DI], F32, kind="ExternalInput")
    w_xT = nc.dram_tensor("w_xT", [DI, 96], F32, kind="ExternalInput")
    w_dtT = nc.dram_tensor("w_dtT", [DTR, DI], F32, kind="ExternalInput")
    w_outT = nc.dram_tensor("w_outT", [DI, DM], F32, kind="ExternalInput")
    wgT = nc.dram_tensor("wgT", [DM, INNER], F32, kind="ExternalInput")
    wuT = nc.dram_tensor("wuT", [DM, INNER], F32, kind="ExternalInput")
    wdT = nc.dram_tensor("wdT", [INNER, DM], F32, kind="ExternalInput")
    b_in_t = nc.dram_tensor("b_in_t", [128, 32], F32, kind="ExternalInput")
    conv_w_t = nc.dram_tensor("conv_w_t", [128, 48], F32, kind="ExternalInput")
    conv_b_t = nc.dram_tensor("conv_b_t", [128, 16], F32, kind="ExternalInput")
    b_dt_t = nc.dram_tensor("b_dt_t", [128, 16], F32, kind="ExternalInput")
    alog_t = nc.dram_tensor("alog_t", [128, 256], F32, kind="ExternalInput")
    b_out_bc_d = nc.dram_tensor("b_out_bc", [128, DM], F32, kind="ExternalInput")
    eye_in = nc.dram_tensor("eye_in", [128, 128], BF16, kind="ExternalInput")
    hmask_in = nc.dram_tensor("hmask_in", [128, 2], F32, kind="ExternalInput")
    out_d = nc.dram_tensor("out", [TOK, DM], F32, kind="ExternalOutput")

    s_winT = nc.dram_tensor("s_winT", [DM, 2 * DI], BF16)
    s_woutT = nc.dram_tensor("s_woutT", [DI, DM], BF16)
    s_xnfm = nc.dram_tensor("s_xnfm", [DM, TOK], BF16)
    s_xqfm = nc.dram_tensor("s_xqfm", [DM, TOK], BF16)
    s_wdq = nc.dram_tensor("s_wdq", [INNER, DM], BF16)
    s_mm = nc.dram_tensor("s_mm", [TOK, DM], F32)

    P = Prog(nc)
    half = 0.5

    pstack = ExitStack()

    def salloc(stack, name, shape, dt):
        return stack.enter_context(nc.sbuf_tensor(name, shape, dt))

    hmask = salloc(pstack, "hmask", [128, 2], F32)
    eye = salloc(pstack, "eye", [128, 128], BF16)
    invsx = salloc(pstack, "invsx", [128, NT], F32)
    swrow = salloc(pstack, "swrow", [1, 16], F32)
    swbc = salloc(pstack, "swbc", [128, 9], F32)
    ones2 = salloc(pstack, "ones2", [128, 128], F32)
    combog = salloc(pstack, "combog", [128, NT], F32)
    combou = salloc(pstack, "combou", [128, NT], F32)
    combod = salloc(pstack, "combod", [128, NT], F32)
    gmparts = salloc(pstack, "gmparts", [128, NT * 8], F32)
    sgu = salloc(pstack, "sgu", [128, NT], F32)
    invsgu = salloc(pstack, "invsgu", [128, NT], F32)
    ones1 = salloc(pstack, "ones1", [128, 1], F32)
    csts = salloc(pstack, "csts", [128, 3], F32)
    acc = salloc(pstack, "acc", [128, 96], F32)

    psA = pstack.enter_context(nc.psum_tensor("psA", [128, 512], F32))
    psB = pstack.enter_context(nc.psum_tensor("psB", [128, 512], F32))

    pmstk = ExitStack()
    psI = [pmstk.enter_context(nc.psum_tensor("psI%d" % i, [128, 256], F32)) for i in range(2)]
    bint = salloc(pmstk, "bint", [128, 32], F32)
    cwt = salloc(pmstk, "cwt", [128, 48], F32)
    cbt = salloc(pmstk, "cbt", [128, 16], F32)
    bdtt = salloc(pmstk, "bdtt", [128, 16], F32)
    alneg = salloc(pmstk, "alneg", [128, 256], F32)
    wxT_bf = salloc(pmstk, "wxT_bf", [128, 16 * 96], BF16)
    wdtT_bf = salloc(pmstk, "wdtT_bf", [64, DI], BF16)
    xnh_fm = salloc(pmstk, "xnh_fm", [128, 16], BF16)

    p0 = ExitStack()
    psT0 = [p0.enter_context(nc.psum_tensor("psT%d" % i, [128, 1024], BF16))
            for i in range(2)]
    ROT = 3
    sxs = salloc(p0, "sxs", [128, NT], F32)
    xt = [salloc(p0, "xt%d" % i, [128, DM], F32) for i in range(5)]
    dump = [salloc(p0, "dump%d" % i, [128, DM], F32) for i in range(2)]
    stats = [salloc(p0, "stats%d" % i, [128, 16], F32) for i in range(ROT)]
    xn = [salloc(p0, "xn%d" % i, [128, DM], F32) for i in range(ROT)]
    xnb = [salloc(p0, "xnb%d" % i, [128, DM], BF16) for i in range(ROT)]
    tq0 = [salloc(p0, "tq0%d" % i, [128, DM], F32) for i in range(ROT)]
    tm0 = [salloc(p0, "tm0%d" % i, [128, DM], F32) for i in range(ROT)]
    xqb = [salloc(p0, "xqb%d" % i, [128, DM], BF16) for i in range(ROT)]
    trev = [salloc(p0, "trev%d" % i, [128, DM], BF16) for i in range(ROT)]
    wcf = [salloc(p0, "wcf%d" % i, [128, 4096], F32) for i in range(2)]
    wcb = [salloc(p0, "wcb%d" % i, [128, 4096], BF16) for i in range(2)]
    xhalo = salloc(p0, "xhalo", [2, DM], F32)
    hstat = salloc(p0, "hstat", [2, 16], F32)
    xnhb = salloc(p0, "xnhb", [2, DM], BF16)
    asum = salloc(p0, "asum", [128, 3], F32)

    P.op("sync", lambda e: e.dma_start(bint[:], b_in_t.ap()), writes=["bint"], dma=True)
    P.op("sync", lambda e: e.dma_start(cwt[:], conv_w_t.ap()), writes=["cwt"], dma=True)
    P.op("sync", lambda e: e.dma_start(cbt[:], conv_b_t.ap()), writes=["cbt"], dma=True)
    P.op("sync", lambda e: e.dma_start(bdtt[:], b_dt_t.ap()), writes=["bdtt"], dma=True)
    P.op("sync", lambda e: e.dma_start(alneg[:], alog_t.ap()), writes=["alraw"], dma=True)
    P.op("sync", lambda e: e.dma_start(hmask[:], hmask_in.ap()), writes=["hmask"], dma=True)
    P.op("sync", lambda e: e.dma_start(eye[:], eye_in.ap()), writes=["eye"], dma=True)
    P.op("vector", lambda v: v.memset(ones1[:], 1.0), writes=["ones1"])
    P.op("vector", lambda v: v.memset(csts[:, 0:1], 1e-5), writes=["cst0"])
    P.op("vector", lambda v: v.memset(csts[:, 1:2], 1.0), writes=["cst1"])
    P.op("vector", lambda v: v.memset(csts[:, 2:3], RBIAS), writes=["cst2"])
    P.op("vector", lambda v: v.memset(ones2[:], 1.0), writes=["ones2"])

    P.op("scalar", lambda s: s.activation(alneg[:], alneg[:], AF.Exp, bias=0.0, scale=1.0),
         reads=["alraw"], writes=["alneg_t"])
    P.op("vector", lambda v: v.tensor_scalar_mul(alneg[:], alneg[:], -1.0),
         reads=["alneg_t"], writes=["alneg"])

    P.op("sync", lambda e: e.dma_start(
        wcf[0][:, 0:16 * 96].rearrange("part (k c) -> part k c", k=16),
        w_xT.ap().rearrange("(k part) c -> part k c", k=16)),
        writes=["wcf:0"], dma=True)
    P.op("vector", lambda v: v.tensor_copy(wxT_bf[:], wcf[0][:, 0:16 * 96]),
         reads=["wcf:0"], writes=["wxT_bf"])
    P.op("sync", lambda e: e.dma_start(wcf[1][0:64, 0:DI], w_dtT.ap()),
         writes=["wcf:1"], dma=True)
    P.op("vector", lambda v: v.tensor_copy(wdtT_bf[:], wcf[1][0:64, 0:DI]),
         reads=["wcf:1"], writes=["wdtT_bf"])

    def wcast_gen():
        wcnt = 0
        for (src, dst, rows, cols, tagbase) in ((w_inT, s_winT, DM, 2 * DI, "swin"),
                                                (w_outT, s_woutT, DI, DM, "swout")):
            ntile = rows // 128
            ncol = cols // 4096 if cols >= 4096 else 1
            cw = 4096 if cols >= 4096 else cols
            for r in range(ntile):
                for c in range(ncol):
                    b = wcnt % 2
                    lq = "sync" if wcnt % 2 == 0 else "scalar"
                    wcnt += 1
                    yield lambda src=src, r=r, c=c, b=b, cw=cw, lq=lq: P.op(
                        lq, lambda e: e.dma_start(
                            wcf[b][:, 0:cw],
                            src.ap()[r * 128:(r + 1) * 128, c * cw:(c + 1) * cw]),
                        writes=["wcf:%d" % b], dma=True)
                    yield lambda b=b, cw=cw: P.op("vector", lambda v: v.tensor_copy(
                        wcb[b][:, 0:cw], wcf[b][:, 0:cw]),
                        reads=["wcf:%d" % b], writes=["wcb:%d" % b])
                    yield lambda dst=dst, r=r, c=c, b=b, cw=cw, tagbase=tagbase: P.op(
                        "gpsimd", lambda e: e.dma_start(
                            dst.ap()[r * 128:(r + 1) * 128, c * cw:(c + 1) * cw],
                            wcb[b][:, 0:cw]),
                        reads=["wcb:%d" % b],
                        writes=["%s:%d:%d" % (tagbase, r, c)], dma=True)

    def _tile_ops(i):
        if True:
            ishalo = (i == NT)
            part = 2 if ishalo else 128
            b = i % ROT
            bx = i % 5
            db = i % 2
            xtt = xhalo if ishalo else xt[bx]
            stt = hstat if ishalo else stats[b]
            xnt = xn[b][0:part, :] if not ishalo else xn[0][0:2, :]
            xnbt = xnhb if ishalo else xnb[b]
            rt = "h" if ishalo else str(bx)
            strt = "hs" if ishalo else "s%d" % b
            txn = ("xn:%d" % b) if not ishalo else "xn:0"
            txnb = ("xnb:%d" % b) if not ishalo else "xnb:h"

            if ishalo:
                yield lambda: P.op("sync", lambda e: e.dma_start(
                    xhalo[:], xh.ap()[0:2, :]),
                    writes=["xt:h"], dma=True)
            else:
                yield lambda i=i, bx=bx, rt=rt: P.op("sync", lambda e: e.dma_start(
                    xt[bx][:], xh.ap()[2 + i * 128:2 + (i + 1) * 128, :]),
                    writes=["xt:" + rt], dma=True)
            xv = xtt[0:part, :]
            sv_ = stt[0:part, :]
            yield lambda xv=xv, sv_=sv_, rt=rt, strt=strt: P.op(
                "vector", lambda v: v.reduce_sum(sv_[:, 0:1], xv, axis=AXX),
                reads=["xt:" + rt], writes=[strt + ":sum"])
            yield lambda xv=xv, sv_=sv_, db=db, part=part, rt=rt, strt=strt: P.op(
                "scalar", lambda s: s.activation(
                    dump[db][0:part, :], xv, AF.Square, bias=0.0, scale=1.0,
                    accum_out=sv_[:, 1:2]),
                reads=["xt:" + rt], writes=["dump:%d" % db, strt + ":sq"])
            yield lambda sv_=sv_, strt=strt: P.op("vector", lambda v: v.tensor_scalar_mul(
                sv_[:, 2:3], sv_[:, 0:1], 1.0 / DM),
                reads=[strt + ":sum"], writes=[strt + ":mean"])
            yield lambda sv_=sv_, strt=strt: P.op("vector", lambda v: v.tensor_scalar_mul(
                sv_[:, 3:4], sv_[:, 1:2], 1.0 / DM),
                reads=[strt + ":sq"], writes=[strt + ":ex2"])
            yield lambda sv_=sv_, strt=strt: P.op("vector", lambda v: v.tensor_mul(
                sv_[:, 4:5], sv_[:, 2:3], sv_[:, 2:3]),
                reads=[strt + ":mean"], writes=[strt + ":msq"])
            yield lambda sv_=sv_, strt=strt: P.op("vector", lambda v: v.tensor_sub(
                sv_[:, 5:6], sv_[:, 3:4], sv_[:, 4:5]),
                reads=[strt + ":ex2", strt + ":msq"], writes=[strt + ":var"])
            yield lambda sv_=sv_, part=part, strt=strt: P.op("scalar", lambda s: s.activation(
                sv_[:, 11:12], sv_[:, 5:6], AF.Sqrt, bias=csts[0:part, 0:1], scale=1.0),
                reads=[strt + ":var", "cst0"], writes=[strt + ":std"])
            yield lambda sv_=sv_, strt=strt: P.op("vector", lambda v: v.reciprocal(
                sv_[:, 6:7], sv_[:, 11:12]),
                reads=[strt + ":std"], writes=[strt + ":rstd"])
            yield lambda sv_=sv_, strt=strt: P.op("vector", lambda v: v.tensor_scalar(
                sv_[:, 7:8], sv_[:, 2:3], sv_[:, 6:7], -1.0, ALU.mult, ALU.mult),
                reads=[strt + ":mean", strt + ":rstd"], writes=[strt + ":mrs"])
            yield lambda xv=xv, sv_=sv_, xnt=xnt, rt=rt, strt=strt, txn=txn: P.op(
                "scalar", lambda s: s.activation(
                    xnt, xv, AF.Identity, bias=sv_[:, 7:8], scale=sv_[:, 6:7]),
                reads=["xt:" + rt, strt + ":rstd", strt + ":mrs"], writes=[txn])
            yield lambda xnt=xnt, xnbt=xnbt, part=part, txn=txn, txnb=txnb: P.op(
                "vector", lambda v: v.tensor_copy(xnbt[0:part, :], xnt),
                reads=[txn], writes=[txnb])
            if not ishalo:
                yield lambda xnt=xnt, sv_=sv_, txn=txn, strt=strt: P.op(
                    "vector", lambda v: v.reduce_max(
                        sv_[:, 9:10], xnt, axis=AXX, apply_absolute_value=True),
                    reads=[txn], writes=[strt + ":rmax"])
                yield lambda sv_=sv_, strt=strt: P.op("vector", lambda v: v.tensor_single_scalar(
                    sv_[:, 8:9], sv_[:, 9:10], 1e-5, ALU.max),
                    reads=[strt + ":rmax"], writes=[strt + ":clip"])
                yield lambda sv_=sv_, i=i, strt=strt: P.op("vector", lambda v: v.tensor_scalar_mul(
                    invsx[:, i:i + 1], sv_[:, 8:9], 1.0 / 127.0),
                    reads=[strt + ":clip"], writes=["invsx:%d" % i])
                yield lambda sv_=sv_, strt=strt: P.op("vector", lambda v: v.reciprocal(
                    sv_[:, 10:11], sv_[:, 8:9]),
                    reads=[strt + ":clip"], writes=[strt + ":rcp"])
                yield lambda sv_=sv_, i=i, strt=strt: P.op("vector", lambda v: v.tensor_scalar_mul(
                    sxs[:, i:i + 1], sv_[:, 10:11], 127.0),
                    reads=[strt + ":rcp"], writes=["sx:%d" % i])
                yield lambda xnt=xnt, b=b, i=i, txn=txn: P.op("scalar", lambda s: s.activation(
                    tq0[b][:], xnt, AF.Identity, bias=csts[:, 2:3], scale=sxs[:, i:i + 1]),
                    reads=[txn, "sx:%d" % i, "cst2"], writes=["tq0:%d" % b])
                yield lambda b=b: P.op("vector", lambda v: v.tensor_single_scalar(
                    tm0[b][:].bitcast(I32), tq0[b][:].bitcast(I32), RMASK, ALU.bitwise_and),
                    reads=["tq0:%d" % b], writes=["tm0:%d" % b])
                yield lambda b=b: P.op("vector", lambda v: v.tensor_single_scalar(
                    xqb[b][:], tm0[b][:], -384.0, ALU.add),
                    reads=["tm0:%d" % b], writes=["xqb:%d" % b])
                for wi, (srcb, srct, dstd, tg) in enumerate(
                        ((xnb[b], txnb, s_xnfm, "sxn"),
                         (xqb[b], "xqb:%d" % b, s_xqfm, "sxq"))):
                    tp = (i * 2 + wi) % 2
                    for kb in range(8):
                        yield lambda srcb=srcb, kb=kb, srct=srct, tp=tp: P.op(
                            "tensor", lambda t: t.matmul(
                                psT0[tp][:, kb * 128:(kb + 1) * 128],
                                srcb[:, kb * 128:(kb + 1) * 128],
                                eye[:], is_transpose=True, start=(kb == 0), stop=(kb == 7)),
                            reads=[srct, "eye"],
                            writes=(["psT:%d" % tp] if kb == 0 else ()),
                            wr_override=(["psT:%d" % tp] if kb == 7 else ()))
                    yield lambda b=b, tp=tp: P.op("scalar", lambda s: s.activation(
                        trev[b][:], psT0[tp][:, 0:1024], AF.Copy, bias=0.0, scale=1.0),
                        reads=["psT:%d" % tp], writes=["trev:%d" % b])
                    yield lambda b=b, dstd=dstd, i=i, tg=tg: P.op(
                        "gpsimd", lambda e: e.dma_start(
                            dstd.ap()[:, i * 128:(i + 1) * 128].rearrange(
                                "(k part) c -> part k c", k=8),
                            trev[b][:].rearrange("part (k c) -> part k c", k=8)),
                        reads=["trev:%d" % b],
                        writes=["%s:%d:%d" % (tg, kb, i) for kb in range(8)], dma=True)
            else:
                for kb in range(8):
                    yield lambda kb=kb: P.op("tensor", lambda t: t.matmul(
                        psT0[0][:, kb * 2:(kb + 1) * 2], xnhb[0:2, kb * 128:(kb + 1) * 128],
                        eye[0:2, 0:2], is_transpose=True, start=(kb == 0), stop=(kb == 7)),
                        reads=["xnb:h", "eye"],
                        writes=(["psT:0"] if kb == 0 else ()),
                        wr_override=(["psT:0"] if kb == 7 else ()))
                yield lambda: P.op("scalar", lambda s: s.activation(
                    xnh_fm[:], psT0[0][:, 0:16], AF.Copy, bias=0.0, scale=1.0),
                    reads=["psT:0"], writes=["xnh_fm"])

    def ln_skewed():
        tiles = []
        for i in range(NT):
            it = _tile_ops(i)
            tiles.append([fn for fn in it])
        cuts = (0, 1, 3, 10, 20, None)
        offs = (0, 2, 3, 4, 5)
        for j in range(NT + 5):
            for st in range(5):
                i = j - offs[st]
                if 0 <= i < NT:
                    a, b = cuts[st], cuts[st + 1]
                    for fn in tiles[i][a:b]:
                        yield fn
        for fn in _tile_ops(NT):
            yield fn

    _wc = iter(wcast_gen())
    _ln = iter(ln_skewed())
    while True:
        got = False
        for _ in range(5):
            fn = next(_ln, None)
            if fn is not None:
                got = True
                fn()
        fn = next(_wc, None)
        if fn is not None:
            got = True
            fn()
        if not got:
            break

    p0.close()
    P.barrier()

    mstk = ExitStack()
    psBC = [mstk.enter_context(nc.psum_tensor("psBC%d" % i, [128, 512], F32))
            for i in range(2)]
    psO2 = mstk.enter_context(nc.psum_tensor("psO2", [128, 512], F32))
    psOut = mstk.enter_context(nc.psum_tensor("psOut", [128, 512], F32))
    xnsc = [salloc(mstk, "xnsc%d" % i, [128, 8 * CH], BF16) for i in range(2)]
    wch = [salloc(mstk, "wch%d" % i, [128, 2048], BF16) for i in range(3)]
    CH2 = CH + 2
    xc_t = salloc(mstk, "xc_t", [128, 16 * CH2], F32)
    sres = [salloc(mstk, "sres%d" % i, [128, 16 * CH], BF16) for i in range(2)]
    delta_t = [salloc(mstk, "delta_t%d" % i, [128, 16 * CH], BF16) for i in range(2)]
    du_t = [salloc(mstk, "du_t%d" % i, [128, 16 * CH], BF16) for i in range(2)]
    dltbf = [salloc(mstk, "dltbf%d" % i, [64, CH], BF16) for i in range(2)]
    BCt = [salloc(mstk, "BCt%d" % i, [128, CH], F32) for i in range(2)]
    ubf = salloc(mstk, "ubf", [128, 16 * CH], BF16)
    yA = salloc(mstk, "yA", [128, 16 * CH], F32)
    gated = [salloc(mstk, "gated%d" % i, [128, 16 * CH], BF16) for i in range(2)]
    shstB = salloc(mstk, "shstB", [128, CH], F32)
    shstC = salloc(mstk, "shstC", [128, CH], F32)
    NDA = 20
    dAb = [salloc(mstk, "dAb%d" % i, [128, CH], BF16) for i in range(NDA)]
    dbub = [salloc(mstk, "dbub%d" % i, [128, CH], BF16) for i in range(3)]
    hbuf = [salloc(mstk, "hbuf%d" % i, [128, CH], BF16) for i in range(3)]
    ypb = [salloc(mstk, "ypb%d" % i, [128, CH], BF16) for i in range(2)]
    pa = [salloc(mstk, "pa%d" % i, [128, CH], BF16) for i in range(16)]
    Bsb = [salloc(mstk, "Bsb%d" % i, [128, CH], BF16) for i in range(3)]
    Csb = [salloc(mstk, "Csb%d" % i, [128, CH], BF16) for i in range(3)]
    cva = [salloc(mstk, "cva%d" % i, [128, CH], F32) for i in range(2)]
    cvb = [salloc(mstk, "cvb%d" % i, [128, CH], F32) for i in range(2)]
    cvc = cva
    xpb = [salloc(mstk, "xpb%d" % i, [128, CH], F32) for i in range(2)]
    axb = [salloc(mstk, "axb0", [128, CH], F32)] * 2
    eab = [salloc(mstk, "eab0", [128, CH], F32)] * 2
    lab = [salloc(mstk, "lab%d" % i, [128, CH], F32) for i in range(2)]
    hv = salloc(mstk, "hv", [128, 2], F32)
    hv2 = salloc(mstk, "hv2", [128, 2], F32)
    wo_buf = [salloc(mstk, "wo_buf%d" % i, [128, 4 * 512], BF16) for i in range(2)]
    xrel = [salloc(mstk, "xrel%d" % i, [128, DM], F32) for i in range(2)]
    mmo = [salloc(mstk, "mmo%d" % i, [128, DM], F32) for i in range(2)]
    wab = [salloc(mstk, "wab%d" % i, [128, 1024], F32) for i in range(2)]

    def pre_a_mm(sc):
        p = sc % 2
        t0 = sc * CH
        yield lambda: P.op("sync", lambda e: e.dma_start(
            xnsc[p][:].rearrange("part (k c) -> part k c", k=8),
            s_xnfm.ap()[:, t0:t0 + CH].rearrange("(k part) c -> part k c", k=8)),
            reads=["sxn:%d:%d" % (k2, (t0 // 128) + j) for k2 in range(8)
                   for j in range(CH // 128)],
            writes=["xnsc:%d" % p], dma=True)
        for mg in range(16):
            wb = mg % 3
            yield lambda mg=mg, wb=wb: P.op("sync", lambda e: e.dma_start(
                wch[wb][:].rearrange("part (k c) -> part k c", k=8),
                s_winT.ap()[:, mg * 256:(mg + 1) * 256].rearrange(
                    "(k part) c -> part k c", k=8)),
                reads=["swin:%d:0" % k2 for k2 in range(8)],
                writes=["wch:%d" % wb], dma=True)
            for m in range(2):
                mt = mg * 2 + m
                mp = m % 2
                for k2 in range(8):
                    yield lambda wb=wb, k2=k2, m=m, mp=mp, p=p: P.op(
                        "tensor", lambda t: t.matmul(
                            psI[mp][:, 0:CH],
                            wch[wb][:, k2 * 256 + m * 128:k2 * 256 + (m + 1) * 128],
                            xnsc[p][:, k2 * CH:(k2 + 1) * CH],
                            start=(k2 == 0), stop=(k2 == 7)),
                        reads=["wch:%d" % wb, "xnsc:%d" % p],
                        writes=(["psI:%d" % mp] if k2 == 0 else ()),
                        wr_override=(["psI:%d" % mp] if k2 == 7 else ()))
                if sc == 0 and mt < 16:
                    for k2 in range(8):
                        yield lambda wb=wb, k2=k2, m=m: P.op(
                            "tensor", lambda t: t.matmul(
                                psA[:, 0:2],
                                wch[wb][:, k2 * 256 + m * 128:k2 * 256 + (m + 1) * 128],
                                xnh_fm[:, k2 * 2:(k2 + 1) * 2],
                                start=(k2 == 0), stop=(k2 == 7)),
                            reads=["wch:%d" % wb, "xnh_fm"],
                            writes=(["psA"] if k2 == 0 else ()),
                            wr_override=(["psA"] if k2 == 7 else ()))
                    yield lambda mt=mt: P.op("scalar", lambda s: s.activation(
                        hv[:], psA[:, 0:2], AF.Identity, bias=bint[:, mt:mt + 1],
                        scale=1.0),
                        reads=["psA", "bint"], writes=["hv"])
                    yield lambda: P.op("vector", lambda v: v.tensor_mul(
                        hv2[:], hv[:], hmask[:]),
                        reads=["hv", "hmask"], writes=["hv2"])
                    yield lambda mt=mt: P.op("vector", lambda v: v.tensor_copy(
                        xc_t[:, mt * CH2:mt * CH2 + 2], hv2[:]),
                        reads=["hv2"], writes=["xch:%d" % mt])
                if mt < 16:
                    yield lambda mp=mp, mt=mt: P.op("scalar", lambda s: s.activation(
                        xc_t[:, mt * CH2 + 2:mt * CH2 + 2 + CH], psI[mp][:, 0:CH],
                        AF.Identity, bias=bint[:, mt:mt + 1], scale=1.0),
                        reads=["psI:%d" % mp, "bint"], writes=["xcb:%d" % mt])
                else:
                    yield lambda mp=mp, mt=mt, p=p: P.op("scalar", lambda s: s.activation(
                        sres[p][:, (mt - 16) * CH:(mt - 15) * CH], psI[mp][:, 0:CH],
                        AF.Identity, bias=bint[:, mt:mt + 1], scale=1.0),
                        reads=["psI:%d" % mp, "bint"], writes=["sr:%d:%d" % (p, mt - 16)])

    def pre_a_conv(sc):
        p = sc % 2
        for q in range(16):
            cb2 = q % 2
            yield lambda q=q, cb2=cb2: P.op("vector", lambda v: v.tensor_single_scalar(
                cva[cb2][:], xc_t[:, q * CH2 + 0:q * CH2 + CH],
                cwt[:, 0 + q:0 + q + 1], ALU.mult),
                reads=["xch:%d" % q, "xcb:%d" % q, "cwt"],
                writes=["cva:%d" % cb2])
            yield lambda q=q, cb2=cb2: P.op("vector", lambda v: v.scalar_tensor_tensor(
                cvb[cb2][:], xc_t[:, q * CH2 + 1:q * CH2 + 1 + CH],
                cwt[:, 16 + q:16 + q + 1], cva[cb2][:], ALU.mult, ALU.add),
                reads=["xch:%d" % q, "xcb:%d" % q, "cva:%d" % cb2, "cwt"],
                writes=["cvb:%d" % cb2])
            yield lambda q=q, cb2=cb2: P.op("vector", lambda v: v.scalar_tensor_tensor(
                cvc[cb2][:], xc_t[:, q * CH2 + 2:q * CH2 + 2 + CH],
                cwt[:, 32 + q:32 + q + 1], cvb[cb2][:], ALU.mult, ALU.add),
                reads=["xcb:%d" % q, "cvb:%d" % cb2, "cwt"],
                writes=["cva:%d" % cb2])
            yield lambda q=q, cb2=cb2: P.op("scalar", lambda s: s.activation(
                ubf[:, q * CH:(q + 1) * CH], cvc[cb2][:], AF.Silu,
                bias=cbt[:, q:q + 1], scale=1.0),
                reads=["cva:%d" % cb2, "cbt"], writes=["ubf:%d" % q])
            if sc + 1 < NSC:
                yield lambda q=q: P.op("vector", lambda v: v.tensor_copy(
                    xc_t[:, q * CH2:q * CH2 + 2],
                    xc_t[:, q * CH2 + CH:q * CH2 + CH + 2]),
                    reads=["xcb:%d" % q], writes=["xch:%d" % q])
        for q in range(16):
            yield lambda q=q, p=p: P.op("scalar", lambda s: s.activation(
                sres[p][:, q * CH:(q + 1) * CH], sres[p][:, q * CH:(q + 1) * CH],
                AF.Silu, bias=0.0, scale=1.0),
                reads=["sr:%d:%d" % (p, q)], writes=["sr:%d:%d" % (p, q)])

    def pre_b(sc):
        p = sc % 2
        for k in range(16):
            yield lambda k=k: P.op("tensor", lambda t: t.matmul(
                psA[0:96, 0:CH], wxT_bf[:, k * 96:(k + 1) * 96],
                ubf[:, k * CH:(k + 1) * CH], start=(k == 0), stop=(k == 15)),
                reads=["wxT_bf", "ubf:%d" % k],
                writes=(["psA"] if k == 0 else ()),
                wr_override=(["psA"] if k == 15 else ()))
        yield lambda p=p: P.op("scalar", lambda s: s.activation(
            dltbf[p][:, :], psA[0:64, 0:CH], AF.Copy, bias=0.0, scale=1.0),
            reads=["psA"], writes=["dltbf:%d" % p])
        yield lambda p=p: P.op("scalar", lambda s: s.activation(
            BCt[p][64:96, :], psA[64:96, 0:CH], AF.Copy, bias=0.0, scale=1.0),
            reads=["psA"], writes=["BCt:%d" % p])
        for mt in range(16):
            b = mt % 2
            yield lambda mt=mt, p=p: P.op("tensor", lambda t: t.matmul(
                psB[:, 0:CH], wdtT_bf[:, mt * 128:(mt + 1) * 128], dltbf[p][:, :],
                start=True, stop=True),
                reads=["wdtT_bf", "dltbf:%d" % p], writes=["psB"], wr_override=["psB"])
            yield lambda b=b, mt=mt: P.op("scalar", lambda s: s.activation(
                xpb[b][:], psB[:, 0:CH], AF.Identity, bias=bdtt[:, mt:mt + 1], scale=1.0),
                reads=["psB", "bdtt"], writes=["xpb:%d" % b])
            yield lambda b=b: P.op("scalar", lambda s: s.activation(
                axb[b][:], xpb[b][:], AF.Abs, bias=0.0, scale=1.0),
                reads=["xpb:%d" % b], writes=["axb:%d" % b])
            yield lambda b=b: P.op("scalar", lambda s: s.activation(
                eab[b][:], axb[b][:], AF.Exp, bias=0.0, scale=-1.0),
                reads=["axb:%d" % b], writes=["eab:%d" % b])
            yield lambda b=b: P.op("scalar", lambda s: s.activation(
                lab[b][:], eab[b][:], AF.Ln, bias=csts[:, 1:2], scale=1.0),
                reads=["eab:%d" % b, "cst1"], writes=["lab:%d" % b])
            yield lambda b=b, mt=mt, p=p: P.op("vector", lambda v: v.scalar_tensor_tensor(
                delta_t[p][:, mt * CH:(mt + 1) * CH], xpb[b][:], 0.0, lab[b][:],
                ALU.max, ALU.add),
                reads=["xpb:%d" % b, "lab:%d" % b], writes=["dl:%d:%d" % (p, mt)])
            yield lambda mt=mt, p=p: P.op("vector", lambda v: v.tensor_mul(
                du_t[p][:, mt * CH:(mt + 1) * CH],
                delta_t[p][:, mt * CH:(mt + 1) * CH], ubf[:, mt * CH:(mt + 1) * CH]),
                reads=["dl:%d:%d" % (p, mt), "ubf:%d" % mt], writes=["du:%d:%d" % (p, mt)])

    def post_scan(sc):
        p = sc % 2
        g0 = sc * 16
        bodies = [(n, q) for n in range(16) for q in range(16)]
        L = len(bodies)

        def emit_bc(gg):
            n = gg % 16
            pp = (gg // 16) % 2
            bk = gg % 2
            e2 = gg % 3
            P.op("vector", lambda v, n=n, pp=pp: v.stream_shuffle(
                shstB[64:96, :], BCt[pp][64:96, :], [n] * 32),
                reads=["BCt:%d" % pp], writes=["shB"])
            P.op("tensor", lambda t, bk=bk: t.matmul(
                psBC[bk][:, 0:CH], ones2[64:65, 0:128], shstB[64:65, :],
                start=True, stop=True),
                reads=["shB", "ones2"], writes=["BbC:%d" % bk])
            P.op("vector", lambda v, n=n, pp=pp: v.stream_shuffle(
                shstC[64:96, :], BCt[pp][64:96, :], [16 + n] * 32),
                reads=["BCt:%d" % pp], writes=["shC"])
            P.op("tensor", lambda t, bk=bk: t.matmul(
                psBC[bk][:, CH:2 * CH], ones2[64:65, 0:128], shstC[64:65, :],
                start=True, stop=True),
                reads=["shC", "ones2"], wr_override=["BbC:%d" % bk])
            P.op("scalar", lambda s, bk=bk, e2=e2: s.activation(
                Bsb[e2][:], psBC[bk][:, 0:CH], AF.Copy, bias=0.0, scale=1.0),
                reads=["BbC:%d" % bk], writes=["Bsb:%d" % e2])
            P.op("scalar", lambda s, bk=bk, e2=e2: s.activation(
                Csb[e2][:], psBC[bk][:, CH:2 * CH], AF.Copy, bias=0.0, scale=1.0),
                reads=["BbC:%d" % bk], writes=["Csb:%d" % e2])

        def emit_dA(i):
            n, q = bodies[i]
            b = i % NDA
            P.op("scalar", lambda s, q=q, n=n, b=b: s.activation(
                dAb[b][:], delta_t[p][:, q * CH:(q + 1) * CH], AF.Exp,
                bias=0.0, scale=alneg[:, q * 16 + n:q * 16 + n + 1]),
                reads=["dl:%d:%d" % (p, q), "alneg"], writes=["dA:%d" % b])

        def emit_dbu(i):
            n, q = bodies[i]
            b3 = i % 3
            e2 = (g0 + n) % 3
            P.op("vector", lambda v, q=q, b3=b3, e2=e2: v.tensor_mul(
                dbub[b3][:], du_t[p][:, q * CH:(q + 1) * CH], Bsb[e2][:]),
                reads=["du:%d:%d" % (p, q), "Bsb:%d" % e2], writes=["dbu:%d" % b3])

        def emit_scan(i):
            b = i % NDA
            b3 = i % 3
            P.op("vector", lambda v, b=b, b3=b3: v.tensor_tensor_scan(
                hbuf[b3][:], dAb[b][:], dbub[b3][:], 0.0, ALU.mult, ALU.add),
                reads=["dA:%d" % b, "dbu:%d" % b3], writes=["h:%d" % b3])

        def emit_yc(i):
            n, q = bodies[i]
            b = i % 2
            b3 = i % 3
            e2 = (g0 + n) % 3
            if n == 0:
                P.op("vector", lambda v, q=q, b3=b3, e2=e2: v.tensor_mul(
                    yA[:, q * CH:(q + 1) * CH], hbuf[b3][:], Csb[e2][:]),
                    reads=["h:%d" % b3, "Csb:%d" % e2], writes=["yA:%d" % q])
            elif n % 2 == 1 and n != 15:
                P.op("vector", lambda v, q=q, b3=b3, e2=e2: v.tensor_mul(
                    pa[q][:], hbuf[b3][:], Csb[e2][:]),
                    reads=["h:%d" % b3, "Csb:%d" % e2], writes=["pa:%d" % q])
            else:
                P.op("vector", lambda v, q=q, b=b, b3=b3, e2=e2: v.tensor_mul(
                    ypb[b][:], hbuf[b3][:], Csb[e2][:]),
                    reads=["h:%d" % b3, "Csb:%d" % e2], writes=["yp:%d" % b])
                if n % 2 == 0:
                    P.op("vector", lambda v, q=q, b=b: v.tensor_add(
                        ypb[b][:], ypb[b][:], pa[q][:]),
                        reads=["yp:%d" % b, "pa:%d" % q], writes=["yp:%d" % b])
                P.op("vector", lambda v, q=q, b=b: v.tensor_add(
                    yA[:, q * CH:(q + 1) * CH],
                    yA[:, q * CH:(q + 1) * CH], ypb[b][:]),
                    reads=["yA:%d" % q, "yp:%d" % b], writes=["yA:%d" % q])

        for i in range(L + 2):
            if i < L:
                n, q = bodies[i]
                if sc == 0 and i == 0:
                    yield lambda: emit_bc(0)
                if q == 2 and n + 1 < 16:
                    yield lambda n=n: emit_bc(g0 + n + 1)
                if i == 226 and sc + 1 < NSC:
                    yield lambda: emit_bc(g0 + 16)
                yield lambda i=i: emit_dA(i)
                yield lambda i=i: emit_dbu(i)
            if 1 <= i <= L:
                yield lambda i=i: emit_scan(i - 1)
            if i >= 2:
                yield lambda i=i: emit_yc(i - 2)

        for q in range(16):
            yield lambda q=q, p=p: P.op("vector", lambda v: v.tensor_mul(
                gated[p][:, q * CH:(q + 1) * CH], yA[:, q * CH:(q + 1) * CH],
                sres[p][:, q * CH:(q + 1) * CH]),
                reads=["yA:%d" % q, "sr:%d:%d" % (p, q)], writes=["gt:%d:%d" % (p, q)])

    def post_out(sc):
        p = sc % 2
        t0 = sc * CH
        for mtok in range(2):
            yield lambda mtok=mtok, t0=t0: P.op("sync", lambda e: e.dma_start(
                xrel[mtok][:], xh.ap()[2 + t0 + mtok * 128:2 + t0 + (mtok + 1) * 128, :]),
                writes=["xrel:%d" % mtok], dma=True)
        for f in range(2):
            for kg in range(4):
                wb2 = (f * 4 + kg) % 2
                yield lambda kg=kg, f=f, wb2=wb2: P.op("sync", lambda e: e.dma_start(
                    wo_buf[wb2][:].rearrange("part (k c) -> part k c", k=4),
                    s_woutT.ap()[kg * 512:(kg + 1) * 512,
                                 f * 512:(f + 1) * 512].rearrange(
                        "(k part) c -> part k c", k=4)),
                    reads=["swout:%d:0" % kq for kq in range(kg * 4, kg * 4 + 4)],
                    writes=["wo:%d" % wb2], dma=True)
                for j in range(4):
                    kq = kg * 4 + j
                    for mtok in range(2):
                        ps_ = psOut if mtok == 0 else psO2
                        pst_ = "psOut" if mtok == 0 else "psO2"
                        yield lambda kq=kq, j=j, mtok=mtok, wb2=wb2, ps_=ps_, pst_=pst_, p=p: P.op(
                            "tensor", lambda t: t.matmul(
                                ps_[:, 0:512],
                                gated[p][:, kq * CH + mtok * 128:kq * CH + (mtok + 1) * 128],
                                wo_buf[wb2][:, j * 512:(j + 1) * 512],
                                start=(kq == 0), stop=(kq == 15)),
                            reads=["gt:%d:%d" % (p, kq), "wo:%d" % wb2],
                            writes=([pst_] if kq == 0 else ()),
                            wr_override=([pst_] if kq == 15 else ()))
            for mtok in range(2):
                ps_ = psOut if mtok == 0 else psO2
                pst_ = "psOut" if mtok == 0 else "psO2"
                yield lambda f=f, mtok=mtok, ps_=ps_, pst_=pst_: P.op(
                    "vector", lambda v: v.tensor_add(
                        mmo[mtok][:, f * 512:(f + 1) * 512], ps_[:, 0:512],
                        xrel[mtok][:, f * 512:(f + 1) * 512]),
                    reads=[pst_, "xrel:%d" % mtok], writes=["mmo:%d:%d" % (mtok, f)])
        for mtok in range(2):
            yield lambda mtok=mtok, t0=t0: P.op("gpsimd", lambda e: e.dma_start(
                s_mm.ap()[t0 + mtok * 128:t0 + (mtok + 1) * 128, :], mmo[mtok][:]),
                reads=["mmo:%d:0" % mtok, "mmo:%d:1" % mtok],
                writes=["smm:%d" % (t0 // 128 + mtok)], dma=True)

    def _drain(it):
        for fn in it:
            fn()

    def _zip(post_it, pre_it, r_post=4, r_pre=1):
        done_a = done_b = False
        while not done_b:
            for _ in range(r_post):
                fn = next(post_it, None)
                if fn is None:
                    done_a = True
                    break
                fn()
            for _ in range(r_pre):
                fn = next(pre_it, None)
                if fn is None:
                    done_b = True
                    break
                fn()
            if done_a and done_b:
                break

    def f0_abs():
        for wi, wsrc in enumerate((wgT, wuT, wdT)):
            rows, cols = (DM, INNER) if wi < 2 else (INNER, DM)
            nt = (rows // 128) * (cols // 1024)
            ncb = cols // 1024
            for t in range(nt):
                r, c = t // ncb, t % ncb
                b = t % 2
                yield lambda wsrc=wsrc, r=r, c=c, b=b: P.op("gpsimd", lambda e: e.dma_start(
                    wab[b][:], wsrc.ap()[r * 128:(r + 1) * 128, c * 1024:(c + 1) * 1024]),
                    writes=["wab:%d" % b], dma=True)
                yield lambda wi=wi, t=t, b=b: P.op("scalar", lambda sx: sx.activation(
                    wab[b][:], wab[b][:], AF.Abs, bias=0.0, scale=1.0,
                    accum_out=acc[:, wi * 32 + t:wi * 32 + t + 1]),
                    reads=["wab:%d" % b], writes=["wab:%d" % b, "acc:%d:%d" % (wi, t)])

    f0_it = iter(f0_abs())

    def f0_take(nmax):
        for _ in range(nmax):
            fn = next(f0_it, None)
            if fn is None:
                return
            yield fn

    def _chain(*its):
        for it in its:
            for x in it:
                yield x

    _drain(pre_a_mm(0))
    _drain(pre_a_conv(0))
    _drain(pre_b(0))
    for sc in range(NSC):
        post_it = iter(post_scan(sc))
        if sc > 0:
            _zip(post_it, iter(post_out(sc - 1)), 1, 1)
        if sc + 1 < NSC:
            _zip(post_it, iter(pre_a_mm(sc + 1)), 2, 3)
            _drain(pre_a_conv(sc + 1))
            _zip(post_it, _chain(iter(pre_b(sc + 1)), f0_take(48)), 4, 1)
        else:
            _zip(post_it, f0_take(1 << 30), 4, 1)
        _drain(post_it)
    _drain(post_out(NSC - 1))
    _drain(f0_it)

    mstk.close()
    pmstk.close()
    P.barrier()

    P.op("tensor", lambda t: t.matmul(psA[0:1, 0:96], ones1[:], acc[:, 0:96],
                                      start=True, stop=True),
         reads=["ones1"] + ["acc:%d:%d" % (wi2, t2) for wi2 in range(3)
                            for t2 in range(32)],
         writes=["psA"], wr_override=["psA"])
    P.op("scalar", lambda s: s.activation(acc[0:1, 0:96], psA[0:1, 0:96], AF.Copy,
                                          bias=0.0, scale=1.0),
         reads=["psA"], writes=["accr"])
    for wi in range(3):
        P.op("vector", lambda v, wi=wi: v.reduce_sum(
            swrow[0:1, 12 + wi:13 + wi], acc[0:1, wi * 32:(wi + 1) * 32], axis=AXX),
            reads=["accr"], writes=["swsum"])
    P.op("vector", lambda v: v.tensor_scalar(swrow[0:1, 0:3], swrow[0:1, 12:15],
                                             1.0 / NW_ELEMS, 1e-5, ALU.mult, ALU.max),
         reads=["swsum"], writes=["swclip"])
    P.op("vector", lambda v: v.tensor_scalar_mul(swrow[0:1, 3:6], swrow[0:1, 0:3], 0.5),
         reads=["swclip"], writes=["swthr"])
    P.op("vector", lambda v: v.tensor_scalar_mul(swrow[0:1, 6:9], swrow[0:1, 0:3], -0.5),
         reads=["swclip"], writes=["swnthr"])
    P.op("tensor", lambda t: t.matmul(psA[:, 0:9], ones2[0:1, 0:128], swrow[0:1, 0:9],
                                      start=True, stop=True),
         reads=["ones2", "swclip", "swthr", "swnthr"], writes=["psA"], wr_override=["psA"])
    P.op("vector", lambda v: v.tensor_copy(swbc[:], psA[:, 0:9]),
         reads=["psA"], writes=["swbc"])


    fboth = ExitStack()
    psT = fboth.enter_context(nc.psum_tensor("psTf", [128, 1024], BF16))
    psC = fboth.enter_context(nc.psum_tensor("psC", [128, 512], F32))
    gu_bf = salloc(fboth, "gu_bf", [128, NT * INNER], BF16)
    fstk = ExitStack()
    xqfm = salloc(fstk, "xqfm", [128, 8 * TOK], BF16)
    wgq = [salloc(fstk, "wgq%d" % i, [128, 8 * 512], BF16) for i in range(2)]
    wuq = [salloc(fstk, "wuq%d" % i, [128, 8 * 512], BF16) for i in range(2)]
    wqf = [salloc(fstk, "wqf%d" % i, [128, 512], F32) for i in range(2)]
    wts_1 = salloc(fstk, "wts", [128, 512], F32)
    wts = [wts_1, wts_1]
    gsb_1 = salloc(fstk, "gsb", [128, 512], BF16)
    usb_1 = salloc(fstk, "usb", [128, 512], BF16)
    gsb = [gsb_1, gsb_1]
    usb = [usb_1, usb_1]
    wdstg = salloc(fstk, "wdstg", [128, 512], BF16)
    wqf3 = [salloc(fstk, "wqf3_%d" % i, [128, 512], F32) for i in range(2)]
    psG2 = fstk.enter_context(nc.psum_tensor("psG2", [128, 512], F32))
    psU2 = fstk.enter_context(nc.psum_tensor("psU2", [128, 512], F32))
    psG3 = fstk.enter_context(nc.psum_tensor("psG3", [128, 512], F32))
    psU3 = fstk.enter_context(nc.psum_tensor("psU3", [128, 512], F32))
    psGpool = [(psB, "psB"), (psG2, "psG2"), (psG3, "psG3")]
    psUpool = [(psC, "psC"), (psU2, "psU2"), (psU3, "psU3")]

    for k2 in range(8):
        P.op("sync", lambda e, k2=k2: e.dma_start(
            xqfm[:, k2 * TOK:(k2 + 1) * TOK], s_xqfm.ap()[k2 * 128:(k2 + 1) * 128, :]),
            reads=["sxq:%d:%d" % (k2, i2) for i2 in range(NT)],
            writes=["xqfm:%d" % k2], dma=True)

    for i in range(NT):
        P.op("vector", lambda v, i=i: v.tensor_mul(
            combog[:, i:i + 1], invsx[:, i:i + 1], swbc[:, 0:1]),
            reads=["invsx:%d" % i, "swbc"], writes=["cg:%d" % i])
        P.op("vector", lambda v, i=i: v.tensor_mul(
            combou[:, i:i + 1], invsx[:, i:i + 1], swbc[:, 1:2]),
            reads=["invsx:%d" % i, "swbc"], writes=["cu:%d" % i])

    def f1_quant(f):
        fb = f % 2
        for (wsrc, wtag, col) in ((wgT, "wgq", 0), (wuT, "wuq", 1)):
            wdst = wgq[fb] if col == 0 else wuq[fb]
            for k2 in range(8):
                b = k2 % 2
                yield lambda wsrc=wsrc, k2=k2, f=f, b=b: P.op("sync", lambda e: e.dma_start(
                    wqf[b][:], wsrc.ap()[k2 * 128:(k2 + 1) * 128, f * 512:(f + 1) * 512]),
                    writes=["wqf:%d" % b], dma=True)
                yield lambda b=b, col=col: P.op("vector", lambda v: v.tensor_single_scalar(
                    wts[b][:], wqf[b][:], swbc[:, 6 + col:7 + col], ALU.is_le),
                    reads=["wqf:%d" % b, "swbc"], writes=["wts"])
                yield lambda b=b, col=col, wdst=wdst, k2=k2, wtag=wtag, fb=fb: P.op(
                    "vector", lambda v: v.scalar_tensor_tensor(
                        wdst[:, k2 * 512:(k2 + 1) * 512], wqf[b][:],
                        swbc[:, 3 + col:4 + col], wts[b][:], ALU.is_ge, ALU.subtract),
                    reads=["wqf:%d" % b, "wts", "swbc"],
                    writes=["%s:%d:%d" % (wtag, fb, k2)])

    def f1_mm(f):
        fb = f % 2
        for mt in range(NT):
            gb = mt % 2
            psg, psgt = psGpool[mt % 3]
            psu, psut = psUpool[mt % 3]
            for (wtag, ps_, pst_) in (("wgq", psg, psgt), ("wuq", psu, psut)):
                wq_ = wgq[fb] if wtag == "wgq" else wuq[fb]
                for k2 in range(8):
                    yield lambda wq_=wq_, k2=k2, mt=mt, ps_=ps_, pst_=pst_, wtag=wtag, fb=fb: P.op(
                        "tensor", lambda t: t.matmul(
                            ps_[:, 0:512],
                            xqfm[:, k2 * TOK + mt * 128:k2 * TOK + (mt + 1) * 128],
                            wq_[:, k2 * 512:(k2 + 1) * 512], start=(k2 == 0), stop=(k2 == 7)),
                        reads=["xqfm:%d" % k2, "%s:%d:%d" % (wtag, fb, k2)],
                        writes=([pst_] if k2 == 0 else ()),
                        wr_override=([pst_] if k2 == 7 else ()))
            yield lambda gb=gb, mt=mt, psg=psg, psgt=psgt: P.op("scalar", lambda s: s.activation(
                gsb[gb][:], psg[:, 0:512], AF.Silu, bias=0.0, scale=combog[:, mt:mt + 1]),
                reads=[psgt, "cg:%d" % mt], writes=["gsb"])
            yield lambda gb=gb, mt=mt, psu=psu, psut=psut: P.op("scalar", lambda s: s.activation(
                usb[gb][:], psu[:, 0:512], AF.Copy, bias=0.0, scale=combou[:, mt:mt + 1]),
                reads=[psut, "cu:%d" % mt], writes=["usb"])
            yield lambda gb=gb, mt=mt, f=f: P.op("vector", lambda v: v.tensor_mul(
                gu_bf[:, mt * INNER + f * 512:mt * INNER + (f + 1) * 512],
                gsb[gb][:], usb[gb][:]),
                reads=["gsb", "usb"], writes=["gu:%d:%d" % (mt, f)])
            yield lambda gb=gb, mt=mt, f=f: P.op("vector", lambda v: v.reduce_max(
                gmparts[:, mt * 8 + f:mt * 8 + f + 1],
                gu_bf[:, mt * INNER + f * 512:mt * INNER + (f + 1) * 512],
                axis=AXX, apply_absolute_value=True),
                reads=["gu:%d:%d" % (mt, f)], writes=["gmp:%d:%d" % (mt, f)])

    def _zipf(a_it, b_it, ra=2, rb=1):
        done_a = done_b = False
        while not (done_a and done_b):
            for _ in range(ra):
                fn = next(a_it, None)
                if fn is None:
                    done_a = True
                    break
                fn()
            for _ in range(rb):
                fn = next(b_it, None)
                if fn is None:
                    done_b = True
                    break
                fn()

    def f1_wd():
        for t in range(32):
            for hcol in range(2):
                wb9 = (t * 2 + hcol) % 2
                yield lambda t=t, hcol=hcol, wb9=wb9: P.op("sync", lambda e: e.dma_start(
                    wqf3[wb9][:], wdT.ap()[t * 128:(t + 1) * 128, hcol * 512:(hcol + 1) * 512]),
                    writes=["wqf3:%d" % wb9], dma=True)
                def _wdq_pair(wb9=wb9):
                    P.op("vector", lambda v: v.tensor_single_scalar(
                        wts[0][:], wqf3[wb9][:], swbc[:, 8:9], ALU.is_le),
                        reads=["wqf3:%d" % wb9, "swbc"], writes=["wts"])
                    P.op("vector", lambda v: v.scalar_tensor_tensor(
                        wdstg[:], wqf3[wb9][:], swbc[:, 5:6], wts[0][:], ALU.is_ge,
                        ALU.subtract),
                        reads=["wqf3:%d" % wb9, "wts", "swbc"], writes=["wdstg"])
                yield _wdq_pair
                yield lambda t=t, hcol=hcol: P.op("gpsimd", lambda e: e.dma_start(
                    s_wdq.ap()[t * 128:(t + 1) * 128, hcol * 512:(hcol + 1) * 512],
                    wdstg[:]),
                    reads=["wdstg"], writes=["swdq:%d:%d" % (t, hcol)], dma=True)

    wd_it = iter(f1_wd())

    def _sec(f):
        q_it = iter(f1_quant(f + 1)) if f + 1 < 8 else iter(())
        wd_left = 32 if f < 7 else 1 << 30
        while True:
            got = False
            for _ in range(3):
                fn = next(q_it, None)
                if fn is not None:
                    got = True
                    yield fn
            for _ in range(2):
                if wd_left <= 0:
                    break
                fn = next(wd_it, None)
                if fn is not None:
                    wd_left -= 1
                    got = True
                    yield fn
            if not got:
                return

    for fn in f1_quant(0):
        fn()
    for f in range(8):
        _zipf(iter(f1_mm(f)), _sec(f), 2, 1)
    for fn in wd_it:
        fn()

    for mt in range(NT):
        P.op("vector", lambda v, mt=mt: v.reduce_max(
            sgu[:, mt:mt + 1], gmparts[:, mt * 8:mt * 8 + 8], axis=AXX),
            reads=["gmp:%d:%d" % (mt, f2) for f2 in range(8)], writes=["gmax:%d" % mt])
        P.op("vector", lambda v, mt=mt: v.tensor_single_scalar(
            invsgu[:, mt:mt + 1], sgu[:, mt:mt + 1], 1e-5, ALU.max),
            reads=["gmax:%d" % mt], writes=["gclip:%d" % mt])
        P.op("vector", lambda v, mt=mt: v.reciprocal(
            gmparts[:, mt * 8:mt * 8 + 1], invsgu[:, mt:mt + 1]),
            reads=["gclip:%d" % mt], writes=["grcp:%d" % mt])
        P.op("vector", lambda v, mt=mt: v.tensor_scalar_mul(
            sgu[:, mt:mt + 1], gmparts[:, mt * 8:mt * 8 + 1], 127.0),
            reads=["grcp:%d" % mt], writes=["sgu:%d" % mt])
        P.op("vector", lambda v, mt=mt: v.tensor_scalar_mul(
            invsgu[:, mt:mt + 1], invsgu[:, mt:mt + 1], 1.0 / 127.0),
            reads=["gclip:%d" % mt], writes=["invsgu:%d" % mt])
        P.op("vector", lambda v, mt=mt: v.tensor_mul(
            combod[:, mt:mt + 1], invsgu[:, mt:mt + 1], swbc[:, 2:3]),
            reads=["invsgu:%d" % mt, "swbc"], writes=["cd:%d" % mt])

    fstk.close()
    P.barrier()

    f2stk = ExitStack()
    psP = [[psB, psC]] + [
        [f2stk.enter_context(nc.psum_tensor("psP%d_%d" % (pp, j), [128, 512], F32))
         for j in range(2)] for pp in (1, 2)]
    bob = salloc(f2stk, "bob", [128, DM], F32)
    tqg = [salloc(f2stk, "tqg%d" % i, [128, 1024], F32) for i in range(2)]
    tmg = tqg
    guqT = [[salloc(f2stk, "guqT%d_%d" % (gb, i), [128, INNER], BF16)
             for i in range(2)] for gb in range(2)]
    wdqt = [salloc(f2stk, "wdqt%d" % i, [128, 8 * 512], BF16) for i in range(2)]
    dnsb = [salloc(f2stk, "dnsb%d" % i, [128, DM], F32) for i in range(2)]
    smt = [salloc(f2stk, "smt%d" % i, [128, DM], F32) for i in range(2)]
    outsb = dnsb
    P.op("sync", lambda e: e.dma_start(bob[:], b_out_bc_d.ap()), writes=["bob"], dma=True)

    def f2_a(g):
        gb = g % 2
        mts = (2 * g, 2 * g + 1)
        for j, mt in enumerate(mts):
            for hh in range(4):
                hb = hh % 2
                yield lambda mt=mt, hh=hh, hb=hb: P.op("scalar", lambda s: s.activation(
                    tqg[hb][:], gu_bf[:, mt * INNER + hh * 1024:mt * INNER + (hh + 1) * 1024],
                    AF.Identity, bias=csts[:, 2:3], scale=sgu[:, mt:mt + 1]),
                    reads=["gu:%d:%d" % (mt, f2) for f2 in range(hh * 2, hh * 2 + 2)]
                          + ["sgu:%d" % mt, "cst2"],
                    writes=["tqg:%d" % hb])
                yield lambda hb=hb: P.op("vector", lambda v: v.tensor_single_scalar(
                    tmg[hb][:].bitcast(I32), tqg[hb][:].bitcast(I32), RMASK, ALU.bitwise_and),
                    reads=["tqg:%d" % hb], writes=["tqg:%d" % hb])
                yield lambda mt=mt, hh=hh, hb=hb: P.op("vector", lambda v: v.tensor_single_scalar(
                    gu_bf[:, mt * INNER + hh * 1024:mt * INNER + (hh + 1) * 1024],
                    tmg[hb][:], -384.0, ALU.add),
                    reads=["tqg:%d" % hb] + ["gu:%d:%d" % (mt, f2) for f2 in range(hh * 2, hh * 2 + 2)],
                    writes=["guqq:%d:%d" % (mt, hh)])
            for gb8 in range(4):
                for kb in range(8):
                    kk = gb8 * 8 + kb
                    yield lambda mt=mt, kk=kk, kb=kb: P.op("tensor", lambda t: t.matmul(
                        psT[:, kb * 128:(kb + 1) * 128],
                        gu_bf[:, mt * INNER + kk * 128:mt * INNER + (kk + 1) * 128],
                        eye[:], is_transpose=True, start=(kb == 0), stop=(kb == 7)),
                        reads=["guqq:%d:%d" % (mt, kk // 8), "eye"],
                        writes=(["psT"] if kb == 0 else ()),
                        wr_override=(["psT"] if kb == 7 else ()))
                yield lambda j=j, gb8=gb8, gb=gb: P.op("scalar", lambda s: s.activation(
                    guqT[gb][j][:, gb8 * 1024:(gb8 + 1) * 1024], psT[:, 0:1024], AF.Copy,
                    bias=0.0, scale=1.0),
                    reads=["psT"], writes=["guqT:%d:%d:%d" % (gb, j, gb8)])

    def f2_mm(g):
        gb = g % 2
        for f2 in range(2):
            pp = (2 * g + f2) % 3
            for kc in range(4):
                wb3 = kc % 2
                yield lambda kc=kc, f2=f2, wb3=wb3: P.op("sync", lambda e: e.dma_start(
                    wdqt[wb3][:].rearrange("part (k c) -> part k c", k=8),
                    s_wdq.ap()[kc * 1024:(kc + 1) * 1024,
                               f2 * 512:(f2 + 1) * 512].rearrange(
                        "(k part) c -> part k c", k=8)),
                    reads=["swdq:%d:%d" % (kc * 8 + k3, f2) for k3 in range(8)],
                    writes=["wdqt:%d" % wb3], dma=True)
                for k3 in range(8):
                    kb = kc * 8 + k3
                    for j in range(2):
                        yield lambda j=j, kb=kb, k3=k3, wb3=wb3, pp=pp, gb=gb: P.op(
                            "tensor", lambda t: t.matmul(
                                psP[pp][j][:, 0:512], guqT[gb][j][:, kb * 128:(kb + 1) * 128],
                                wdqt[wb3][:, k3 * 512:(k3 + 1) * 512],
                                start=(kb == 0), stop=(kb == 31)),
                            reads=["guqT:%d:%d:%d" % (gb, j, kb // 8), "wdqt:%d" % wb3],
                            writes=(["psP:%d:%d" % (pp, j)] if kb == 0 else ()),
                            wr_override=(["psP:%d:%d" % (pp, j)] if kb == 31 else ()))

    def f2_out(g):
        mts = (2 * g, 2 * g + 1)
        for f2 in range(2):
            pp = (2 * g + f2) % 3
            for j, mt in enumerate(mts):
                yield lambda j=j, mt=mt, f2=f2, pp=pp: P.op(
                    "scalar", lambda s: s.activation(
                        dnsb[j][:, f2 * 512:(f2 + 1) * 512], psP[pp][j][:, 0:512], AF.Copy,
                        bias=0.0, scale=combod[:, mt:mt + 1]),
                    reads=["psP:%d:%d" % (pp, j), "cd:%d" % mt], writes=["dn:%d:%d" % (j, f2)])
        for j, mt in enumerate(mts):
            yield lambda j=j, mt=mt: P.op("sync", lambda e: e.dma_start(
                smt[j][:], s_mm.ap()[mt * 128:(mt + 1) * 128, :]),
                reads=["smm:%d" % mt], writes=["smt:%d" % j], dma=True)
            yield lambda j=j: P.op("vector", lambda v: v.tensor_add(
                outsb[j][:], dnsb[j][:], smt[j][:]),
                reads=["dn:%d:0" % j, "dn:%d:1" % j, "smt:%d" % j],
                writes=["dn:%d:0" % j, "dn:%d:1" % j])
            yield lambda j=j: P.op("vector", lambda v: v.tensor_add(
                outsb[j][:], outsb[j][:], bob[:]),
                reads=["dn:%d:0" % j, "dn:%d:1" % j, "bob"],
                writes=["dn:%d:0" % j, "dn:%d:1" % j])
            yield lambda j=j, mt=mt: P.op("gpsimd", lambda e: e.dma_start(
                out_d.ap()[mt * 128:(mt + 1) * 128, :], outsb[j][:]),
                reads=["dn:%d:0" % j, "dn:%d:1" % j], writes=["outd:%d" % mt], dma=True)

    for fn in f2_a(0):
        fn()
    for g in range(NT // 2):
        sec = _chain(iter(f2_out(g - 1)) if g > 0 else iter(()),
                     iter(f2_a(g + 1)) if g + 1 < NT // 2 else iter(()))
        _zipf(iter(f2_mm(g)), sec, 3, 2)
    for fn in f2_out(NT // 2 - 1):
        fn()

    f2stk.close()
    fboth.close()

    P.op("sync", None, reads=["outd:%d" % mt for mt in range(NT)])
    P.op("gpsimd", None, reads=["outd:%d" % mt for mt in range(NT)])

    P.emit()
    pstack.close()
    nc.finalize()
    return nc


_NC_CACHE = {}


def _get_nc(TOK):
    if TOK not in _NC_CACHE:
        _NC_CACHE[TOK] = build_program(TOK)
    return _NC_CACHE[TOK]


def _host_inputs(inputs, TOK=2048):
    f32 = np.float32
    x = np.asarray(inputs["L0_attn_x"], dtype=f32)
    B, L, _ = x.shape
    flat = np.ascontiguousarray(x.reshape(B * L, DM))
    W_in = np.asarray(inputs["L0_attn_W_in"], dtype=f32)
    b_in = np.asarray(inputs["L0_attn_b_in"], dtype=f32)
    conv_w = np.asarray(inputs["L0_attn_conv_w"], dtype=f32)
    conv_b = np.asarray(inputs["L0_attn_conv_b"], dtype=f32)
    W_x = np.asarray(inputs["L0_attn_W_x"], dtype=f32)
    W_dt = np.asarray(inputs["L0_attn_W_dt"], dtype=f32)
    b_dt = np.asarray(inputs["L0_attn_b_dt"], dtype=f32)
    A_log = np.asarray(inputs["L0_attn_A_log"], dtype=f32)
    W_out = np.asarray(inputs["L0_attn_W_out"], dtype=f32)
    b_out = np.asarray(inputs["L0_attn_b_out"], dtype=f32)
    w_gate = np.asarray(inputs["L0_ffn_w_gate"], dtype=f32)
    w_up = np.asarray(inputs["L0_ffn_w_up"], dtype=f32)
    w_down = np.asarray(inputs["L0_ffn_w_down"], dtype=f32)

    shared = {
        "w_inT": np.ascontiguousarray(W_in.T),
        "w_xT": np.ascontiguousarray(W_x.T),
        "w_dtT": np.ascontiguousarray(W_dt.T),
        "w_outT": np.ascontiguousarray(W_out.T),
        "wgT": np.ascontiguousarray(w_gate.T),
        "wuT": np.ascontiguousarray(w_up.T),
        "wdT": np.ascontiguousarray(w_down.T),
        "b_in_t": np.ascontiguousarray(b_in.reshape(32, 128).T),
        "conv_w_t": np.ascontiguousarray(
            conv_w[:, 0, :].reshape(16, 128, 3).transpose(1, 2, 0).reshape(128, 48)),
        "conv_b_t": np.ascontiguousarray(conv_b.reshape(16, 128).T),
        "b_dt_t": np.ascontiguousarray(b_dt.reshape(16, 128).T),
        "alog_t": np.ascontiguousarray(
            A_log.reshape(16, 128, 16).transpose(1, 0, 2).reshape(128, 256)),
        "b_out_bc": np.ascontiguousarray(np.broadcast_to(b_out.reshape(1, DM), (128, DM))),
        "eye_in": np.eye(128, dtype=BFNP),
    }
    ncores = (B * L) // TOK
    in_maps = []
    for k in range(ncores):
        start = k * TOK
        seq_start = (start % L) == 0
        halo = np.zeros((2, DM), f32) if seq_start else flat[start - 2:start]
        xh_k = np.ascontiguousarray(np.concatenate([halo, flat[start:start + TOK]], axis=0))
        hm = np.full((128, 2), 0.0 if seq_start else 1.0, dtype=f32)
        m = dict(shared)
        m["xh"] = xh_k
        m["hmask_in"] = hm
        in_maps.append(m)
    return in_maps, (B, L)


def kernel(**inputs):
    TOK = 2048
    in_maps, (B, L) = _host_inputs(inputs, TOK)
    nc = _get_nc(TOK)
    res = run_bass_kernel_spmd(nc, in_maps, core_ids=list(range(len(in_maps))))
    outs = [np.asarray(r["out"]) for r in res.results]
    full = np.concatenate(outs, axis=0).reshape(B, L, DM)
    return full.astype(np.float32)
